# version 05
# speedup vs baseline: 2.0081x; 1.0708x over previous
import numpy as np
from contextlib import ExitStack

import concourse.bass as bass
import concourse.mybir as mybir
from concourse.bass_utils import run_bass_kernel_spmd

F32 = mybir.dt.float32
BF16 = mybir.dt.bfloat16
AF = mybir.ActivationFunctionType
ALU = mybir.AluOpType
AX = mybir.AxisListType

NCORES = 8
B, N, D = 4, 8192, 1024
FFN_H = 2816
FFN_HC = FFN_H // 128
TOK = N // 2
NT = TOK // 128


NSLOT = 8
ENG_OF_QUEUE = {"spq": "sp", "actq": "act", "poolq": "pool"}
ENGINES = ["pe", "dve", "act", "pool", "sp"]
DMA_STREAMS = [f"{q}{i}" for q in ENG_OF_QUEUE for i in range(NSLOT)]
STREAMS = ENGINES + DMA_STREAMS


class Prog:
    def __init__(self, nc, es):
        self.nc = nc
        self.es = es
        self.sems = {s: nc.alloc_semaphore(f"sem_{s}") for s in STREAMS}
        self.count = {s: 0 for s in STREAMS}
        self.mark = {e: {s: 0 for s in STREAMS} for e in ENGINES}
        self.items = {e: [] for e in ENGINES}
        self._consts = {}
        self._rr = {}
        self._scratch = {}
        self.alloc_fn = alloc
        self.psum_fn = alloc_ps
        self.epoch = 0
        self.phase_base = []

    def scratch(self, name, shape, dtype):
        key = ("sb", self.epoch, name, tuple(shape), dtype)
        if key not in self._scratch:
            self._scratch[key] = self.alloc_fn(self.es, self.nc, name, shape,
                                               dtype)
        return self._scratch[key]

    def psum(self, name, shape, dtype=F32):
        key = ("ps", self.epoch, name, tuple(shape), dtype)
        if key not in self._scratch:
            self._scratch[key] = self.psum_fn(self.es, self.nc, name, shape,
                                              dtype)
        return self._scratch[key]

    def constf(self, value, dtype=F32):
        key = (dtype, float(value))
        if key not in self._consts:
            t = alloc(self.es, self.nc, f"constf_{len(self._consts)}",
                      [128, 1], dtype)
            tok = self.op("dve", lambda v, t=t: v.memset(t[:], value))
            self._consts[key] = (t, tok)
        t, tok = self._consts[key]
        return t, tok

    def _waits(self, eng, deps):
        waits = []
        for (s, v) in deps:
            if s == eng:
                pass
            if v > self.mark[eng][s]:
                waits.append((self.sems[s], v))
                self.mark[eng][s] = v
        return waits

    def op(self, eng, fn, deps=(), n=1):
        waits = self._waits(eng, list(deps) + self.phase_base)
        self.count[eng] += n
        tok = (eng, self.count[eng])
        self.items[eng].append((waits, fn, self.sems[eng], n))
        return tok

    def dma(self, queue, out, in_, deps=(), **kw):
        eng = ENG_OF_QUEUE[queue]
        slot = f"{queue}{self._rr.setdefault(queue, 0)}"
        self._rr[queue] = (self._rr[queue] + 1) % NSLOT
        deps = list(deps) + self.phase_base + [(slot, self.count[slot])]
        waits = self._waits(eng, deps)
        self.count[slot] += 16
        tok = (slot, self.count[slot])

        def fn(b, out=out, in_=in_, kw=kw):
            return b.dma_start(out=out, in_=in_, **kw)

        self.items[eng].append((waits, fn, self.sems[slot], 16))
        return tok

    def barrier_token(self):
        return [(s, self.count[s]) for s in STREAMS if self.count[s] > 0]

    def engines_token(self, engines=("pe", "dve", "act", "pool")):
        return [(e, self.count[e]) for e in engines if self.count[e] > 0]

    def build(self, block):
        builders = {
            "pe": block.tensor, "dve": block.vector, "act": block.scalar,
            "pool": block.gpsimd, "sp": block.sync,
        }
        for eng, items in self.items.items():
            if not items:
                continue

            def body(b, items=items):
                for waits, fn, sem, n in items:
                    for (s, v) in waits:
                        b.wait_ge(s, v)
                    ins = fn(b)
                    ins.then_inc(sem, n)

            builders[eng](body)


def finish(prog, block, final_deps):
    def fn(b):
        return b.nop()
    prog.op("sp", fn, deps=final_deps)



def alloc(es, nc, name, shape, dtype):
    return es.enter_context(nc.sbuf_tensor(name, shape, dtype))


def alloc_ps(es, nc, name, shape, dtype=F32):
    return es.enter_context(nc.psum_tensor(name, shape, dtype))


def make_const_tiles(p, es, nc):
    if hasattr(p, "_ln_consts"):
        return p._ln_consts
    c = {}
    p._ln_consts = c
    c["ones_f32_row"] = alloc(es, nc, "c_ones_f32", [1, 128], F32)
    t0 = p.op("dve", lambda v: v.memset(c["ones_f32_row"][:], 1.0))
    c["oo1024"] = alloc(es, nc, "c_oo1024", [128, 1], F32)
    t1 = p.op("dve", lambda v: v.memset(c["oo1024"][:], 1.0 / 1024.0))
    c["deps"] = [t0, t1]
    return c


def emit_ln_fm(p, es, nc, consts, x_tiles, x_deps, out_bf, out_f32, T, TT=512,
               nchunks=8, tag="ln"):
    sq = [p.scratch(f"{tag}_sq{c}", [128, TT], F32) for c in range(nchunks)]
    mu_ps = p.psum(f"{tag}_mups", [1, TT], F32)
    ms_ps = p.psum(f"{tag}_msps", [1, TT], F32)
    ab_ps = p.psum(f"{tag}_abps", [128, 2 * TT], F32)
    mu_sb = p.scratch(f"{tag}_musb", [1, TT], F32)
    mu2_sb = p.scratch(f"{tag}_mu2sb", [1, TT], F32)
    var_sb = p.scratch(f"{tag}_varsb", [1, TT], F32)
    sd_sb = p.scratch(f"{tag}_sdsb", [1, TT], F32)
    a_sb = p.scratch(f"{tag}_asb", [1, TT], F32)
    b_sb = p.scratch(f"{tag}_bsb", [1, TT], F32)
    tmp = [p.scratch(f"{tag}_tmp{c}", [128, TT], F32) for c in range(nchunks)]

    oo = consts["oo1024"]
    ones_row = consts["ones_f32_row"]
    cdeps = consts["deps"]

    it_prev = []
    for s0 in range(0, T, TT):
        sl = slice(s0, s0 + TT)
        sq_toks = []
        for c in range(nchunks):
            ts = p.op("act", lambda a, c=c, sl=sl: a.activation(
                sq[c][:], x_tiles[c][:, sl], AF.Square),
                deps=list(x_deps) + it_prev)
            sq_toks.append(ts)
        for c in range(nchunks):
            tm = p.op("pe", lambda t, c=c, sl=sl: t.matmul(
                mu_ps[:], oo[:], x_tiles[c][:, sl],
                start=(c == 0), stop=(c == nchunks - 1)),
                deps=cdeps + list(x_deps) + it_prev)
        tmu = (("pe", p.count["pe"]))
        for c in range(nchunks):
            tm2 = p.op("pe", lambda t, c=c: t.matmul(
                ms_ps[:], oo[:], sq[c][:],
                start=(c == 0), stop=(c == nchunks - 1)),
                deps=[sq_toks[c]])
        tms = (("pe", p.count["pe"]))
        t1 = p.op("act", lambda a: a.copy(mu_sb[:], mu_ps[:]), deps=[tmu])
        t2 = p.op("act", lambda a: a.activation(mu2_sb[:], mu_sb[:], AF.Square),
                  deps=[t1])
        t3 = p.op("dve", lambda v: v.tensor_sub(var_sb[:], ms_ps[:], mu2_sb[:]),
                  deps=[tms, t2])
        eps_t, eps_tok = p.constf(1e-5)
        t4 = p.op("act", lambda a, eps_t=eps_t: a.activation(
            sd_sb[:], var_sb[:], AF.Sqrt, bias=eps_t[0:1, 0:1]),
            deps=[t3, eps_tok])
        t5 = p.op("dve", lambda v: v.reciprocal(a_sb[:], sd_sb[:]), deps=[t4])
        t6 = p.op("dve", lambda v: v.scalar_tensor_tensor(
            b_sb[:], mu_sb[:], -1.0, a_sb[:], ALU.mult, ALU.mult),
            deps=[t5, t1])
        t7 = p.op("pe", lambda t: t.matmul(ab_ps[:, 0:TT], ones_row[:],
                                           a_sb[:], start=True, stop=True),
                  deps=cdeps + [t5])
        t8 = p.op("pe", lambda t: t.matmul(ab_ps[:, TT:2 * TT], ones_row[:],
                                           b_sb[:], start=True, stop=True),
                  deps=[t6])
        out_toks = []
        for c in range(nchunks):
            u1 = p.op("dve", lambda v, c=c, sl=sl: v.tensor_mul(
                tmp[c][:], x_tiles[c][:, sl], ab_ps[:, 0:TT]),
                deps=[t7] + x_deps)
            if out_f32 is not None:
                u2 = p.op("dve", lambda v, c=c, sl=sl: v.tensor_add(
                    out_f32[c][:, sl], tmp[c][:], ab_ps[:, TT:2 * TT]),
                    deps=[t8, u1])
                u3 = p.op("dve", lambda v, c=c, sl=sl: v.tensor_copy(
                    out_bf[c][:, sl], out_f32[c][:, sl]), deps=[u2])
                out_toks.append(u3)
            else:
                u2 = p.op("dve", lambda v, c=c, sl=sl: v.tensor_add(
                    out_bf[c][:, sl], tmp[c][:], ab_ps[:, TT:2 * TT]),
                    deps=[t8, u1])
                out_toks.append(u2)
        it_prev = p.engines_token()
        yield s0, out_toks



def build_ln0(T, SW=1024):
    SW = min(SW, T)
    nc = bass.Bass("TRN2", target_bir_lowering=False, debug=False)
    x_in = nc.dram_tensor("x_fm", [D, T], F32, kind="ExternalInput").ap()
    xn_out = nc.dram_tensor("xn_f32", [D, T], F32, kind="ExternalOutput").ap()
    xnb_out = nc.dram_tensor("xn_bf", [D, T], BF16, kind="ExternalOutput").ap()
    es = ExitStack()
    with es:
        p = Prog(nc, es)
        x_t = [alloc(es, nc, f"x{c}", [128, SW], F32) for c in range(8)]
        xn_t = [alloc(es, nc, f"xn{c}", [128, SW], F32) for c in range(8)]
        xnb_t = [alloc(es, nc, f"xnb{c}", [128, SW], BF16) for c in range(8)]
        consts = make_const_tiles(p, es, nc)
        st = []
        prev = []
        for w0 in range(0, T, SW):
            ld = []
            for c in range(8):
                ld.append(p.dma("spq", x_t[c][:],
                                x_in[c * 128:(c + 1) * 128, w0:w0 + SW],
                                deps=prev))
            for s0, toks in emit_ln_fm(p, es, nc, consts, x_t, ld, xnb_t,
                                       xn_t, SW):
                for c in range(8):
                    sl = slice(s0, s0 + 512)
                    gl = slice(w0 + s0, w0 + s0 + 512)
                    st.append(p.dma("spq", xn_out[c * 128:(c + 1) * 128, gl],
                                    xn_t[c][:, sl], deps=toks))
                    st.append(p.dma("spq", xnb_out[c * 128:(c + 1) * 128, gl],
                                    xnb_t[c][:, sl], deps=toks))
            prev = p.barrier_token()
        block = es.enter_context(nc.Block())
        finish(p, block, st)
        p.build(block)
    return nc



def emit_identity(p, es, nc, name="identf", dtype=F32):
    idn = alloc(es, nc, name, [128, 128], dtype)
    t0 = p.op("dve", lambda v: v.memset(idn[:], 0.0))
    t1 = p.op("pool", lambda g: g.affine_select(
        out=idn[:], in_=idn[:], pattern=[[-1, 128]],
        compare_op=ALU.not_equal, fill=1.0, base=0, channel_multiplier=1),
        deps=[t0])
    return idn, [t1]


def emit_moe(p, es, nc, A, APS, io, shared, T, TG=1024, NE=8, H=1408, prev=()):
    if True:
        HC = H // 128
        NTG = TG // 128
        xn1_g = [A(es, nc, f"xn1g{c}", [128, TG], BF16) for c in range(8)]
        acc = [A(es, nc, f"acc{i}", [128, D], F32) for i in range(NTG)]
        wts_sb = A(es, nc, "wts_sb", [128, NTG * NE], F32)
        w1_sb = A(es, nc, "w1_sb", [128, 8 * H], BF16)
        w3_sb = A(es, nc, "w3_sb", [128, 8 * H], BF16)
        w2_sb = A(es, nc, "w2_sb", [128, HC * D], BF16)
        h1s = A(es, nc, "h1s", [128, 512], BF16)
        he = A(es, nc, "he", [128, HC * 512], BF16)
        a1_t = A(es, nc, "a1_t", [128, D], F32)
        s1_sb = A(es, nc, "s1_sb", [128, D], F32)
        x1_t = A(es, nc, "x1_t", [128, D], F32)
        out_sb = A(es, nc, "out_sb", [128, D], F32)
        h_ps = [APS(es, nc, f"h_ps{i}", [128, 512], F32) for i in range(2)]
        y_ps = [APS(es, nc, f"y_ps{i}", [128, 512], F32) for i in range(2)]
        tr_ps = [APS(es, nc, f"tr_ps{i}", [128, 512], F32) for i in range(2)]

        ident, ident_deps = shared['identf'], shared['identf_deps']

        st_all = []
        prev_group = list(prev)
        for g0 in range(0, T, TG):
            gsl = slice(g0, g0 + TG)
            ld_x = [p.dma("spq", xn1_g[c][:], io["xn1_bf"][c * 128:(c + 1) * 128, gsl],
                          deps=prev_group) for c in range(8)]
            ld_w = p.dma("spq", wts_sb[:].rearrange("p (n e) -> p n e", e=NE),
                         io["wts_tm"][gsl, :].rearrange("(n p) e -> p n e", p=128),
                         deps=prev_group)
            acc_tok = {i: None for i in range(NTG)}
            dbg_tok = []
            ydep = [[], []]
            last_he = []
            wr1 = [list(prev_group) for _ in range(HC)]
            wr3 = [list(prev_group) for _ in range(HC)]
            wr2 = [list(prev_group) for _ in range(HC)]
            for e in range(NE):
                lw1 = [p.dma("poolq",
                             w1_sb[:].rearrange("p (c h) -> p c h", c=8)
                             [:, :, hc * 128:hc * 128 + 128],
                             io["moe_w1"][e].rearrange("(c p) h -> p c h", p=128)
                             [:, :, hc * 128:hc * 128 + 128],
                             deps=wr1[hc]) for hc in range(HC)]
                lw3 = [p.dma("poolq",
                             w3_sb[:].rearrange("p (c h) -> p c h", c=8)
                             [:, :, hc * 128:hc * 128 + 128],
                             io["moe_w3"][e].rearrange("(c p) h -> p c h", p=128)
                             [:, :, hc * 128:hc * 128 + 128],
                             deps=wr3[hc]) for hc in range(HC)]
                lw2 = [p.dma("poolq",
                             w2_sb[:].rearrange("p (c d) -> p c d", c=HC)[:, hc, :],
                             io["moe_w2"][e].rearrange("(c p) d -> p c d", p=128)[:, hc, :],
                             deps=wr2[hc]) for hc in range(HC)]
                wr1 = [[] for _ in range(HC)]
                wr3 = [[] for _ in range(HC)]
                wr2 = [[] for _ in range(HC)]
                for tt in range(TG // 512):
                    tsl = slice(tt * 512, tt * 512 + 512)
                    he_toks = []
                    for hc in range(HC):
                        pa, pb = h_ps[0], h_ps[1]
                        for c in range(8):
                            tok = p.op("pe", lambda t, c=c, hc=hc, tsl=tsl, pa=pa: t.matmul(
                                pa[:], w1_sb[:, c * H + hc * 128:c * H + hc * 128 + 128],
                                xn1_g[c][:, tsl], start=(c == 0), stop=(c == 7)),
                                deps=[lw1[hc]] + ld_x + (he_toks[-1:] if he_toks else last_he or list(prev_group)))
                        t_h1 = ("pe", p.count["pe"])
                        wr1[hc].append(t_h1)
                        t_silu = p.op("act", lambda a, pa=pa: a.activation(
                            h1s[:], pa[:], AF.Silu), deps=[t_h1])
                        for c in range(8):
                            tok = p.op("pe", lambda t, c=c, hc=hc, tsl=tsl, pb=pb: t.matmul(
                                pb[:], w3_sb[:, c * H + hc * 128:c * H + hc * 128 + 128],
                                xn1_g[c][:, tsl], start=(c == 0), stop=(c == 7)),
                                deps=[lw3[hc]] + ld_x)
                        t_h3 = ("pe", p.count["pe"])
                        wr3[hc].append(t_h3)
                        t_he = p.op("dve", lambda v, hc=hc, pb=pb: v.tensor_mul(
                            he[:, hc * 512:hc * 512 + 512], h1s[:], pb[:]),
                            deps=[t_silu, t_h3] + dbg_tok)
                        he_toks.append(t_he)
                        last_he = [t_he]
                    for q4 in range(4):
                        ti = (tt * 512 + q4 * 128) // 128
                        for dh in range(2):
                            yp = y_ps[dh]
                            for hc in range(HC):
                                tok = p.op("pe", lambda t, hc=hc, q4=q4, dh=dh, tt=tt, yp=yp: t.matmul(
                                    yp[:], he[:, hc * 512 + q4 * 128:hc * 512 + q4 * 128 + 128],
                                    w2_sb[:, hc * D + dh * 512:hc * D + dh * 512 + 512],
                                    start=(hc == 0), stop=(hc == HC - 1)),
                                    deps=[lw2[hc]] + he_toks + ydep[dh] +
                                    ([acc_tok[ti]] if acc_tok[ti] else []))
                                wr2[hc].append(tok)
                            t_y = ("pe", p.count["pe"])
                            wcol = wts_sb[:, ti * NE + e:ti * NE + e + 1]
                            if e == 0:
                                atok = p.op("dve", lambda v, ti=ti, dh=dh, yp=yp, wcol=wcol: v.tensor_scalar(
                                    acc[ti][:, dh * 512:dh * 512 + 512], yp[:],
                                    wcol, None, ALU.mult),
                                    deps=[t_y, ld_w])
                            else:
                                atok = p.op("dve", lambda v, ti=ti, dh=dh, yp=yp, wcol=wcol: v.scalar_tensor_tensor(
                                    acc[ti][:, dh * 512:dh * 512 + 512], yp[:],
                                    wcol, acc[ti][:, dh * 512:dh * 512 + 512],
                                    ALU.mult, ALU.add),
                                    deps=[t_y, ld_w] + ([acc_tok[ti]] if acc_tok[ti] else []))
                            acc_tok[ti] = atok
                            ydep[dh] = [atok]
            so_prev = []
            for ti in range(NTG):
                t0g = g0 + ti * 128
                la = p.dma("spq", a1_t[:], io["a1_tm"][t0g:t0g + 128, :],
                           deps=p.engines_token(("dve",)))
                ts1 = p.op("dve", lambda v, ti=ti: v.tensor_add(
                    s1_sb[:], acc[ti][:], a1_t[:]), deps=[la, acc_tok[ti]])
                lx = p.dma("spq", x1_t[:].rearrange("p (c t) -> p c t", c=8),
                           io["x1_fm"][:, t0g:t0g + 128].rearrange("(c p) t -> p c t", p=128),
                           deps=p.engines_token(("dve",)))
                tr_toks = []
                for c8 in range(8):
                    trp = tr_ps[c8 // 4]
                    csl = slice((c8 % 4) * 128, (c8 % 4) * 128 + 128)
                    ttr = p.op("pe", lambda t, c8=c8, trp=trp, csl=csl: t.transpose(
                        trp[:, csl], s1_sb[:, c8 * 128:c8 * 128 + 128], ident[:]),
                        deps=[ts1] + ident_deps + p.engines_token(("dve",)))
                    to = p.op("dve", lambda v, c8=c8, trp=trp, csl=csl: v.tensor_add(
                        out_sb[:, c8 * 128:c8 * 128 + 128], trp[:, csl], x1_t[:, c8 * 128:c8 * 128 + 128]),
                        deps=[ttr, lx] + so_prev)
                    tr_toks.append(to)
                so = p.dma("spq", io["out_fm"][:, t0g:t0g + 128].rearrange(
                    "(c p) t -> p c t", p=128),
                    out_sb[:].rearrange("p (c t) -> p c t", c=8), deps=tr_toks)
                so_prev = [so]
                st_all.append(so)
            prev_group = p.barrier_token()

        return p.barrier_token(), st_all


def build_moe(T, TG=1024, NE=8, H=1408):
    HC = H // 128
    nc = bass.Bass("TRN2", target_bir_lowering=False, debug=False)
    io = {
        "xn1_bf": nc.dram_tensor("xn1_bf", [D, T], BF16, kind="ExternalInput").ap(),
        "x1_fm": nc.dram_tensor("x1_fm", [D, T], F32, kind="ExternalInput").ap(),
        "a1_tm": nc.dram_tensor("a1_tm", [T, D], F32, kind="ExternalInput").ap(),
        "wts_tm": nc.dram_tensor("wts_tm", [T, NE], F32, kind="ExternalInput").ap(),
        "moe_w1": nc.dram_tensor("moe_w1", [NE, D, H], F32, kind="ExternalInput").ap(),
        "moe_w3": nc.dram_tensor("moe_w3", [NE, D, H], F32, kind="ExternalInput").ap(),
        "moe_w2": nc.dram_tensor("moe_w2", [NE, H, D], F32, kind="ExternalInput").ap(),
        "out_fm": nc.dram_tensor("out_fm", [D, T], F32, kind="ExternalOutput").ap(),
    }
    es = ExitStack()
    with es:
        p = Prog(nc, es)
        idt, idt_deps = emit_identity(p, es, nc)
        shared = {"identf": idt, "identf_deps": idt_deps}
        emit_moe(p, es, nc, alloc, alloc_ps, io, shared, T, TG, NE, H)
        block = es.enter_context(nc.Block())
        finish(p, block, p.barrier_token())
        p.build(block)
    return nc



def build_l1attn(Q0, QN, GQ=2048, KT=512):
    KVN = Q0 + QN
    nc = bass.Bass("TRN2", target_bir_lowering=False, debug=False)
    xn1_in = nc.dram_tensor("xn1_bf", [D, KVN], BF16, kind="ExternalInput").ap()
    wq_in = nc.dram_tensor("l1_wq", [D, D], F32, kind="ExternalInput").ap()
    wk_in = nc.dram_tensor("l1_wk", [D, D], F32, kind="ExternalInput").ap()
    wv_in = nc.dram_tensor("l1_wv", [D, D], F32, kind="ExternalInput").ap()
    a1_out = nc.dram_tensor("a1_tm", [QN, D], F32, kind="ExternalOutput").ap()
    kfm_scr = nc.dram_tensor("kfm_scr", [D, KVN], BF16, kind="Internal").ap()
    vtm_scr = nc.dram_tensor("vtm_scr", [KVN, D], BF16, kind="Internal").ap()
    qfm_scr = nc.dram_tensor("qfm_scr", [D, QN], BF16, kind="Internal").ap()

    es = ExitStack()
    with es:
        p = Prog(nc, es)
        wq_sb = alloc(es, nc, "wq_sb", [128, 8 * D], BF16)
        wk_sb = alloc(es, nc, "wk_sb", [128, 8 * D], BF16)
        wv_sb = alloc(es, nc, "wv_sb", [128, 8 * D], BF16)
        xn_t = [alloc(es, nc, f"xn_t{c}", [128, 512], BF16) for c in range(8)]
        ev_k = alloc(es, nc, "ev_k", [128, 512], BF16)
        ev_q = alloc(es, nc, "ev_q", [128, 512], BF16)
        ev_v = alloc(es, nc, "ev_v", [128, D], BF16)
        ps512 = [alloc_ps(es, nc, f"ps512_{i}", [128, 512], F32) for i in range(2)]
        ps1024 = [alloc_ps(es, nc, f"ps1024_{i}", [128, D], F32) for i in range(2)]
        k_ps, q_ps, v_ps = ps512[0], ps512[1], ps1024[0]

        lwq = p.dma("poolq", wq_sb[:].rearrange("p (c d) -> p c d", c=8),
                    wq_in.rearrange("(c p) d -> p c d", p=128))
        lwk = p.dma("poolq", wk_sb[:].rearrange("p (c d) -> p c d", c=8),
                    wk_in.rearrange("(c p) d -> p c d", p=128))
        lwv = p.dma("poolq", wv_sb[:].rearrange("p (c d) -> p c d", c=8),
                    wv_in.rearrange("(c p) d -> p c d", p=128))
        prev_it = []
        st_k = st_q = st_v = None
        te_k = te_q = te_v = None
        for t0 in range(0, KVN, 512):
            ld = [p.dma("spq", xn_t[c][:], xn1_in[c * 128:(c + 1) * 128,
                                                  t0:t0 + 512],
                        deps=prev_it) for c in range(8)]
            for m in range(8):
                for c in range(8):
                    p.op("pe", lambda t, m=m, c=c: t.matmul(
                        k_ps[:], wk_sb[:, c * D + m * 128:c * D + m * 128 + 128],
                        xn_t[c][:], start=(c == 0), stop=(c == 7)),
                        deps=[lwk] + ld + prev_it +
                        ([te_k] if te_k else []))
                tk = ("pe", p.count["pe"])
                te_k = p.op("act", lambda a: a.copy(ev_k[:], k_ps[:]),
                            deps=[tk] + ([st_k] if st_k else []))
                st_k = p.dma("spq", kfm_scr[m * 128:(m + 1) * 128, t0:t0 + 512],
                             ev_k[:], deps=[te_k])
            if t0 >= Q0:
                for m in range(8):
                    for c in range(8):
                        p.op("pe", lambda t, m=m, c=c: t.matmul(
                            q_ps[:], wq_sb[:, c * D + m * 128:c * D + m * 128 + 128],
                            xn_t[c][:], start=(c == 0), stop=(c == 7)),
                            deps=[lwq] + ld + prev_it +
                            ([te_q] if te_q else []))
                    tq = ("pe", p.count["pe"])
                    te_q = p.op("act", lambda a: a.copy(ev_q[:], q_ps[:]),
                                deps=[tq] + ([st_q] if st_q else []))
                    st_q = p.dma("spq", qfm_scr[m * 128:(m + 1) * 128,
                                                t0 - Q0:t0 - Q0 + 512],
                                 ev_q[:], deps=[te_q])
            for r in range(4):
                for dh in range(2):
                    for c in range(8):
                        p.op("pe", lambda t, r=r, dh=dh, c=c: t.matmul(
                            v_ps[:, dh * 512:dh * 512 + 512],
                            xn_t[c][:, r * 128:r * 128 + 128],
                            wv_sb[:, c * D + dh * 512:c * D + dh * 512 + 512],
                            start=(c == 0), stop=(c == 7)),
                            deps=[lwv] + ld + prev_it +
                            ([te_v] if te_v else []))
                tv = ("pe", p.count["pe"])
                te_v = p.op("act", lambda a: a.copy(ev_v[:], v_ps[:]),
                            deps=[tv] + ([st_v] if st_v else []))
                st_v = p.dma("spq", vtm_scr[t0 + r * 128:t0 + r * 128 + 128, :],
                             ev_v[:], deps=[te_v])
            prev_it = p.engines_token()
        phaseA_done = p.barrier_token()

        NKT = KVN // KT
        NQTG = GQ // 128
        qg = [alloc(es, nc, f"qg{c}", [128, GQ], BF16) for c in range(8)]
        kv_k = [alloc(es, nc, f"kv_k{i}", [128, 8 * KT], BF16) for i in range(2)]
        kv_v = [alloc(es, nc, f"kv_v{i}", [128, 4 * D], BF16) for i in range(2)]
        o_sb = [alloc(es, nc, f"o_sb{i}", [128, D], F32) for i in range(NQTG)]
        mcols = [alloc(es, nc, f"mcols{i}", [128, NKT], F32) for i in range(NQTG)]
        lcols = [alloc(es, nc, f"lcols{i}", [128, NKT], F32) for i in range(NQTG)]
        m_sb = alloc(es, nc, "m_sb", [128, NQTG], F32)
        nm_sb = alloc(es, nc, "nm_sb", [128, NQTG], F32)
        l_sb = alloc(es, nc, "l_sb", [128, NQTG], F32)
        li_sb = alloc(es, nc, "li_sb", [128, NQTG], F32)
        s_m = alloc(es, nc, "s_m", [128, KT], F32)
        p_sb = alloc(es, nc, "p_sb", [128, KT], BF16)
        pt_sb = alloc(es, nc, "pt_sb", [128, KT], BF16)
        a1_sb = alloc(es, nc, "a1_sb", [128, D], F32)
        s_ps = ps512
        pt_ps = alloc_ps(es, nc, "pt_ps", [128, KT], BF16)
        pv_ps = ps1024

        identb, identb_deps = emit_identity(p, es, nc, name="identb", dtype=BF16)
        masks = []
        mtoks = []
        for oi, off in enumerate(range(0, KT, 128)):
            mt = alloc(es, nc, f"maskadd{oi}", [128, KT], F32)
            t0_ = p.op("dve", lambda v, mt=mt: v.memset(mt[:], 0.0))
            t1_ = p.op("pool", lambda g, mt=mt, off=off: g.affine_select(
                out=mt[:], in_=mt[:], pattern=[[-1, KT]],
                compare_op=ALU.is_ge, fill=-1e30, base=off,
                channel_multiplier=1), deps=[t0_])
            masks.append(mt)
            mtoks.append(t1_)
        c32_t, c32_tok = p.constf(1.0 / 32.0)

        prev_grp = phaseA_done
        for g0 in range(0, QN, GQ):
            NQT = min(GQ, QN - g0) // 128
            lq = [p.dma("spq", qg[c][:, 0:NQT * 128],
                        qfm_scr[c * 128:(c + 1) * 128, g0:g0 + NQT * 128],
                        deps=prev_grp) for c in range(8)]
            kend = Q0 + g0 + NQT * 128
            nkt = (kend + KT - 1) // KT
            o_tok = {}
            a1_prev = []
            pvdep = [[], []]
            kbuf_rd = [list(prev_grp), list(prev_grp)]
            for pss in (0, 1):
                for kt in range(nkt):
                    kb = kt % 2
                    lk = p.dma("spq", kv_k[kb][:].rearrange(
                        "p (c t) -> p c t", c=8),
                        kfm_scr[:, kt * KT:(kt + 1) * KT].rearrange(
                            "(c p) t -> p c t", p=128),
                        deps=kbuf_rd[kb])
                    lv = None
                    if pss == 1:
                        lv = p.dma("actq", kv_v[kb][:].rearrange(
                            "p (r d) -> p r d", r=4),
                            vtm_scr[kt * KT:(kt + 1) * KT, :].rearrange(
                                "(r p) d -> p r d", p=128),
                            deps=kbuf_rd[kb])
                    kbuf_rd[kb] = []
                    for qt in range(NQT):
                        qglob = Q0 + g0 + qt * 128
                        if kt * KT >= qglob + 128:
                            continue
                        sp = s_ps[kt % 2]
                        for c in range(8):
                            p.op("pe", lambda t, c=c, qt=qt, sp=sp, kb=kb: t.matmul(
                                sp[:], qg[c][:, qt * 128:qt * 128 + 128],
                                kv_k[kb][:, c * KT:c * KT + KT],
                                start=(c == 0), stop=(c == 7)),
                                deps=[lk] + lq + p.engines_token(("dve", "act")))
                        ts = ("pe", p.count["pe"])
                        kbuf_rd[kb].append(ts)
                        is_diag = kt == qglob // KT
                        if is_diag:
                            oi = (qglob % KT) // 128
                            tm_ = p.op("dve", lambda v, sp=sp, oi=oi: v.tensor_add(
                                s_m[:], sp[:], masks[oi][:]),
                                deps=[ts] + mtoks + p.engines_token(("act",)))
                            src, dep0 = s_m, [tm_]
                        else:
                            src, dep0 = sp, [ts]
                        if pss == 0:
                            p.op("dve", lambda v, src=src, qt=qt, kt=kt: v.reduce_max(
                                mcols[qt][:, kt:kt + 1], src[:], axis=AX.X),
                                deps=dep0)
                        else:
                            texp = p.op("act", lambda a, src=src, qt=qt, kt=kt: a.activation(
                                p_sb[:], src[:], AF.Exp,
                                bias=nm_sb[:, qt:qt + 1],
                                scale=c32_t[:, 0:1],
                                accum_out=lcols[qt][:, kt:kt + 1]),
                                deps=dep0 + [("dve", p.count["dve"]), c32_tok])
                            trt = []
                            for j in range(4):
                                tt_ = p.op("pe", lambda t, j=j: t.transpose(
                                    pt_ps[:, j * 128:j * 128 + 128],
                                    p_sb[:, j * 128:j * 128 + 128], identb[:]),
                                    deps=[texp] + identb_deps)
                                trt.append(tt_)
                            tev = p.op("act", lambda a: a.copy(pt_sb[:], pt_ps[:]),
                                       deps=trt)
                            ov = pv_ps[qt % 2]
                            nvr = min(4, ((qglob + 128 - kt * KT) + 127) // 128)
                            for dh in range(2):
                                for r in range(nvr):
                                    p.op("pe", lambda t, r=r, dh=dh, ov=ov, kb=kb, nvr=nvr: t.matmul(
                                        ov[:, dh * 512:dh * 512 + 512],
                                        pt_sb[:, r * 128:r * 128 + 128],
                                        kv_v[kb][:, r * D + dh * 512:r * D + dh * 512 + 512],
                                        start=(r == 0), stop=(r == nvr - 1)),
                                        deps=[tev, lv] + pvdep[qt % 2] +
                                        ([o_tok[qt]] if qt in o_tok else []))
                            tpv = ("pe", p.count["pe"])
                            kbuf_rd[kb].append(tpv)
                            if kt == 0:
                                oa = p.op("dve", lambda v, qt=qt, ov=ov: v.tensor_copy(
                                    o_sb[qt][:], ov[:]), deps=[tpv])
                            else:
                                oa = p.op("dve", lambda v, qt=qt, ov=ov: v.tensor_add(
                                    o_sb[qt][:], o_sb[qt][:], ov[:]),
                                    deps=[tpv, o_tok[qt]])
                            o_tok[qt] = oa
                            pvdep[qt % 2] = [oa]
                if pss == 0:
                    for qt in range(NQT):
                        nk_q = (Q0 + g0 + qt * 128) // KT + 1
                        t1_ = p.op("dve", lambda v, qt=qt, nk_q=nk_q: v.reduce_max(
                            m_sb[:, qt:qt + 1], mcols[qt][:, 0:nk_q], axis=AX.X),
                            deps=p.engines_token(("dve",)))
                        p.op("dve", lambda v, qt=qt: v.tensor_scalar_mul(
                            nm_sb[:, qt:qt + 1], m_sb[:, qt:qt + 1], -1.0 / 32.0),
                            deps=[t1_])
                else:
                    for qt in range(NQT):
                        nk_q = (Q0 + g0 + qt * 128) // KT + 1
                        t1_ = p.op("dve", lambda v, qt=qt, nk_q=nk_q: v.reduce_sum(
                            l_sb[:, qt:qt + 1], lcols[qt][:, 0:nk_q], axis=AX.X),
                            deps=p.engines_token(("act",)) + [o_tok[qt]])
                        t2_ = p.op("dve", lambda v, qt=qt: v.reciprocal(
                            li_sb[:, qt:qt + 1], l_sb[:, qt:qt + 1]),
                            deps=[t1_])
                        t3_ = p.op("dve", lambda v, qt=qt: v.tensor_scalar(
                            a1_sb[:], o_sb[qt][:], li_sb[:, qt:qt + 1],
                            None, ALU.mult), deps=[t2_, o_tok[qt]] + a1_prev)
                        sot = p.dma("spq",
                                    a1_out[g0 + qt * 128:g0 + qt * 128 + 128, :],
                                    a1_sb[:], deps=[t3_])
                        a1_prev = [sot]
            prev_grp = p.barrier_token()

        block = es.enter_context(nc.Block())
        finish(p, block, p.barrier_token())
        p.build(block)
    return nc



def emit_l0attn(p, es, nc, A, APS, io, shared, TPRE, T, prev=()):
    NH, FD, HD = 16, 16, 64
    TTOT = TPRE + T
    if True:
        identb, identb_deps = shared['identb'], shared['identb_deps']
        wq_sb = A(es, nc, "wq_sb", [128, 8 * 256], BF16)
        wk_sb = A(es, nc, "wk_sb", [128, 8 * 256], BF16)
        wv_sb = A(es, nc, "wv_sb", [128, 8 * D], BF16)
        wo_sb = A(es, nc, "wo_sb", [128, 8 * D], BF16)
        gam_sb = A(es, nc, "gam_sb", [128, FD], F32)
        bet_sb = A(es, nc, "bet_sb", [128, FD], F32)
        gq_sb = A(es, nc, "gq_sb", [128, FD], F32)
        bq_sb = A(es, nc, "bq_sb", [128, FD], F32)
        lw = [p.dma("poolq", wq_sb[:].rearrange("p (c m) -> p c m", c=8),
                    io["wq_t"].rearrange("(c p) m -> p c m", p=128)),
              p.dma("poolq", wk_sb[:].rearrange("p (c m) -> p c m", c=8),
                    io["wk_t"].rearrange("(c p) m -> p c m", p=128)),
              p.dma("poolq", wv_sb[:].rearrange("p (c m) -> p c m", c=8),
                    io["wv_t"].rearrange("(c p) m -> p c m", p=128)),
              p.dma("poolq", wo_sb[:].rearrange("p (c m) -> p c m", c=8),
                    io["wo_t"].rearrange("(c p) m -> p c m", p=128)),
              p.dma("spq", gam_sb[:], io["gam16"]),
              p.dma("spq", bet_sb[:], io["bet16"])]
        tg1 = p.op("dve", lambda v: v.tensor_scalar_mul(gq_sb[:], gam_sb[:], 0.25),
                   deps=[lw[4]])
        tg2 = p.op("dve", lambda v: v.tensor_scalar_mul(bq_sb[:], bet_sb[:], 0.25),
                   deps=[lw[5]])
        gb_deps = [tg1, tg2, lw[4], lw[5]]

        S_m = [A(es, nc, f"S_m{m}", [128, NH * 65], F32) for m in range(2)]
        S_bf = [A(es, nc, f"S_bf{m}", [128, NH * 65], BF16) for m in range(2)]
        s_up = []
        for m in range(2):
            s_up.append(p.op("dve", lambda v, m=m: v.memset(S_m[m][:], 0.0)))
            s_up.append(p.op("dve", lambda v, m=m: v.memset(S_bf[m][:], 0.0)))

        mask01 = A(es, nc, "mask01", [128, 128], F32)
        tmk0 = p.op("dve", lambda v: v.memset(mask01[:], 1.0))
        tmk1 = p.op("pool", lambda g: g.affine_select(
            out=mask01[:], in_=mask01[:], pattern=[[1, 128]],
            compare_op=ALU.is_ge, fill=0.0, base=0, channel_multiplier=-1),
            deps=[tmk0])
        mask_deps = [tmk1]
        sel2 = A(es, nc, "sel2", [2, 128], BF16)
        ts0 = p.op("dve", lambda v: v.memset(sel2[:], 1.0))
        ts1 = p.op("pool", lambda g: g.affine_select(
            out=sel2[:], in_=sel2[:], pattern=[[1, 128]],
            compare_op=ALU.is_ge, fill=0.0, base=0, channel_multiplier=-64),
            deps=[ts0])
        ts2 = p.op("pool", lambda g: g.affine_select(
            out=sel2[:], in_=sel2[:], pattern=[[-1, 128]],
            compare_op=ALU.is_ge, fill=0.0, base=63, channel_multiplier=64),
            deps=[ts1])
        sel2_deps = [ts2]

        apre_bf = None
        apre_deps = []
        if "a_pre" in io:
            apre_f = A(es, nc, "apre_f", [128, 1], F32)
            apre_bf = A(es, nc, "apre_b", [128, 1], BF16)
            lap = p.dma("spq", apre_f[:], io["a_pre"])
            tap = p.op("dve", lambda v: v.tensor_copy(apre_bf[:], apre_f[:]),
                       deps=[lap])
            apre_deps = [tap]

        xn_t = [A(es, nc, f"xn_t{c}", [128, 512], BF16) for c in range(8)]
        qraw = A(es, nc, "qraw", [128, 256], F32)
        kraw = A(es, nc, "kraw", [128, 256], F32)
        cen = A(es, nc, "cen", [128, 256], F32)
        sq_t = A(es, nc, "sq_t", [128, 256], F32)
        mu_t = A(es, nc, "mu_t", [128, NH], F32)
        var_t = A(es, nc, "var_t", [128, NH], F32)
        sd_t = A(es, nc, "sd_t", [128, NH], F32)
        rstd_t = A(es, nc, "rstd_t", [128, NH], F32)
        qn2 = A(es, nc, "qn2", [128, 256], BF16)
        kn2 = A(es, nc, "kn2", [128, 256], BF16)
        phiq = A(es, nc, "phiq", [128, 4096], BF16)
        phik = A(es, nc, "phik", [128, 4096], BF16)
        qfm = A(es, nc, "qfm", [16, NH * 128], BF16)
        kfm = A(es, nc, "kfm", [16, NH * 128], BF16)
        phiqfm = A(es, nc, "phiqfm", [128, 4096], BF16)
        s_sb = A(es, nc, "s_sb", [128, NH * 128], BF16)
        sm_f = A(es, nc, "sm_f", [128, 512], F32)
        v_aug = A(es, nc, "v_aug", [128, NH * 65], BF16)
        zrow = A(es, nc, "zrow", [65, 1024], F32)
        zasm = A(es, nc, "zasm", [2, 1024], F32)
        zinv = A(es, nc, "zinv", [2, 1024], BF16)
        zb_sb = A(es, nc, "zb_sb", [128, 1024], F32)
        attn_all = A(es, nc, "attn_all", [128, 8 * 512], BF16)
        a0ev = A(es, nc, "a0ev", [128, 512], F32)

        PSA = APS(es, nc, "PSA", [128, 2048], F32)
        PSB = APS(es, nc, "PSB", [128, 2048], F32)
        PSA_bf = PSA[:].bitcast(BF16)
        eps_t, eps_tok = p.constf(1e-5)
        eng = p.engines_token

        prev_tile = list(prev)
        a0_store_prev = []
        for t0 in range(0, TTOT, 512):
            own = t0 >= TPRE
            ld_x = [p.dma("spq", xn_t[c][:],
                          io["xn_bf"][c * 128:(c + 1) * 128, t0:t0 + 512],
                          deps=prev_tile) for c in range(8)]
            attn_toks = {}
            for cc in range(4):
                ck = slice(cc * 128, cc * 128 + 128)
                prj_dep = [lw[0], lw[1], lw[2]] + ld_x + eng(("act", "dve"))
                if own:
                    for c in range(8):
                        p.op("pe", lambda t, c=c, ck=ck: t.matmul(
                            PSB[:, 0:256], xn_t[c][:, ck],
                            wq_sb[:, c * 256:(c + 1) * 256],
                            start=(c == 0), stop=(c == 7)), deps=prj_dep)
                    t_q = ("pe", p.count["pe"])
                for c in range(8):
                    p.op("pe", lambda t, c=c, ck=ck: t.matmul(
                        PSB[:, 256:512], xn_t[c][:, ck],
                        wk_sb[:, c * 256:(c + 1) * 256],
                        start=(c == 0), stop=(c == 7)), deps=prj_dep)
                t_k = ("pe", p.count["pe"])
                for dh in range(2):
                    for c in range(8):
                        p.op("pe", lambda t, c=c, dh=dh, ck=ck: t.matmul(
                            PSB[:, 1024 + dh * 512:1024 + dh * 512 + 512],
                            xn_t[c][:, ck],
                            wv_sb[:, c * D + dh * 512:c * D + dh * 512 + 512],
                            start=(c == 0), stop=(c == 7)), deps=prj_dep)
                t_v = ("pe", p.count["pe"])
                tv1 = p.op("act", lambda a: a.copy(
                    v_aug[:].rearrange("p (h j) -> p h j", j=65)[:, :, 0:64],
                    PSB[:, 1024:2048].rearrange("p (h j) -> p h j", j=64)),
                    deps=[t_v])
                tv2 = p.op("dve", lambda v: v.memset(
                    v_aug[:].rearrange("p (h j) -> p h j", j=65)[:, :, 64:65],
                    1.0), deps=[t_v])
                v_deps = [tv1, tv2]
                if not own and apre_bf is not None:
                    tvm = p.op("dve", lambda v: v.tensor_tensor(
                        v_aug[:], v_aug[:],
                        apre_bf[:, 0:1].broadcast_to([128, NH * 65]),
                        ALU.mult), deps=v_deps + apre_deps)
                    v_deps = [tvm]

                def fmln(src_sl, raw, dst, gcol, bcol, extra_deps):
                    r3 = lambda ap: ap.rearrange("p (h d) -> p h d", d=FD)
                    t1 = p.op("act", lambda a, src_sl=src_sl, raw=raw: a.copy(
                        raw[:], src_sl), deps=extra_deps)
                    t2 = p.op("dve", lambda v, raw=raw: v.tensor_reduce(
                        mu_t[:].unsqueeze(2), r3(raw[:]), AX.X, ALU.add),
                        deps=[t1])
                    t2b = p.op("dve", lambda v: v.tensor_scalar_mul(
                        mu_t[:], mu_t[:], 1.0 / FD), deps=[t2])
                    t3 = p.op("dve", lambda v, raw=raw: v.tensor_sub(
                        r3(cen[:]), r3(raw[:]),
                        mu_t[:].unsqueeze(2).broadcast_to([128, NH, FD])),
                        deps=[t2b, t1])
                    t4 = p.op("act", lambda a: a.activation(
                        sq_t[:], cen[:], AF.Square), deps=[t3])
                    t5 = p.op("dve", lambda v: v.tensor_reduce(
                        var_t[:].unsqueeze(2), r3(sq_t[:]), AX.X, ALU.add),
                        deps=[t4])
                    t5b = p.op("dve", lambda v: v.tensor_scalar_mul(
                        var_t[:], var_t[:], 1.0 / FD), deps=[t5])
                    t6 = p.op("act", lambda a, eps_t=eps_t: a.activation(
                        sd_t[:], var_t[:], AF.Sqrt, bias=eps_t[:, 0:1]),
                        deps=[t5b, eps_tok])
                    t7 = p.op("dve", lambda v: v.reciprocal(rstd_t[:], sd_t[:]),
                              deps=[t6])
                    t8 = p.op("dve", lambda v: v.tensor_mul(
                        r3(cen[:]), r3(cen[:]),
                        rstd_t[:].unsqueeze(2).broadcast_to([128, NH, FD])),
                        deps=[t7, t3])
                    t9 = p.op("dve", lambda v, gcol=gcol: v.tensor_mul(
                        r3(cen[:]), r3(cen[:]),
                        gcol[:].unsqueeze(1).broadcast_to([128, NH, FD])),
                        deps=[t8] + gb_deps)
                    t10 = p.op("dve", lambda v, dst=dst, bcol=bcol: v.tensor_add(
                        r3(dst[:]), r3(cen[:]),
                        bcol[:].unsqueeze(1).broadcast_to([128, NH, FD])),
                        deps=[t9])
                    return t10

                tkn = fmln(PSB[:, 256:512], kraw, kn2, gam_sb, bet_sb, [t_k])
                if own:
                    tqn = fmln(PSB[:, 0:256], qraw, qn2, gq_sb, bq_sb,
                               [t_q, tkn])

                r_phi = lambda x: x.rearrange("p (h d) -> p h d", d=FD)
                tphik = p.op("dve", lambda v, r_phi=r_phi: v.tensor_tensor(
                    phik[:].rearrange("p (h d e) -> p h d e", d=FD, e=FD),
                    r_phi(kn2[:]).unsqueeze(3).broadcast_to([128, NH, FD, FD]),
                    r_phi(kn2[:]).unsqueeze(2).broadcast_to([128, NH, FD, FD]),
                    ALU.mult), deps=[tkn])
                if own:
                    tphiq = p.op("dve", lambda v, r_phi=r_phi: v.tensor_tensor(
                        phiq[:].rearrange("p (h d e) -> p h d e", d=FD, e=FD),
                        r_phi(qn2[:]).unsqueeze(3).broadcast_to([128, NH, FD, FD]),
                        r_phi(qn2[:]).unsqueeze(2).broadcast_to([128, NH, FD, FD]),
                        ALU.mult), deps=[tqn])

                    trq = [p.op("pe", lambda t, h=h: t.transpose(
                        PSA_bf[0:16, h * 128:h * 128 + 128],
                        qn2[:, h * FD:h * FD + FD], identb[:]),
                        deps=[tqn] + identb_deps + eng(("act",)))
                        for h in range(NH)]
                    tevq = p.op("act", lambda a: a.copy(
                        qfm[:], PSA_bf[0:16, 0:2048]), deps=trq)
                    trk = [p.op("pe", lambda t, h=h: t.transpose(
                        PSA_bf[32:48, h * 128:h * 128 + 128],
                        kn2[:, h * FD:h * FD + FD], identb[:]),
                        deps=[tkn] + identb_deps + eng(("act",)))
                        for h in range(NH)]
                    tevk = p.op("act", lambda a: a.copy(
                        kfm[:], PSA_bf[32:48, 0:2048]), deps=trk)
                    phifm_toks = []
                    for rnd in range(2):
                        trs = [p.op("pe", lambda t, fi=rnd * 16 + jj, jj=jj: t.transpose(
                            PSA_bf[:, 2048 + jj * 128:2048 + jj * 128 + 128],
                            phiq[:, fi * 128:fi * 128 + 128], identb[:]),
                            deps=[tphiq] + identb_deps + eng(("act",)))
                            for jj in range(16)]
                        tev = p.op("act", lambda a, rnd=rnd: a.copy(
                            phiqfm[:, rnd * 2048:rnd * 2048 + 2048],
                            PSA_bf[:, 2048:4096]), deps=trs)
                        phifm_toks.append(tev)

                    sq_toks = []
                    for grp in range(4):
                        mms = [p.op("pe", lambda t, h=grp * 4 + hh, hh=hh: t.matmul(
                            PSB[:, 512 + hh * 128:512 + hh * 128 + 128],
                            kfm[0:16, h * 128:h * 128 + 128],
                            qfm[0:16, h * 128:h * 128 + 128],
                            start=True, stop=True),
                            deps=[tevq, tevk] + eng(("dve",)))
                            for hh in range(4)]
                        tm1 = p.op("dve", lambda v: v.tensor_tensor(
                            sm_f[:].rearrange("p (g t) -> p g t", g=4),
                            PSB[:, 512:1024].rearrange("p (g t) -> p g t", g=4),
                            mask01[:].unsqueeze(1).broadcast_to([128, 4, 128]),
                            ALU.mult),
                            deps=mms + mask_deps + sq_toks[-1:])
                        tm2 = p.op("act", lambda a, grp=grp: a.activation(
                            s_sb[:, grp * 512:grp * 512 + 512], sm_f[:],
                            AF.Square), deps=[tm1])
                        sq_toks.append(tm2)

                    o_toks = []
                    z_toks = []
                    for h in range(NH):
                        j, pb = h // 2, 64 * (h % 2)
                        tp = (0, pb)
                        m1 = p.op("pe", lambda t, h=h, j=j, pb=pb, tp=tp: t.matmul(
                            PSA[pb:pb + 64, j * 128:j * 128 + 128],
                            S_bf[0][:, h * 65:h * 65 + 64],
                            phiqfm[:, (2 * h) * 128:(2 * h) * 128 + 128],
                            start=True, stop=False, tile_position=tp),
                            deps=phifm_toks + s_up + eng(("dve",)))
                        m2 = p.op("pe", lambda t, h=h, j=j, pb=pb, tp=tp: t.matmul(
                            PSA[pb:pb + 64, j * 128:j * 128 + 128],
                            S_bf[1][:, h * 65:h * 65 + 64],
                            phiqfm[:, (2 * h + 1) * 128:(2 * h + 1) * 128 + 128],
                            start=False, stop=False, tile_position=tp), deps=[])
                        m3 = p.op("pe", lambda t, h=h, j=j, pb=pb, tp=tp: t.matmul(
                            PSA[pb:pb + 64, j * 128:j * 128 + 128],
                            v_aug[:, h * 65:h * 65 + 64],
                            s_sb[:, h * 128:h * 128 + 128],
                            start=False, stop=True, tile_position=tp),
                            deps=v_deps + sq_toks)
                        o_toks.append(m3)
                        z1 = p.op("pe", lambda t, h=h, j=j, pb=pb, tp=tp: t.matmul(
                            PSA[pb:pb + 1, 1024 + j * 128:1024 + j * 128 + 128],
                            S_bf[0][:, h * 65 + 64:h * 65 + 65],
                            phiqfm[:, (2 * h) * 128:(2 * h) * 128 + 128],
                            start=True, stop=False, tile_position=tp), deps=[])
                        z2 = p.op("pe", lambda t, h=h, j=j, pb=pb, tp=tp: t.matmul(
                            PSA[pb:pb + 1, 1024 + j * 128:1024 + j * 128 + 128],
                            S_bf[1][:, h * 65 + 64:h * 65 + 65],
                            phiqfm[:, (2 * h + 1) * 128:(2 * h + 1) * 128 + 128],
                            start=False, stop=False, tile_position=tp), deps=[])
                        z3 = p.op("pe", lambda t, h=h, j=j, pb=pb, tp=tp: t.matmul(
                            PSA[pb:pb + 1, 1024 + j * 128:1024 + j * 128 + 128],
                            v_aug[:, h * 65 + 64:h * 65 + 65],
                            s_sb[:, h * 128:h * 128 + 128],
                            start=False, stop=True, tile_position=tp), deps=[])
                        z_toks.append(z3)

                    tz1 = p.op("act", lambda a: a.copy(
                        zrow[0:1, 0:1024], PSA[0:1, 1024:2048]), deps=z_toks)
                    tz2 = p.op("act", lambda a: a.copy(
                        zrow[64:65, 0:1024], PSA[64:65, 1024:2048]),
                        deps=z_toks)
                    zr = zrow.ap()
                    zpitch = zr.ap[0][0]
                    zsrc = bass.AP(tensor=zr.tensor, offset=zr.offset,
                                   ap=[[64 * zpitch, 2], [128, 8], [1, 128]])
                    tz3 = p.dma("spq",
                                zasm[:].rearrange("p (j t) -> p j t", t=128),
                                zsrc, deps=[tz1, tz2])
                    tz4 = p.op("dve", lambda v: v.tensor_scalar_add(
                        zasm[:], zasm[:], 1e-5), deps=[tz3])
                    tz5 = p.op("dve", lambda v: v.reciprocal(
                        zasm[:], zasm[:]), deps=[tz4])
                    tz6 = p.op("dve", lambda v: v.tensor_copy(
                        zinv[:], zasm[:]), deps=[tz5])
                    zb_mms = [p.op("pe", lambda t, j=j: t.matmul(
                        PSB[:, j * 128:j * 128 + 128], sel2[:],
                        zinv[:, j * 128:j * 128 + 128],
                        start=True, stop=True),
                        deps=[tz6] + sel2_deps + eng(("act",)))
                        for j in range(8)]
                    tz7 = p.op("act", lambda a: a.copy(
                        zb_sb[:], PSB[:, 0:1024]), deps=zb_mms)
                    tev2 = p.op("dve", lambda v, cc=cc: v.tensor_tensor(
                        attn_all[:].rearrange("p (j t) -> p j t", t=512)[:, :, cc * 128:cc * 128 + 128],
                        PSA[:, 0:1024].rearrange("p (j t) -> p j t", t=128),
                        zb_sb[:].rearrange("p (j t) -> p j t", t=128),
                        ALU.mult), deps=[tz7] + o_toks)
                    attn_toks[cc] = tev2

                upd_deps = [tphik] + v_deps + \
                    ([tev2, tz1, tz2, tz7] if own else []) + eng(("dve",))
                st_mms = []
                for h in range(NH):
                    for m in range(2):
                        tgt = PSA if m == 0 else PSB
                        mm = p.op("pe", lambda t, h=h, m=m, tgt=tgt: t.matmul(
                            tgt[:, h * 128:h * 128 + 65],
                            phik[:, h * 256 + m * 128:h * 256 + m * 128 + 128],
                            v_aug[:, h * 65:h * 65 + 65],
                            start=True, stop=True), deps=upd_deps)
                        st_mms.append(mm)
                up_toks = []
                for m in range(2):
                    tgt = PSA if m == 0 else PSB
                    t1 = p.op("dve", lambda v, m=m, tgt=tgt: v.tensor_add(
                        S_m[m][:].rearrange("p (h x) -> p h x", x=65),
                        S_m[m][:].rearrange("p (h x) -> p h x", x=65),
                        tgt[:].rearrange("p (h x) -> p h x", x=128)[:, :, 0:65]),
                        deps=st_mms + s_up)
                    t2 = p.op("dve", lambda v, m=m: v.tensor_copy(
                        S_bf[m][:], S_m[m][:]),
                        deps=[t1] + (o_toks + z_toks if own else []))
                    up_toks += [t1, t2]
                s_up = up_toks

            if own:
                for mch in range(8):
                    for j in range(8):
                        p.op("pe", lambda t, mch=mch, j=j: t.matmul(
                            PSB[:, 1024:1536],
                            wo_sb[:, j * D + mch * 128:j * D + mch * 128 + 128],
                            attn_all[:, j * 512:j * 512 + 512],
                            start=(j == 0), stop=(j == 7)),
                            deps=[lw[3]] + [attn_toks[cc] for cc in range(4)] +
                            eng(("act", "dve")))
                    tmch = ("pe", p.count["pe"])
                    tev = p.op("act", lambda a: a.copy(a0ev[:], PSB[:, 1024:1536]),
                               deps=[tmch] + a0_store_prev)
                    sot = p.dma("spq",
                                io["a0_fm"][mch * 128:(mch + 1) * 128,
                                       t0 - TPRE:t0 - TPRE + 512],
                                a0ev[:], deps=[tev])
                    a0_store_prev = [sot]
            prev_tile = p.barrier_token()

        return p.barrier_token()


def build_l0attn(TPRE, T):
    nc = bass.Bass("TRN2", target_bir_lowering=False, debug=False)
    TTOT = TPRE + T
    io = {
        "xn_bf": nc.dram_tensor("xn_bf", [D, TTOT], BF16, kind="ExternalInput").ap(),
        "wq_t": nc.dram_tensor("wq_t", [D, 256], F32, kind="ExternalInput").ap(),
        "wk_t": nc.dram_tensor("wk_t", [D, 256], F32, kind="ExternalInput").ap(),
        "wv_t": nc.dram_tensor("wv_t", [D, D], F32, kind="ExternalInput").ap(),
        "wo_t": nc.dram_tensor("wo_t", [D, D], F32, kind="ExternalInput").ap(),
        "gam16": nc.dram_tensor("gam16", [128, 16], F32, kind="ExternalInput").ap(),
        "bet16": nc.dram_tensor("bet16", [128, 16], F32, kind="ExternalInput").ap(),
        "a0_fm": nc.dram_tensor("a0_fm", [D, T], F32, kind="ExternalOutput").ap(),
    }
    es = ExitStack()
    with es:
        p = Prog(nc, es)
        idb, idb_deps = emit_identity(p, es, nc, "identb0", BF16)
        shared = {"identb": idb, "identb_deps": idb_deps}
        emit_l0attn(p, es, nc, alloc, alloc_ps, io, shared, TPRE, T)
        block = es.enter_context(nc.Block())
        finish(p, block, p.barrier_token())
        p.build(block)
    return nc


def emit_rot64(p, es, nc, tag):
    P = alloc(es, nc, f"{tag}_P64", [64, 64], BF16)
    t0 = p.op("dve", lambda v: v.memset(P[:], 0.0))
    t1 = p.op("pool", lambda g: g.affine_select(
        out=P[:], in_=P[:], pattern=[[1, 64]], compare_op=ALU.not_equal,
        fill=1.0, base=-32, channel_multiplier=-1), deps=[t0])
    t2 = p.op("pool", lambda g: g.affine_select(
        out=P[:], in_=P[:], pattern=[[-1, 64]], compare_op=ALU.not_equal,
        fill=-1.0, base=-32, channel_multiplier=1), deps=[t1])
    return P, [t2]


def emit_planes_window(p, sc, pos_sb, pos_deps, base, pl_out, tag):
    tpos, fr, cs, sn, tmp, ifq, lnb, scl, isc = (
        sc["tpos"], sc["fr"], sc["cs"], sc["sn"], sc["tmp"], sc["ifq"],
        sc["lnb"], sc["scl"], sc["isc"])
    TWO_PI = float(2.0 * np.pi)
    deps0 = list(pos_deps) + sc["prev"]
    t5 = p.op("pool", lambda g, base=base: g.iota(
        tpos[:], pattern=[[1, 512]], base=base, channel_multiplier=0,
        allow_small_or_imprecise_dtypes=True), deps=deps0)
    t6 = p.op("dve", lambda v: v.tensor_scalar_add(
        tpos[:], tpos[:], pos_sb[0:128, 0:1]), deps=[t5] + deps0)
    t7 = p.op("dve", lambda v: v.tensor_scalar_mul(
        fr[:], tpos[:], ifq[:, 0:1]), deps=[t6] + sc["const_deps"])
    MAGIC = 12582912.0

    def red_sin(dst, src_expr_tok):
        ta = p.op("dve", lambda v: v.tensor_single_scalar(
            tmp[:], dst[:], 1.0 / TWO_PI, ALU.mult), deps=[src_expr_tok])
        tb = p.op("dve", lambda v: v.tensor_single_scalar(
            tmp[:], tmp[:], MAGIC, ALU.add), deps=[ta])
        tc = p.op("dve", lambda v: v.tensor_single_scalar(
            tmp[:], tmp[:], MAGIC, ALU.subtract), deps=[tb])
        td = p.op("dve", lambda v: v.scalar_tensor_tensor(
            dst[:], tmp[:], -TWO_PI, dst[:], ALU.mult, ALU.add), deps=[tc])
        te = p.op("act", lambda a: a.activation(dst[:], dst[:], AF.Sin),
                  deps=[td])
        return te

    t7b = p.op("dve", lambda v: v.tensor_copy(sn[:], fr[:]), deps=[t7])
    t11 = red_sin(sn, t7b)
    t12 = p.op("dve", lambda v: v.tensor_scalar(
        cs[:], fr[:], -1.0, float(np.pi / 2.0), ALU.mult, ALU.add),
        deps=[t7, t11])
    t16 = red_sin(cs, t12)
    t17 = p.op("dve", lambda v: v.tensor_scalar(
        tmp[:], tpos[:], 1.0 / 256.0, -4096.0 / 256.0, ALU.mult, ALU.add),
        deps=[t6, t16])
    t18 = p.op("dve", lambda v: v.tensor_scalar_mul(
        scl[:], tmp[:], lnb[:, 0:1]), deps=[t17] + sc["const_deps"])
    t19 = p.op("act", lambda a: a.activation(isc[:], scl[:], AF.Exp,
                                             scale=-1.0), deps=[t18])
    t20 = p.op("act", lambda a: a.activation(scl[:], scl[:], AF.Exp),
               deps=[t19])
    o1 = p.op("dve", lambda v: v.tensor_mul(pl_out[0], cs[:], scl[:]),
              deps=[t16, t20])
    o2 = p.op("dve", lambda v: v.tensor_mul(pl_out[1], sn[:], scl[:]),
              deps=[t11, t20])
    o3 = p.op("dve", lambda v: v.tensor_mul(pl_out[2], cs[:], isc[:]),
              deps=[t16, t19])
    o4 = p.op("dve", lambda v: v.tensor_mul(pl_out[3], sn[:], isc[:]),
              deps=[t11, t19])
    sc["prev"] = [o1, o2, o3, o4]
    return [o1, o2, o3, o4]


def emit_l0ffn(p, es, nc, A, APS, io, shared, TIN1, TOUT1, TOUT2, prev=(), bnd_windows=(None, None)):
    TIN2 = TOUT1
    eng = p.engines_token
    if True:
        POOL1 = A(es, nc, "POOL1", [128, 8 * 2 * FFN_H], BF16)
        POOL2 = A(es, nc, "POOL2", [128, FFN_HC * D], BF16)
        P1F = POOL1[:].bitcast(F32)
        _f = lambda a, b: P1F[0:128, a:b]
        qraw = [_f(0, 512), _f(3072, 3584)]
        nrm = [_f(1024, 1536), _f(4096, 4608)]
        qn_f = [_f(1536, 2048), _f(4608, 5120)]
        rt1 = [_f(2048, 2560), _f(5120, 5632)]
        rt2 = [_f(2560, 3072), _f(5632, 6144)]
        psc_f32 = {k: _f(6144 + i * 512, 6656 + i * 512) for i, k in
                   enumerate(("tpos", "fr", "cs", "sn", "tmp", "scl", "isc"))}
        maskP = [_f(9728 + qt * 512, 10240 + qt * 512) for qt in range(4)]
        maskC = _f(11776, 11904)
        _b = lambda a, b: POOL1[0:128, a:b]
        qf_all = _b(23808, 25856)
        kr_all = _b(25856, 29952)
        vr = _b(29952, 34048)
        p_sb = [_b(34048, 35072), _b(35072, 36096)]
        pt_sb = [_b(36096, 37120), _b(37120, 38144)]
        attn_w = _b(38144, 40192)
        qn_b = [_b(40192, 40704), _b(40704, 41216)]
        sqb = [_b(41216, 41728), _b(41728, 42240)]
        PLN = _b(42240, 44288)
        xw = [A(es, nc, f"xw{c}", [128, 513], F32) for c in range(8)]
        hw = [POOL2[0:128, 12288 + c * 512:12288 + (c + 1) * 512]
              for c in range(8)]
        sum128 = A(es, nc, "f_sum128", [128, 128], BF16)
        P128 = A(es, nc, "f_P128", [128, 128], BF16)
        mx_c = [A(es, nc, f"f_mx{i}", [128, 1], F32) for i in range(2)]
        nm_c = [A(es, nc, f"f_nm{i}", [128, 1], F32) for i in range(2)]
        l_c = [A(es, nc, f"f_l{i}", [128, 1], F32) for i in range(2)]
        li_c = [A(es, nc, f"f_li{i}", [128, 1], F32) for i in range(2)]
        xnew = [A(es, nc, f"f_xnew{i}", [128, 512], F32) for i in range(2)]
        he_sb = A(es, nc, "f_he", [128, FFN_HC * 512], BF16)
        gelu_sb = [A(es, nc, f"f_gelu{i}", [128, 512], F32) for i in range(2)]
        hn_bf = [A(es, nc, f"f_hn{c}", [128, 512], BF16) for c in range(8)]
        ln_sq = A(es, nc, "f_lnsq", [128, 512], F32)
        mu_sb = A(es, nc, "f_musb", [1, 512], F32)
        mu2_sb = A(es, nc, "f_mu2sb", [1, 512], F32)
        var_sb = A(es, nc, "f_varsb", [1, 512], F32)
        sd_sb = A(es, nc, "f_sdsb", [1, 512], F32)
        a_sb = A(es, nc, "f_asb", [1, 512], F32)
        b_sb = A(es, nc, "f_bsb", [1, 512], F32)
        tmpc = ln_sq
        qs_sb = A(es, nc, "f_qssb", [128, 2], F32)
        ks_sb = A(es, nc, "f_kssb", [128, 2], F32)
        lg_sb = A(es, nc, "f_lgsb", [128, 16], F32)
        lb_sb = A(es, nc, "f_lbsb", [128, 16], F32)
        b1_sb = A(es, nc, "f_b1sb", [128, 4 * FFN_HC], F32)
        b2_sb = A(es, nc, "f_b2sb", [128, 16], F32)
        pos_sb = A(es, nc, "f_possb", [128, 1], F32)
        zc_sb = A(es, nc, "f_zc", [128, 8], F32)
        psc = dict(psc_f32)
        psc["ifq"] = A(es, nc, "f_psc_ifq", [128, 1], F32)
        psc["lnb"] = A(es, nc, "f_psc_lnb", [128, 1], F32)
        psc["prev"] = []

        PSA = APS(es, nc, "fPSA", [128, 2048], F32)
        PSB = APS(es, nc, "fPSB", [128, 2048], F32)
        PSB_bf = PSB[:].bitcast(BF16)

        bnd_sb = A(es, nc, "f_bndsb", [128, 2], F32)
        if "f_bnd" in io:
            lbnd = p.dma("spq", bnd_sb[:], io["f_bnd"])
        lds = [p.dma("spq", qs_sb[:], io["f_qs"]), p.dma("spq", ks_sb[:], io["f_ks"]),
               p.dma("spq", lg_sb[:], io["f_lg"]), p.dma("spq", lb_sb[:], io["f_lb"]),
               p.dma("spq", b1_sb[:], io["f_b1"]), p.dma("spq", b2_sb[:], io["f_b2"]),
               p.dma("spq", pos_sb[:], io["f_pos"])]
        consts = make_const_tiles(p, es, nc)
        identb2, identb2_deps = shared['identb'], shared['identb_deps']
        eps24_t, eps24_tok = p.constf(1e-24)
        c8_t, c8_tok = p.constf(8.0)

        ts0 = p.op("dve", lambda v: v.memset(sum128[:], 0.0))
        ts1 = p.op("dve", lambda v: v.memset(sum128[0:64, 0:64], 1.0),
                   deps=[ts0])
        ts2 = p.op("dve", lambda v: v.memset(sum128[64:128, 64:128], 1.0),
                   deps=[ts1])
        sum_deps = [ts2]
        tr0 = p.op("dve", lambda v: v.memset(P128[:], 0.0))
        tr1 = p.op("pool", lambda g: g.affine_select(
            out=P128[:], in_=P128[:], pattern=[[1, 128]],
            compare_op=ALU.not_equal, fill=1.0, base=-32,
            channel_multiplier=-1), deps=[tr0])
        tr2 = p.op("pool", lambda g: g.affine_select(
            out=P128[:], in_=P128[:], pattern=[[-1, 128]],
            compare_op=ALU.not_equal, fill=-1.0, base=-32,
            channel_multiplier=1), deps=[tr1])
        tr3 = p.op("dve", lambda v: v.memset(P128[32:64, 64:96], 0.0),
                   deps=[tr2])
        tr4 = p.op("dve", lambda v: v.memset(P128[64:96, 32:64], 0.0),
                   deps=[tr3])
        P128_deps = [tr4]

        tpc0 = p.op("pool", lambda g: g.iota(
            psc["ifq"][:], pattern=[[0, 1]], base=0, channel_multiplier=1,
            allow_small_or_imprecise_dtypes=True))
        tpcA = p.op("dve", lambda v: v.tensor_single_scalar(
            psc["lnb"][:], psc["ifq"][:], 63.5, ALU.is_gt), deps=[tpc0])
        tpcB = p.op("dve", lambda v: v.scalar_tensor_tensor(
            psc["ifq"][:], psc["lnb"][:], -64.0, psc["ifq"][:],
            ALU.mult, ALU.add), deps=[tpcA])
        tpc1a = p.op("dve", lambda v: v.tensor_single_scalar(
            psc["lnb"][:], psc["ifq"][:], 31.5, ALU.is_gt), deps=[tpcB])
        tpc1 = p.op("dve", lambda v: v.scalar_tensor_tensor(
            psc["ifq"][:], psc["lnb"][:], -32.0, psc["ifq"][:],
            ALU.mult, ALU.add), deps=[tpc1a])
        tpc2 = p.op("dve", lambda v: v.tensor_scalar(
            psc["lnb"][:], psc["ifq"][:], 2.0 / 89.6, 25.6 / 89.6,
            ALU.mult, ALU.add), deps=[tpc1])
        tpc3 = p.op("act", lambda a: a.activation(
            psc["lnb"][:], psc["lnb"][:], AF.Ln), deps=[tpc2])
        tpc4 = p.op("act", lambda a: a.activation(
            psc["ifq"][:], psc["ifq"][:], AF.Exp,
            scale=float(-2.0 / 64.0 * np.log(10000.0))), deps=[tpc1, tpc3])
        psc["const_deps"] = [tpc3, tpc4]

        def build_masks(prev):
            deps = []
            for qt in range(4):
                mt = maskP[qt]
                t0_ = p.op("dve", lambda v, mt=mt: v.memset(mt[:], 0.0),
                           deps=prev)
                t1_ = p.op("pool", lambda g, mt=mt, qt=qt: g.affine_select(
                    out=mt[:], in_=mt[:], pattern=[[1, 512]],
                    compare_op=ALU.is_ge, fill=-1e30, base=-128 * qt,
                    channel_multiplier=-1), deps=[t0_])
                deps.append(t1_)
            t0_ = p.op("dve", lambda v: v.memset(maskC[:], 0.0), deps=prev)
            tmc = p.op("pool", lambda g: g.affine_select(
                out=maskC[:], in_=maskC[:], pattern=[[-1, 128]],
                compare_op=ALU.is_ge, fill=-1e30, base=0,
                channel_multiplier=1), deps=[t0_])
            return deps + [tmc]

        tzc = p.op("dve", lambda v: v.memset(zc_sb[:], 0.0))
        ez1 = p.dma("spq", io["SCR_a"][:, 0:1].rearrange("(c p) o -> p c o", p=128),
                    zc_sb[:, 0:8].unsqueeze(2), deps=[tzc],
                    allow_slow_non_contiguous=True)
        ez2 = p.dma("spq", io["SCR_y1"][:, 0:1].rearrange("(c p) o -> p c o", p=128),
                    zc_sb[:, 0:8].unsqueeze(2), deps=[tzc],
                    allow_slow_non_contiguous=True)

        def run_layer(lidx, xin_ap, xin_T, yout_ap, yout_T, loc0, prev_all,
                      final_store):
            NW = xin_T // 512
            W_B0 = (xin_T - yout_T) // 512
            lqkv = p.dma("poolq",
                         POOL2[:, 0:8 * 1536].rearrange("p (c m) -> p c m", c=8),
                         io["f_wqkv"][lidx].rearrange("(c p) m -> p c m", p=128),
                         deps=prev_all)
            lwo = p.dma("poolq",
                        POOL2[:, 16384:16384 + 4 * D].rearrange(
                            "p (c m) -> p c m", c=4),
                        io["f_wo"][lidx].rearrange("(c p) m -> p c m", p=128),
                        deps=prev_all)
            mask_deps = build_masks(prev_all)
            prev_w = list(prev_all) + mask_deps
            pchain = [None, None]
            for w in range(NW):
                has_prev = (w >= 1)
                do_attn = (w >= 1) or (loc0 == 0)
                ldx = [p.dma("spq", xw[c][:],
                             xin_ap[c * 128:(c + 1) * 128,
                                    512 * w:512 * w + 513],
                             deps=prev_w) for c in range(8)]
                hcast = [p.op("dve", lambda v, c=c, sl=(slice(1, 513) if c < 4
                                                        else slice(0, 512)):
                              v.tensor_copy(hw[c][:], xw[c][:, sl]),
                              deps=[ldx[c]]) for c in range(8)]
                wpl = [PLN[0:128, i * 512:(i + 1) * 512] for i in range(4)]
                wpl_deps = emit_planes_window(
                    p, psc, pos_sb, [lds[6]] + list(prev_w), loc0 + 512 * w,
                    wpl, f"pl{lidx}_{w}")
                v_done = []
                tev = None
                for sub in range(4):
                    slot = (4 * w + sub) % 8
                    for c in range(8):
                        p.op("pe", lambda t, c=c, sub=sub: t.matmul(
                            PSB[:, 0:512], hw[c][:, sub * 128:sub * 128 + 128],
                            POOL2[:, c * 1536 + 1024:c * 1536 + 1536],
                            start=(c == 0), stop=(c == 7)),
                            deps=[lqkv] + hcast + ([tev] if tev else list(prev_w)))
                    tv = ("pe", p.count["pe"])
                    tev = p.op("act", lambda a, slot=slot: a.copy(
                        vr[:, slot * 512:slot * 512 + 512], PSB[:, 0:512]),
                        deps=[tv])
                    v_done.append(tev)
                ci = 0
                qk_done = []
                for pr in range(4):
                    for which in ("q", "k"):
                        if which == "q" and not do_attn:
                            continue
                        pi = ci % 2
                        ci += 1
                        pc = pchain[pi]
                        qc0 = (0 if which == "q" else 512) + pr * 128
                        scol = (qs_sb if which == "q" else ks_sb)[:, lidx:lidx + 1]
                        pcs = wpl[0] if which == "q" else wpl[2]
                        pss = wpl[1] if which == "q" else wpl[3]
                        if which == "q":
                            dst, dcol = qf_all, pr * 512
                        else:
                            dst, dcol = kr_all, pr * 1024 + (w % 2) * 512
                        PROJ = PSA[:, pi * 512:pi * 512 + 512]
                        MM = PSA[:, 1024 + pi * 512:1536 + pi * 512]
                        prj_dep = [lqkv] + hcast + \
                            ([pc["te1"]] if pc else list(prev_w))
                        for c in range(8):
                            p.op("pe", lambda t, c=c, qc0=qc0, PROJ=PROJ: t.matmul(
                                PROJ,
                                POOL2[:, c * 1536 + qc0:c * 1536 + qc0 + 128],
                                hw[c][:], start=(c == 0), stop=(c == 7)),
                                deps=prj_dep)
                        tpj = ("pe", p.count["pe"])
                        te1 = p.op("act", lambda a, pi=pi, PROJ=PROJ: a.copy(
                            qraw[pi][:], PROJ),
                            deps=[tpj] + ([pc["te7"]] if pc else []))
                        te2 = p.op("act", lambda a, pi=pi: a.activation(
                            sqb[pi][:], qraw[pi][:], AF.Square),
                            deps=[te1] + ([pc["mm1"]] if pc else []))
                        mm1 = p.op("pe", lambda t, pi=pi, MM=MM: t.matmul(
                            MM, sum128[:], sqb[pi][:], start=True, stop=True),
                            deps=[te2] + sum_deps +
                            ([pc["te10"]] if pc else []))
                        te4 = p.op("act", lambda a, pi=pi, MM=MM: a.activation(
                            nrm[pi][:], MM, AF.Sqrt), deps=[mm1])
                        te5 = p.op("dve", lambda v, pi=pi: v.tensor_scalar_max(
                            nrm[pi][:], nrm[pi][:], 1e-12), deps=[te4])
                        te6 = p.op("dve", lambda v, pi=pi: v.reciprocal(
                            nrm[pi][:], nrm[pi][:]), deps=[te5])
                        te7 = p.op("dve", lambda v, pi=pi, scol=scol:
                                   v.scalar_tensor_tensor(
                            qn_f[pi][:], qraw[pi][:], scol, nrm[pi][:],
                            ALU.mult, ALU.mult), deps=[te6] + lds)
                        te8 = p.op("dve", lambda v, pi=pi: v.tensor_copy(
                            qn_b[pi][:], qn_f[pi][:]),
                            deps=[te7] + ([pc["mm2"]] if pc else []))
                        mm2 = p.op("pe", lambda t, pi=pi, MM=MM: t.matmul(
                            MM, P128[:], qn_b[pi][:], start=True, stop=True),
                            deps=[te8, te4] + P128_deps)
                        te9 = p.op("dve", lambda v, pi=pi, pcs=pcs: v.tensor_mul(
                            rt1[pi][:], qn_f[pi][:], pcs),
                            deps=[te7] + wpl_deps +
                            ([pc["te12"]] if pc else []))
                        te10 = p.op("dve", lambda v, pi=pi, pss=pss, MM=MM:
                                    v.tensor_mul(rt2[pi][:], MM, pss),
                                    deps=[mm2] + wpl_deps +
                                    ([pc["te12"]] if pc else []))
                        te12 = p.op("dve", lambda v, pi=pi, dst=dst, dcol=dcol:
                                    v.tensor_add(
                            dst[0:128, dcol:dcol + 512], rt1[pi][:],
                            rt2[pi][:]), deps=[te9, te10])
                        pchain[pi] = dict(te1=te1, te7=te7, mm1=mm1,
                                          te10=te10, mm2=mm2, te12=te12)
                        qk_done.append(te12)
                attn_evac = []
                if do_attn:
                    sc_free = [[], []]
                    p_free = [[], []]
                    pt_free = [[], []]
                    si = 0
                    tevo_prev = []
                    for qt in range(4):
                        cur = (qt + 1) * 128
                        o_grp = []
                        for h in range(8):
                            pi = si % 2
                            si += 1
                            pr, ro = h // 2, 64 * (h % 2)
                            SC = PSA[:, pi * 1024:pi * 1024 + 1024]
                            span = (512 + cur) if has_prev else cur
                            qsl = qf_all[ro:ro + 64,
                                         pr * 512 + qt * 128:
                                         pr * 512 + qt * 128 + 128]
                            smm_dep = qk_done + sc_free[pi]
                            if has_prev:
                                kprev = kr_all[ro:ro + 64,
                                               pr * 1024 + ((w - 1) % 2) * 512:
                                               pr * 1024 + ((w - 1) % 2) * 512 + 512]
                                p.op("pe", lambda t, SC=SC, qsl=qsl,
                                     kprev=kprev: t.matmul(
                                    SC[:, 0:512], qsl, kprev,
                                    start=True, stop=True), deps=smm_dep)
                            cb = 512 if has_prev else 0
                            kcur = kr_all[ro:ro + 64,
                                          pr * 1024 + (w % 2) * 512:
                                          pr * 1024 + (w % 2) * 512 + cur]
                            p.op("pe", lambda t, SC=SC, qsl=qsl, kcur=kcur,
                                 cb=cb, cur=cur: t.matmul(
                                SC[:, cb:cb + cur], qsl, kcur,
                                start=True, stop=True), deps=smm_dep)
                            tsm = ("pe", p.count["pe"])
                            mk = []
                            if has_prev:
                                mk.append(p.op("dve", lambda v, SC=SC, qt=qt:
                                               v.tensor_add(
                                    SC[:, 0:512], SC[:, 0:512], maskP[qt][:]),
                                    deps=[tsm] + mask_deps))
                                if "f_bnd" in io and w == bnd_w[lidx]:
                                    mk.append(p.op("dve", lambda v, SC=SC,
                                                   lidx=lidx: v.tensor_scalar_add(
                                        SC[:, 0:512], SC[:, 0:512],
                                        bnd_sb[:, lidx:lidx + 1]),
                                        deps=mk + [lbnd]))
                            mk.append(p.op("dve", lambda v, SC=SC, span=span:
                                           v.tensor_add(
                                SC[:, span - 128:span],
                                SC[:, span - 128:span], maskC[:]),
                                deps=[tsm] + mask_deps + mk))
                            t_mx = p.op("dve", lambda v, SC=SC, span=span,
                                        pi=pi: v.reduce_max(
                                mx_c[pi][:], SC[:, 0:span], axis=AX.X),
                                deps=mk + [tsm])
                            t_nm = p.op("dve", lambda v, pi=pi: v.tensor_scalar_mul(
                                nm_c[pi][:], mx_c[pi][:], -8.0), deps=[t_mx])
                            t_ex = p.op("act", lambda a, SC=SC, span=span,
                                        pi=pi: a.activation(
                                p_sb[pi][:, 0:span], SC[:, 0:span], AF.Exp,
                                bias=nm_c[pi][:, 0:1], scale=c8_t[:, 0:1],
                                accum_out=l_c[pi][:, 0:1]),
                                deps=[t_nm, c8_tok] + p_free[pi])
                            sc_free[pi] = [t_ex]
                            t_li = p.op("dve", lambda v, pi=pi: v.reciprocal(
                                li_c[pi][:], l_c[pi][:]), deps=[t_ex])
                            t_ps = p.op("dve", lambda v, pi=pi, span=span:
                                        v.tensor_scalar_mul(
                                p_sb[pi][:, 0:span], p_sb[pi][:, 0:span],
                                li_c[pi][:, 0:1]), deps=[t_li])
                            nsb = span // 128
                            PT = PSB_bf[:, 2048 + pi * 1024:
                                        2048 + pi * 1024 + 1024]
                            trs = [p.op("pe", lambda t, j=j, PT=PT, pi=pi:
                                        t.transpose(
                                PT[:, j * 128:j * 128 + 128],
                                p_sb[pi][:, j * 128:j * 128 + 128],
                                identb2[:]),
                                deps=[t_ps] + identb2_deps + pt_free[pi])
                                for j in range(nsb)]
                            t_pt = p.op("act", lambda a, nsb=nsb, PT=PT,
                                        pi=pi: a.copy(
                                pt_sb[pi][:, 0:nsb * 128], PT[:, 0:nsb * 128]),
                                deps=trs)
                            pt_free[pi] = [t_pt]
                            p_free[pi] = trs
                            kb0 = 4 * (w - 1) if has_prev else 4 * w
                            for j in range(nsb):
                                slot = (kb0 + j) % 8
                                p.op("pe", lambda t, j=j, slot=slot, h=h,
                                     nsb=nsb, pi=pi: t.matmul(
                                    PSB[64 * (h % 2):64 * (h % 2) + 64,
                                        512 + (h // 2) * 128:
                                        512 + (h // 2) * 128 + 128],
                                    vr[:, slot * 512 + h * 64:
                                       slot * 512 + h * 64 + 64],
                                    pt_sb[pi][:, j * 128:j * 128 + 128],
                                    start=(j == 0), stop=(j == nsb - 1),
                                    tile_position=(0, 64 * (h % 2))),
                                    deps=[t_pt] + v_done + o_grp + tevo_prev)
                            o_grp.append(("pe", p.count["pe"]))
                        tevo = p.op("act", lambda a, qt=qt: a.copy(
                            attn_w[:].rearrange("p (j t) -> p j t", t=512)
                            [:, :, qt * 128:qt * 128 + 128],
                            PSB[:, 512:1024].rearrange("p (j t) -> p j t",
                                                       t=128)),
                            deps=o_grp)
                        tevo_prev = [tevo]
                        attn_evac.append(tevo)
                    if w >= W_B0 - 1:
                        st_prev = [[], []]
                        txn_prev = [[], []]
                        for mch in range(8):
                            mpi = mch % 2
                            WO = PSA[:, mpi * 512:mpi * 512 + 512]
                            for j in range(4):
                                p.op("pe", lambda t, mch=mch, j=j, WO=WO:
                                     t.matmul(
                                    WO, POOL2[:, 16384 + j * D + mch * 128:
                                              16384 + j * D + mch * 128 + 128],
                                    attn_w[:, j * 512:j * 512 + 512],
                                    start=(j == 0), stop=(j == 3)),
                                    deps=[lwo] + attn_evac + txn_prev[mpi])
                            twoe = ("pe", p.count["pe"])
                            txn = p.op("dve", lambda v, mch=mch, mpi=mpi,
                                       WO=WO: v.tensor_add(
                                xnew[mpi][:], WO, xw[mch][:, 1:513]),
                                deps=[twoe] + st_prev[mpi])
                            txn_prev[mpi] = [txn]
                            sot = p.dma("spq",
                                        io["SCR_a"][mch * 128:(mch + 1) * 128,
                                              1 + 512 * w:1 + 512 * w + 512],
                                        xnew[mpi][:], deps=[txn])
                            st_prev[mpi] = [sot]
                prev_w = p.barrier_token()
            lw1a, lw1g = [], []
            for j in range(FFN_HC):
                lw1a.append(p.dma("poolq",
                    POOL1[:].rearrange("p (c m) -> p c m", c=8)
                    [:, :, j * 128:j * 128 + 128],
                    io["f_w1"][lidx].rearrange("(c p) m -> p c m", p=128)
                    [:, :, j * 128:j * 128 + 128], deps=prev_w))
                lw1g.append(p.dma("poolq",
                    POOL1[:].rearrange("p (c m) -> p c m", c=8)
                    [:, :, FFN_H + j * 128:FFN_H + j * 128 + 128],
                    io["f_w1"][lidx].rearrange("(c p) m -> p c m", p=128)
                    [:, :, FFN_H + j * 128:FFN_H + j * 128 + 128],
                    deps=prev_w))
            lw2c = [p.dma("poolq",
                    POOL2[:].rearrange("p (c m) -> p c m", c=FFN_HC)[:, j, :],
                    io["f_w2"][lidx].rearrange("(c p) m -> p c m", p=128)[:, j, :],
                    deps=prev_w) for j in range(FFN_HC)]
            y_stores = []
            prev_b = list(prev_w)
            oo = consts["oo1024"]
            ones_row = consts["ones_f32_row"]
            for ob in range(yout_T // 512):
                wb = W_B0 + ob
                ldx = [p.dma("spq", xw[c][:],
                             io["SCR_a"][c * 128:(c + 1) * 128,
                                   512 * wb:512 * wb + 513],
                             deps=prev_b) for c in range(8)]
                shs = [slice(1, 513) if c < 4 else slice(0, 512)
                       for c in range(8)]
                sq_toks = []
                for c in range(8):
                    ts_ = p.op("act", lambda a, c=c, sl=shs[c]: a.activation(
                        ln_sq[:], xw[c][:, sl], AF.Square),
                        deps=[ldx[c]] + eng(("pe", "dve")))
                    tm_ = p.op("pe", lambda t, c=c: t.matmul(
                        PSA[0:1, 1536:2048], oo[:], ln_sq[:],
                        start=(c == 0), stop=(c == 7)),
                        deps=[ts_] + consts["deps"])
                    sq_toks.append(ts_)
                t_ms = ("pe", p.count["pe"])
                for c in range(8):
                    p.op("pe", lambda t, c=c, sl=shs[c]: t.matmul(
                        PSA[0:1, 1024:1536], oo[:], xw[c][:, sl],
                        start=(c == 0), stop=(c == 7)),
                        deps=[ldx[c]] + consts["deps"] + eng(("act", "dve")))
                t_mu = ("pe", p.count["pe"])
                t1 = p.op("act", lambda a: a.copy(mu_sb[:], PSA[0:1, 1024:1536]),
                          deps=[t_mu])
                t2 = p.op("act", lambda a: a.activation(
                    mu2_sb[:], mu_sb[:], AF.Square), deps=[t1])
                t3 = p.op("dve", lambda v: v.tensor_sub(
                    var_sb[:], PSA[0:1, 1536:2048], mu2_sb[:]),
                    deps=[t_ms, t2])
                eps_t, eps_tok = p.constf(1e-5)
                t4 = p.op("act", lambda a, eps_t=eps_t: a.activation(
                    sd_sb[:], var_sb[:], AF.Sqrt, bias=eps_t[0:1, 0:1]),
                    deps=[t3, eps_tok])
                t5 = p.op("dve", lambda v: v.reciprocal(a_sb[:], sd_sb[:]),
                          deps=[t4])
                t6 = p.op("dve", lambda v: v.scalar_tensor_tensor(
                    b_sb[:], mu_sb[:], -1.0, a_sb[:], ALU.mult, ALU.mult),
                    deps=[t5, t1])
                t7 = p.op("pe", lambda t: t.matmul(
                    PSA[:, 0:512], ones_row[:], a_sb[:], start=True, stop=True),
                    deps=[t5] + consts["deps"] + eng(("act", "dve")))
                t8 = p.op("pe", lambda t: t.matmul(
                    PSA[:, 512:1024], ones_row[:], b_sb[:], start=True,
                    stop=True), deps=[t6])
                tab = ("pe", p.count["pe"])
                hn_toks = []
                for c in range(8):
                    u1 = p.op("dve", lambda v, c=c, sl=shs[c]: v.tensor_mul(
                        tmpc[:], xw[c][:, sl], PSA[:, 0:512]),
                        deps=[tab] + hn_toks[-1:])
                    u2 = p.op("dve", lambda v, c=c: v.tensor_add(
                        tmpc[:], tmpc[:], PSA[:, 512:1024]), deps=[u1])
                    u3 = p.op("dve", lambda v, c=c, lidx=lidx: v.tensor_scalar(
                        hn_bf[c][:], tmpc[:],
                        lg_sb[:, lidx * 8 + c:lidx * 8 + c + 1],
                        lb_sb[:, lidx * 8 + c:lidx * 8 + c + 1],
                        ALU.mult, ALU.add), deps=[u2] + lds)
                    hn_toks.append(u3)
                he_toks = []
                prev_he = [[], []]
                for j in range(FFN_HC):
                    jpi = j % 2
                    if jpi == 0:
                        RA, RG = PSB[:, 0:512], PSB[:, 512:1024]
                    else:
                        RA, RG = PSA[:, 1024:1536], PSA[:, 1536:2048]
                    extra = ([t1, t3] if jpi == 1 else [])
                    for c in range(8):
                        p.op("pe", lambda t, c=c, j=j, RA=RA: t.matmul(
                            RA, POOL1[:, c * 2 * FFN_H + j * 128:
                                  c * 2 * FFN_H + j * 128 + 128],
                            hn_bf[c][:], start=(c == 0), stop=(c == 7)),
                            deps=[lw1a[j], hn_toks[c]] + prev_he[jpi] + extra)
                    t_a = ("pe", p.count["pe"])
                    for c in range(8):
                        p.op("pe", lambda t, c=c, j=j, RG=RG: t.matmul(
                            RG, POOL1[:, c * 2 * FFN_H + FFN_H + j * 128:
                                  c * 2 * FFN_H + FFN_H + j * 128 + 128],
                            hn_bf[c][:], start=(c == 0), stop=(c == 7)),
                            deps=[lw1g[j], hn_toks[c]] + prev_he[jpi] + extra)
                    t_g = ("pe", p.count["pe"])
                    t_gl = p.op("act", lambda a, j=j, jpi=jpi, RG=RG,
                                lidx=lidx: a.activation(
                        gelu_sb[jpi][:], RG, AF.Gelu,
                        bias=b1_sb[:, lidx * 2 * FFN_HC + FFN_HC + j:
                              lidx * 2 * FFN_HC + FFN_HC + j + 1]),
                        deps=[t_g] + lds)
                    t_he = p.op("dve", lambda v, j=j, jpi=jpi, RA=RA,
                                lidx=lidx: v.scalar_tensor_tensor(
                        he_sb[:, j * 512:j * 512 + 512], RA,
                        b1_sb[:, lidx * 2 * FFN_HC + j:
                              lidx * 2 * FFN_HC + j + 1],
                        gelu_sb[jpi][:], ALU.add, ALU.mult),
                        deps=[t_a, t_gl] + lds)
                    prev_he[jpi] = [t_he]
                    he_toks.append(t_he)
                st_prev2 = [[], []]
                tev_prev = [[], []]
                for mch in range(8):
                    mpi = mch % 2
                    W2R = PSB[:, 1024 + mpi * 512:1536 + mpi * 512]
                    for j in range(FFN_HC):
                        p.op("pe", lambda t, mch=mch, j=j, W2R=W2R: t.matmul(
                            W2R, POOL2[:, j * D + mch * 128:j * D + mch * 128 + 128],
                            he_sb[:, j * 512:j * 512 + 512],
                            start=(j == 0), stop=(j == FFN_HC - 1)),
                            deps=[lw2c[j], he_toks[j]] + tev_prev[mpi])
                    t_y = ("pe", p.count["pe"])
                    t_ev = p.op("dve", lambda v, mch=mch, mpi=mpi, W2R=W2R,
                                lidx=lidx: v.scalar_tensor_tensor(
                        xnew[mpi][:], W2R,
                        b2_sb[:, lidx * 8 + mch:lidx * 8 + mch + 1],
                        xw[mch][:, 1:513], ALU.add, ALU.add),
                        deps=[t_y] + lds + st_prev2[mpi])
                    tev_prev[mpi] = [t_ev]
                    if final_store:
                        sot = p.dma("spq",
                                    io["f0_fm"][mch * 128:(mch + 1) * 128,
                                          ob * 512:ob * 512 + 512],
                                    xnew[mpi][:], deps=[t_ev])
                    else:
                        sot = p.dma("spq",
                                    io["SCR_y1"][mch * 128:(mch + 1) * 128,
                                           1 + ob * 512:1 + ob * 512 + 512],
                                    xnew[mpi][:], deps=[t_ev])
                    st_prev2[mpi] = [sot]
                    y_stores.append(sot)
                prev_b = p.barrier_token()
            return p.barrier_token()

        init_done = p.barrier_token()
        bnd_w = bnd_windows
        done1 = run_layer(0, io["x_fm"], TIN1, None, TIN2, 0, init_done,
                          final_store=False)
        done2 = run_layer(1, io["SCR_y1"], TIN2, None, TOUT2, TIN1 - TIN2,
                          done1, final_store=True)

        return p.barrier_token()


def build_l0ffn(TIN1, TOUT1, TOUT2):
    nc = bass.Bass("TRN2", target_bir_lowering=False, debug=False)
    TIN2 = TOUT1
    io = {
        "x_fm": nc.dram_tensor("x_fm", [D, TIN1 + 1], F32, kind="ExternalInput").ap(),
        "f_wqkv": nc.dram_tensor("f_wqkv", [2, D, 1536], F32, kind="ExternalInput").ap(),
        "f_wo": nc.dram_tensor("f_wo", [2, 512, D], F32, kind="ExternalInput").ap(),
        "f_w1": nc.dram_tensor("f_w1", [2, D, 2 * FFN_H], F32, kind="ExternalInput").ap(),
        "f_w2": nc.dram_tensor("f_w2", [2, FFN_H, D], F32, kind="ExternalInput").ap(),
        "f_qs": nc.dram_tensor("f_qs", [128, 2], F32, kind="ExternalInput").ap(),
        "f_ks": nc.dram_tensor("f_ks", [128, 2], F32, kind="ExternalInput").ap(),
        "f_lg": nc.dram_tensor("f_lg", [128, 16], F32, kind="ExternalInput").ap(),
        "f_lb": nc.dram_tensor("f_lb", [128, 16], F32, kind="ExternalInput").ap(),
        "f_b1": nc.dram_tensor("f_b1", [128, 4 * FFN_HC], F32, kind="ExternalInput").ap(),
        "f_b2": nc.dram_tensor("f_b2", [128, 16], F32, kind="ExternalInput").ap(),
        "f_pos": nc.dram_tensor("f_pos", [128, 1], F32, kind="ExternalInput").ap(),
        "f0_fm": nc.dram_tensor("f0_fm", [D, TOUT2], F32, kind="ExternalOutput").ap(),
        "SCR_a": nc.dram_tensor("a_scr", [D, TIN1 + 1], F32, kind="Internal").ap(),
        "SCR_y1": nc.dram_tensor("y1_scr", [D, TIN2 + 1], F32, kind="Internal").ap(),
    }
    es = ExitStack()
    with es:
        p = Prog(nc, es)
        idb, idb_deps = emit_identity(p, es, nc, "identb2", BF16)
        shared = {"identb": idb, "identb_deps": idb_deps}
        emit_l0ffn(p, es, nc, alloc, alloc_ps, io, shared, TIN1, TOUT1, TOUT2)
        block = es.enter_context(nc.Block())
        finish(p, block, p.barrier_token())
        p.build(block)
    return nc



def build_mix0(T, SW=1024):
    NE = 8
    nc = bass.Bass("TRN2", target_bir_lowering=False, debug=False)
    x_in = nc.dram_tensor("x_fm", [D, T], F32, kind="ExternalInput").ap()
    a0_in = nc.dram_tensor("a0_fm", [D, T], F32, kind="ExternalInput").ap()
    f0_in = nc.dram_tensor("f0_fm", [D, T], F32, kind="ExternalInput").ap()
    wr_in = nc.dram_tensor("wr_t", [D, NE], F32, kind="ExternalInput").ap()
    x1_out = nc.dram_tensor("x1_fm", [D, T], F32, kind="ExternalOutput").ap()
    xn1_out = nc.dram_tensor("xn1_bf", [D, T], BF16, kind="ExternalOutput").ap()
    wts_out = nc.dram_tensor("wts_tm", [T, NE], F32, kind="ExternalOutput").ap()
    es = ExitStack()
    with es:
        p = Prog(nc, es)
        eng = p.engines_token
        x_t = [alloc(es, nc, f"x{c}", [128, SW], F32) for c in range(8)]
        a_t = [alloc(es, nc, f"a{c}", [128, SW], F32) for c in range(8)]
        f_t = [alloc(es, nc, f"f{c}", [128, SW], F32) for c in range(8)]
        xn_t = [alloc(es, nc, f"xn{c}", [128, SW], F32) for c in range(8)]
        xnb_t = [alloc(es, nc, f"xnb{c}", [128, SW], BF16) for c in range(8)]
        wr_sb = alloc(es, nc, "wr_sb", [128, 8 * NE], F32)
        sc_ps = alloc_ps(es, nc, "sc_ps", [128, 2 * NE], F32)
        probs = alloc(es, nc, "probs", [128, NE], F32)
        p2 = alloc(es, nc, "p2", [128, NE], F32)
        wts_sb = alloc(es, nc, "wts_sb", [128, NE], F32)
        m_c = alloc(es, nc, "mx_c", [128, 1], F32)
        nm_c = alloc(es, nc, "nm_c", [128, 1], F32)
        l_cc = alloc(es, nc, "l_cc", [128, 1], F32)
        li_c = alloc(es, nc, "li_c", [128, 1], F32)
        m2_c = alloc(es, nc, "m2_c", [128, 1], F32)
        consts = make_const_tiles(p, es, nc)
        lwr = p.dma("spq", wr_sb[:].rearrange("p (c e) -> p c e", c=8),
                    wr_in.rearrange("(c p) e -> p c e", p=128))
        st = []
        prev = []
        for w0 in range(0, T, SW):
            ldx = [p.dma("spq", x_t[c][:], x_in[c * 128:(c + 1) * 128,
                                                w0:w0 + SW], deps=prev)
                   for c in range(8)]
            lda = [p.dma("spq", a_t[c][:], a0_in[c * 128:(c + 1) * 128,
                                                 w0:w0 + SW], deps=prev)
                   for c in range(8)]
            ldf = [p.dma("actq", f_t[c][:], f0_in[c * 128:(c + 1) * 128,
                                                  w0:w0 + SW], deps=prev)
                   for c in range(8)]
            s_toks = []
            for c in range(8):
                t1 = p.op("dve", lambda v, c=c: v.tensor_add(
                    x_t[c][:], x_t[c][:], a_t[c][:]),
                    deps=[ldx[c], lda[c]] + prev)
                t2 = p.op("dve", lambda v, c=c: v.tensor_add(
                    x_t[c][:], x_t[c][:], f_t[c][:]), deps=[t1, ldf[c]])
                s_toks.append(t2)
                st.append(p.dma("spq", x1_out[c * 128:(c + 1) * 128,
                                               w0:w0 + SW], x_t[c][:],
                                deps=[t2]))
            for s0, toks in emit_ln_fm(p, es, nc, consts, x_t, s_toks,
                                       xnb_t, xn_t, SW):
                for c in range(8):
                    sl = slice(s0, s0 + 512)
                    gl = slice(w0 + s0, w0 + s0 + 512)
                    st.append(p.dma("spq", xn1_out[c * 128:(c + 1) * 128, gl],
                                    xnb_t[c][:, sl], deps=toks))
                for sub in range(4):
                    t0c = s0 + sub * 128
                    pp = sc_ps[:, (sub % 2) * NE:(sub % 2) * NE + NE]
                    for c in range(8):
                        p.op("pe", lambda t, c=c, t0c=t0c, pp=pp: t.matmul(
                            pp, xn_t[c][:, t0c:t0c + 128],
                            wr_sb[:, c * NE:(c + 1) * NE],
                            start=(c == 0), stop=(c == 7)),
                            deps=toks + [lwr] + eng(("dve", "act")))
                    tsc = ("pe", p.count["pe"])
                    r1 = p.op("dve", lambda v, pp=pp: v.reduce_max(
                        m_c[:], pp, axis=AX.X), deps=[tsc] + eng(("act",)))
                    r2 = p.op("dve", lambda v: v.tensor_scalar_mul(
                        nm_c[:], m_c[:], -1.0), deps=[r1])
                    r3 = p.op("act", lambda a, pp=pp: a.activation(
                        probs[:], pp, AF.Exp, bias=nm_c[:, 0:1],
                        accum_out=l_cc[:, 0:1]), deps=[r2] + eng(("dve",)))
                    r4 = p.op("dve", lambda v: v.reciprocal(
                        li_c[:], l_cc[:]), deps=[r3])
                    r5 = p.op("dve", lambda v: v.tensor_scalar_mul(
                        probs[:], probs[:], li_c[:, 0:1]), deps=[r4, r3])
                    r6 = p.op("dve", lambda v: v.reduce_max(
                        m_c[:], probs[:], axis=AX.X), deps=[r5])
                    r7 = p.op("dve", lambda v: v.tensor_scalar(
                        p2[:], probs[:], m_c[:, 0:1], None, ALU.is_lt),
                        deps=[r6])
                    r8 = p.op("dve", lambda v: v.tensor_mul(
                        p2[:], probs[:], p2[:]), deps=[r7])
                    r9 = p.op("dve", lambda v: v.reduce_max(
                        m2_c[:], p2[:], axis=AX.X), deps=[r8])
                    r10 = p.op("dve", lambda v: v.tensor_scalar(
                        wts_sb[:], probs[:], m2_c[:, 0:1], None, ALU.is_ge),
                        deps=[r9])
                    r11 = p.op("dve", lambda v: v.tensor_mul(
                        wts_sb[:], wts_sb[:], probs[:]), deps=[r10])
                    st.append(p.dma("spq", wts_out[w0 + t0c:w0 + t0c + 128, :],
                                    wts_sb[:], deps=[r11]))
            prev = p.barrier_token()
        block = es.enter_context(nc.Block())
        finish(p, block, p.barrier_token())
        p.build(block)
    return nc



S1 = 5120
HALO1 = 3072


def _ct(a):
    return np.ascontiguousarray(a)


def kernel(**inputs):
    import concourse.bass_utils as bu
    f32 = np.float32
    x = np.asarray(inputs["L0_attn_x"], dtype=f32)

    def tile128(vec):
        return _ct(np.tile(np.asarray(vec, f32)[None, :], (128, 1)))

    xT = [_ct(x[b].T) for b in range(B)]

    nc = build_ln0(TOK)
    maps = [{"x_fm": _ct(xT[b][:, h * TOK:(h + 1) * TOK])}
            for b in range(B) for h in range(2)]
    res = run_bass_kernel_spmd(nc, maps, core_ids=list(range(8)))
    xn_f = [np.concatenate([res.results[2 * b]["xn_f32"],
                            res.results[2 * b + 1]["xn_f32"]], 1)
            for b in range(B)]
    xn_b = [np.concatenate([res.results[2 * b]["xn_bf"],
                            res.results[2 * b + 1]["xn_bf"]], 1)
            for b in range(B)]

    wq_t = _ct(np.asarray(inputs["L0_attn_Wq"], f32).T)
    wk_t = _ct(np.asarray(inputs["L0_attn_Wk"], f32).T)
    wv_t = _ct(np.asarray(inputs["L0_attn_Wv"], f32).T)
    wo_t = _ct(np.asarray(inputs["L0_attn_Wo"], f32).T)
    gam16 = tile128(inputs["L0_attn_gamma"])
    bet16 = tile128(inputs["L0_attn_beta"])
    wcom = dict(wq_t=wq_t, wk_t=wk_t, wv_t=wv_t, wo_t=wo_t,
                gam16=gam16, bet16=bet16)
    nc0 = build_l0attn(0, TOK)
    maps0 = [dict(xn_bf=_ct(xn_b[b][:, 0:TOK]), **wcom) for b in range(B)]
    r0 = run_bass_kernel_spmd(nc0, maps0, core_ids=[0, 2, 4, 6])
    nc1 = build_l0attn(TOK, TOK)
    maps1 = [dict(xn_bf=_ct(xn_b[b]), **wcom) for b in range(B)]
    r1 = run_bass_kernel_spmd(nc1, maps1, core_ids=[1, 3, 5, 7])
    a0 = [np.concatenate([r0.results[b]["a0_fm"], r1.results[b]["a0_fm"]], 1)
          for b in range(B)]

    wqkv = np.asarray(inputs["L0_ffn_wqkv"], f32)
    wo2 = np.asarray(inputs["L0_ffn_wo"], f32)
    w1 = np.asarray(inputs["L0_ffn_w1"], f32)
    b1 = np.asarray(inputs["L0_ffn_b1"], f32)
    w2 = np.asarray(inputs["L0_ffn_w2"], f32)
    b2 = np.asarray(inputs["L0_ffn_b2"], f32)
    qsc = np.asarray(inputs["L0_ffn_q_scale"], f32)
    ksc = np.asarray(inputs["L0_ffn_k_scale"], f32)
    lg = np.asarray(inputs["L0_ffn_ln_g"], f32)
    lb = np.asarray(inputs["L0_ffn_ln_b"], f32)
    HID = 2730
    w1p = np.zeros((2, D, 2 * FFN_H), f32)
    w1p[:, :, :HID] = w1[:, :, :HID]
    w1p[:, :, FFN_H:FFN_H + HID] = w1[:, :, HID:]
    b1p = np.zeros((2, 2 * FFN_H), f32)
    b1p[:, :HID] = b1[:, :HID]
    b1p[:, FFN_H:FFN_H + HID] = b1[:, HID:]
    w2p = np.zeros((2, FFN_H, D), f32)
    w2p[:, :HID, :] = w2
    fcom = dict(
        f_wqkv=wqkv, f_wo=wo2, f_w1=w1p, f_w2=w2p,
        f_qs=_ct(np.tile(qsc.T, (2, 1))), f_ks=_ct(np.tile(ksc.T, (2, 1))),
        f_lg=_ct(lg.reshape(2, 8, 128).transpose(2, 0, 1).reshape(128, 16)),
        f_lb=_ct(lb.reshape(2, 8, 128).transpose(2, 0, 1).reshape(128, 16)),
        f_b1=_ct(b1p.reshape(2, 2 * FFN_HC, 128).transpose(2, 0, 1)
                 .reshape(128, -1)),
        f_b2=_ct(b2.reshape(2, 8, 128).transpose(2, 0, 1).reshape(128, 16)),
    )
    nc0 = build_l0ffn(TOK, TOK, TOK)
    maps0 = []
    for b in range(B):
        xfm = np.concatenate([np.zeros((D, 1), f32), xn_f[b][:, 0:TOK]], 1)
        maps0.append(dict(x_fm=_ct(xfm),
                          f_pos=np.full((128, 1), 0.0, f32), **fcom))
    r0 = run_bass_kernel_spmd(nc0, maps0, core_ids=[0, 2, 4, 6])
    s_in = N - (TOK + HALO1)
    nc1 = build_l0ffn(TOK + HALO1, TOK + HALO1 // 2, TOK)
    maps1 = []
    for b in range(B):
        xfm = np.concatenate([xn_f[b][:, s_in - 1:s_in],
                              xn_f[b][:, s_in:N]], 1)
        maps1.append(dict(x_fm=_ct(xfm),
                          f_pos=np.full((128, 1), float(s_in), f32), **fcom))
    r1 = run_bass_kernel_spmd(nc1, maps1, core_ids=[1, 3, 5, 7])
    f0 = [np.concatenate([r0.results[b]["f0_fm"], r1.results[b]["f0_fm"]], 1)
          for b in range(B)]

    wr_t = _ct(np.asarray(inputs["L1_ffn_w_router"], f32).T)
    nc = build_mix0(TOK)
    maps = []
    for b in range(B):
        for h in range(2):
            sl = slice(h * TOK, (h + 1) * TOK)
            maps.append(dict(x_fm=_ct(xT[b][:, sl]), a0_fm=_ct(a0[b][:, sl]),
                             f0_fm=_ct(f0[b][:, sl]), wr_t=wr_t))
    res = run_bass_kernel_spmd(nc, maps, core_ids=list(range(8)))
    x1 = [np.concatenate([res.results[2 * b]["x1_fm"],
                          res.results[2 * b + 1]["x1_fm"]], 1)
          for b in range(B)]
    xn1 = [np.concatenate([res.results[2 * b]["xn1_bf"],
                           res.results[2 * b + 1]["xn1_bf"]], 1)
           for b in range(B)]
    wts = [np.concatenate([res.results[2 * b]["wts_tm"],
                           res.results[2 * b + 1]["wts_tm"]], 0)
           for b in range(B)]

    l1w = dict(l1_wq=np.asarray(inputs["L1_attn_W_Q"], f32),
               l1_wk=np.asarray(inputs["L1_attn_W_K"], f32),
               l1_wv=np.asarray(inputs["L1_attn_W_V"], f32))
    nc0 = build_l1attn(0, S1)
    maps0 = [dict(xn1_bf=_ct(xn1[b][:, 0:S1]), **l1w) for b in range(B)]
    r0 = run_bass_kernel_spmd(nc0, maps0, core_ids=[0, 2, 4, 6])
    nc1 = build_l1attn(S1, N - S1)
    maps1 = [dict(xn1_bf=_ct(xn1[b]), **l1w) for b in range(B)]
    r1 = run_bass_kernel_spmd(nc1, maps1, core_ids=[1, 3, 5, 7])
    a1 = [np.concatenate([r0.results[b]["a1_tm"], r1.results[b]["a1_tm"]], 0)
          for b in range(B)]

    mw = dict(moe_w1=np.asarray(inputs["L1_ffn_w1"], f32),
              moe_w3=np.asarray(inputs["L1_ffn_w3"], f32),
              moe_w2=np.asarray(inputs["L1_ffn_w2"], f32))
    nc0 = build_moe(S1, TG=1024)
    maps0 = [dict(xn1_bf=_ct(xn1[b][:, 0:S1]), x1_fm=_ct(x1[b][:, 0:S1]),
                  a1_tm=_ct(a1[b][0:S1]), wts_tm=_ct(wts[b][0:S1]), **mw)
             for b in range(B)]
    r0 = run_bass_kernel_spmd(nc0, maps0, core_ids=[0, 2, 4, 6])
    nc1 = build_moe(N - S1, TG=1024)
    maps1 = [dict(xn1_bf=_ct(xn1[b][:, S1:]), x1_fm=_ct(x1[b][:, S1:]),
                  a1_tm=_ct(a1[b][S1:]), wts_tm=_ct(wts[b][S1:]), **mw)
             for b in range(B)]
    r1 = run_bass_kernel_spmd(nc1, maps1, core_ids=[1, 3, 5, 7])

    out = np.empty((B, N, D), f32)
    for b in range(B):
        out[b, 0:S1] = r0.results[b]["out_fm"].T
        out[b, S1:] = r1.results[b]["out_fm"].T
    return out



class Arena:

    def __init__(self, es, nc, sbuf_cols_bf16, name="ARENA"):
        self.nc = nc
        self.pool = alloc(es, nc, name, [128, sbuf_cols_bf16], BF16)
        self.pool_f32 = self.pool[:].bitcast(F32)
        self.cols = sbuf_cols_bf16
        self.off = 0
        self.ps = [alloc_ps(es, nc, f"{name}_psA", [128, 2048], F32),
                   alloc_ps(es, nc, f"{name}_psB", [128, 2048], F32)]
        self.ps_off = 0

    def reset(self):
        self.off = 0
        self.ps_off = 0

    def __call__(self, es_unused, nc_unused, name, shape, dtype):
        P, Nfree = shape
        units = Nfree if dtype == BF16 else 2 * Nfree
        units = (units + 7) & ~7
        assert self.off + units <= self.cols, f"arena OOM at {name}"
        if dtype == BF16:
            t = self.pool[0:P, self.off:self.off + units][:, 0:Nfree]
        else:
            t = self.pool_f32[0:P, self.off // 2:(self.off + units) // 2][:, 0:Nfree]
        self.off += units
        return _APWrap(t)

    def psum(self, es_unused, nc_unused, name, shape, dtype=F32):
        P, Nfree = shape
        nb = ((Nfree if dtype == F32 else Nfree // 2) + 511) // 512
        assert nb <= 4, f"psum request too big: {name}"
        which = self.ps_off // 4
        rem = 4 - (self.ps_off % 4)
        if nb > rem:
            self.ps_off += rem
            which = self.ps_off // 4
        assert which < 2, f"psum arena OOM at {name}"
        b0 = self.ps_off % 4
        self.ps_off += nb
        t = self.ps[which]
        if dtype == F32:
            return _APWrap(t[0:P, b0 * 512:b0 * 512 + Nfree])
        return _APWrap(t[:].bitcast(BF16)[0:P, b0 * 1024:b0 * 1024 + Nfree])


class _APWrap:

    def __init__(self, ap):
        self._ap = ap

    def __getitem__(self, key):
        return self._ap[key]

    def ap(self):
        return self._ap



def emit_lnphase(p, es, nc, A, APS, io, consts, T, SW=1024, prev=()):
    x_in, xn_out, xnb_out = io["x_src"], io["xn_f32"], io["xn_bf"]
    x_t = [A(es, nc, f"lnx{c}", [128, SW], F32) for c in range(8)]
    xn_t = [A(es, nc, f"lnxn{c}", [128, SW], F32) for c in range(8)]
    xnb_t = [A(es, nc, f"lnxnb{c}", [128, SW], BF16) for c in range(8)]
    st = []
    prev = list(prev)
    for w0 in range(0, T, SW):
        ld = [p.dma("spq", x_t[c][:], x_in[c * 128:(c + 1) * 128, w0:w0 + SW],
                    deps=prev) for c in range(8)]
        for s0, toks in emit_ln_fm(p, es, nc, consts, x_t, ld, xnb_t, xn_t, SW):
            for c in range(8):
                sl = slice(s0, s0 + 512)
                gl = slice(w0 + s0, w0 + s0 + 512)
                st.append(p.dma("spq", xn_out[c * 128:(c + 1) * 128, gl],
                                xn_t[c][:, sl], deps=toks))
                st.append(p.dma("spq", xnb_out[c * 128:(c + 1) * 128, gl],
                                xnb_t[c][:, sl], deps=toks))
        prev = p.barrier_token()
    return p.barrier_token()


def emit_mixphase(p, es, nc, A, APS, io, consts, T, SW=1024, prev=(), NE=8):
    x_in, a0_in, f0_in = io["x_own"], io["a0_fm"], io["f0_fm"]
    x1_out, xn1_out, wts_out = io["x1_fm"], io["xn1_bf"], io["wts_tm"]
    wr_in = io["wr_t"]
    eng = p.engines_token
    x_t = [A(es, nc, f"mx{c}", [128, SW], F32) for c in range(8)]
    a_t = [A(es, nc, f"ma{c}", [128, SW], F32) for c in range(8)]
    f_t = [A(es, nc, f"mf{c}", [128, SW], F32) for c in range(8)]
    xn_t = [A(es, nc, f"mxn{c}", [128, SW], F32) for c in range(8)]
    xnb_t = [A(es, nc, f"mxnb{c}", [128, SW], BF16) for c in range(8)]
    wr_sb = A(es, nc, "m_wr", [128, 8 * NE], F32)
    sc_ps = APS(es, nc, "m_scps", [128, 2 * NE], F32)
    probs = A(es, nc, "m_probs", [128, NE], F32)
    p2 = A(es, nc, "m_p2", [128, NE], F32)
    wts_sb = A(es, nc, "m_wts", [128, NE], F32)
    m_c = A(es, nc, "m_mx", [128, 1], F32)
    nm_c = A(es, nc, "m_nm", [128, 1], F32)
    l_cc = A(es, nc, "m_l", [128, 1], F32)
    li_c = A(es, nc, "m_li", [128, 1], F32)
    m2_c = A(es, nc, "m_m2", [128, 1], F32)
    lwr = p.dma("spq", wr_sb[:].rearrange("p (c e) -> p c e", c=8),
                wr_in.rearrange("(c p) e -> p c e", p=128), deps=list(prev))
    st = []
    prev = list(prev)
    for w0 in range(0, T, SW):
        ldx = [p.dma("spq", x_t[c][:], x_in[c * 128:(c + 1) * 128, w0:w0 + SW],
                     deps=prev) for c in range(8)]
        lda = [p.dma("spq", a_t[c][:], a0_in[c * 128:(c + 1) * 128, w0:w0 + SW],
                     deps=prev) for c in range(8)]
        ldf = [p.dma("actq", f_t[c][:], f0_in[c * 128:(c + 1) * 128, w0:w0 + SW],
                     deps=prev) for c in range(8)]
        s_toks = []
        for c in range(8):
            t1 = p.op("dve", lambda v, c=c: v.tensor_add(
                x_t[c][:], x_t[c][:], a_t[c][:]),
                deps=[ldx[c], lda[c]] + prev)
            t2 = p.op("dve", lambda v, c=c: v.tensor_add(
                x_t[c][:], x_t[c][:], f_t[c][:]), deps=[t1, ldf[c]])
            s_toks.append(t2)
            st.append(p.dma("spq", x1_out[c * 128:(c + 1) * 128, w0:w0 + SW],
                            x_t[c][:], deps=[t2]))
        for s0, toks in emit_ln_fm(p, es, nc, consts, x_t, s_toks,
                                   xnb_t, xn_t, SW, tag="mln"):
            for c in range(8):
                sl = slice(s0, s0 + 512)
                gl = slice(w0 + s0, w0 + s0 + 512)
                st.append(p.dma("spq", xn1_out[c * 128:(c + 1) * 128, gl],
                                xnb_t[c][:, sl], deps=toks))
            for sub in range(4):
                t0c = s0 + sub * 128
                pp = sc_ps[:, (sub % 2) * NE:(sub % 2) * NE + NE]
                for c in range(8):
                    p.op("pe", lambda t, c=c, t0c=t0c, pp=pp: t.matmul(
                        pp, xn_t[c][:, t0c:t0c + 128],
                        wr_sb[:, c * NE:(c + 1) * NE],
                        start=(c == 0), stop=(c == 7)),
                        deps=toks + [lwr] + eng(("dve", "act")))
                tsc = ("pe", p.count["pe"])
                r1 = p.op("dve", lambda v, pp=pp: v.reduce_max(
                    m_c[:], pp, axis=AX.X), deps=[tsc] + eng(("act",)))
                r2 = p.op("dve", lambda v: v.tensor_scalar_mul(
                    nm_c[:], m_c[:], -1.0), deps=[r1])
                r3 = p.op("act", lambda a, pp=pp: a.activation(
                    probs[:], pp, AF.Exp, bias=nm_c[:, 0:1],
                    accum_out=l_cc[:, 0:1]), deps=[r2] + eng(("dve",)))
                r4 = p.op("dve", lambda v: v.reciprocal(li_c[:], l_cc[:]),
                          deps=[r3])
                r5 = p.op("dve", lambda v: v.tensor_scalar_mul(
                    probs[:], probs[:], li_c[:, 0:1]), deps=[r4, r3])
                r6 = p.op("dve", lambda v: v.reduce_max(
                    m_c[:], probs[:], axis=AX.X), deps=[r5])
                r7 = p.op("dve", lambda v: v.tensor_scalar(
                    p2[:], probs[:], m_c[:, 0:1], None, ALU.is_lt), deps=[r6])
                r8 = p.op("dve", lambda v: v.tensor_mul(
                    p2[:], probs[:], p2[:]), deps=[r7])
                r9 = p.op("dve", lambda v: v.reduce_max(
                    m2_c[:], p2[:], axis=AX.X), deps=[r8])
                r10 = p.op("dve", lambda v: v.tensor_scalar(
                    wts_sb[:], probs[:], m2_c[:, 0:1], None, ALU.is_ge),
                    deps=[r9])
                r11 = p.op("dve", lambda v: v.tensor_mul(
                    wts_sb[:], wts_sb[:], probs[:]), deps=[r10])
                st.append(p.dma("spq", wts_out[w0 + t0c:w0 + t0c + 128, :],
                                wts_sb[:], deps=[r11]))
        prev = p.barrier_token()
    return p.barrier_token()


def emit_l1attn_sym(p, es, nc, A, APS, io, shared, prev=(), GQB=512, KT=512):
    xg = io["xg_bf"]
    xq = io["xn1own_bf"]
    wq_in, wk_in, wv_in = io["l1_wq"], io["l1_wk"], io["l1_wv"]
    kfm, vtm, qfm = io["S_kfm"], io["S_vtm"], io["S_qfm"]
    a1_out = io["a1_tm"]
    eng = p.engines_token
    NB = 8
    TOWN = NB * GQB

    wq_sb = A(es, nc, "awq", [128, 8 * D], BF16)
    wk_sb = A(es, nc, "awk", [128, 8 * D], BF16)
    wv_sb = A(es, nc, "awv", [128, 8 * D], BF16)
    xn_t = [A(es, nc, f"axn{c}", [128, 512], BF16) for c in range(8)]
    ev_k = A(es, nc, "aevk", [128, 512], BF16)
    ev_v = A(es, nc, "aevv", [128, D], BF16)
    P512a = APS(es, nc, "aP512a", [128, 512], F32)
    P512b = APS(es, nc, "aP512b", [128, 512], F32)
    P1024a = APS(es, nc, "aP1024a", [128, D], F32)
    P1024b = APS(es, nc, "aP1024b", [128, D], F32)
    PTbf = APS(es, nc, "aPTbf", [128, 1024], BF16)
    k_ps, v_ps = P512a, P1024a
    lwq = p.dma("poolq", wq_sb[:].rearrange("p (c d) -> p c d", c=8),
                wq_in.rearrange("(c p) d -> p c d", p=128), deps=list(prev))
    lwk = p.dma("poolq", wk_sb[:].rearrange("p (c d) -> p c d", c=8),
                wk_in.rearrange("(c p) d -> p c d", p=128), deps=list(prev))
    lwv = p.dma("poolq", wv_sb[:].rearrange("p (c d) -> p c d", c=8),
                wv_in.rearrange("(c p) d -> p c d", p=128), deps=list(prev))
    prev_it = list(prev)
    st_k = st_q = st_v = None
    te_k = te_q = te_v = None
    for half in range(2):
        for t0 in range(0, TOWN, 512):
            gcol = half * TOWN + t0
            ld = [p.dma("spq", xn_t[c][:],
                        xg[half * 1024 + c * 128:half * 1024 + (c + 1) * 128,
                           t0:t0 + 512], deps=prev_it) for c in range(8)]
            for m in range(8):
                for c in range(8):
                    p.op("pe", lambda t, m=m, c=c: t.matmul(
                        k_ps[:], wk_sb[:, c * D + m * 128:c * D + m * 128 + 128],
                        xn_t[c][:], start=(c == 0), stop=(c == 7)),
                        deps=[lwk] + ld + prev_it + ([te_k] if te_k else []))
                tk = ("pe", p.count["pe"])
                te_k = p.op("act", lambda a: a.copy(ev_k[:], k_ps[:]),
                            deps=[tk] + ([st_k] if st_k else []))
                st_k = p.dma("spq", kfm[m * 128:(m + 1) * 128, gcol:gcol + 512],
                             ev_k[:], deps=[te_k])
            for r in range(4):
                for dh in range(2):
                    for c in range(8):
                        p.op("pe", lambda t, r=r, dh=dh, c=c: t.matmul(
                            v_ps[:, dh * 512:dh * 512 + 512],
                            xn_t[c][:, r * 128:r * 128 + 128],
                            wv_sb[:, c * D + dh * 512:c * D + dh * 512 + 512],
                            start=(c == 0), stop=(c == 7)),
                            deps=[lwv] + ld + prev_it + ([te_v] if te_v else []))
                tv = ("pe", p.count["pe"])
                te_v = p.op("act", lambda a: a.copy(ev_v[:], v_ps[:]),
                            deps=[tv] + ([st_v] if st_v else []))
                st_v = p.dma("spq",
                             vtm[gcol + r * 128:gcol + r * 128 + 128, :],
                             ev_v[:], deps=[te_v])
            prev_it = p.engines_token()
    for t0 in range(0, TOWN, 512):
        ld = [p.dma("spq", xn_t[c][:],
                    xq[c * 128:(c + 1) * 128, t0:t0 + 512], deps=prev_it)
              for c in range(8)]
        for m in range(8):
            for c in range(8):
                p.op("pe", lambda t, m=m, c=c: t.matmul(
                    k_ps[:], wq_sb[:, c * D + m * 128:c * D + m * 128 + 128],
                    xn_t[c][:], start=(c == 0), stop=(c == 7)),
                    deps=[lwq] + ld + prev_it + ([te_k] if te_k else []))
            tq = ("pe", p.count["pe"])
            te_k = p.op("act", lambda a: a.copy(ev_k[:], k_ps[:]),
                        deps=[tq] + ([st_q] if st_q else []))
            st_q = p.dma("spq", qfm[m * 128:(m + 1) * 128, t0:t0 + 512],
                         ev_k[:], deps=[te_k])
        prev_it = p.engines_token()
    phaseA_done = p.barrier_token()

    qg = [A(es, nc, f"aqg{c}", [128, GQB], BF16) for c in range(8)]
    kv_k = [A(es, nc, f"akvk{i}", [128, 8 * KT], BF16) for i in range(2)]
    kv_v = [A(es, nc, f"akvv{i}", [128, 4 * D], BF16) for i in range(2)]
    o_sb = [A(es, nc, f"aosb{i}", [128, D], F32) for i in range(4)]
    m_sb = A(es, nc, "amsb", [128, 4], F32)
    mn_sb = A(es, nc, "amnsb", [128, 4], F32)
    rm_sb = A(es, nc, "armsb", [128, 4], F32)
    alq_sb = A(es, nc, "aalq", [128, 4], F32)
    bia_sb = A(es, nc, "abia", [128, 4], F32)
    lp_sb = A(es, nc, "alpsb", [128, 4], F32)
    l_sb = A(es, nc, "alsb", [128, 4], F32)
    li_sb = A(es, nc, "alisb", [128, 4], F32)
    p2_sb = [A(es, nc, f"apsb{i}", [128, KT], BF16) for i in range(2)]
    pt2_sb = [A(es, nc, f"aptsb{i}", [128, KT], BF16) for i in range(2)]
    a1_sb = A(es, nc, "aa1sb", [128, D], F32)
    bm_sb = A(es, nc, "abm", [128, 100], F32)
    al_sb = A(es, nc, "aal", [128, 16], F32)
    s_ps = [P512a, P512b]
    pv_ps = [P1024a, P1024b]
    pt_ps = PTbf
    identb, identb_deps = shared["identb"], shared["identb_deps"]
    lbm = p.dma("spq", bm_sb[:], io["msel_bm"], deps=list(prev))
    lal = p.dma("spq", al_sb[:], io["msel_al"], deps=list(prev))
    masks, mtoks = [], []
    for oi in range(4):
        mt = A(es, nc, f"amask{oi}", [128, KT], F32)
        t0_ = p.op("dve", lambda v, mt=mt: v.memset(mt[:], 0.0),
                   deps=list(prev))
        t1_ = p.op("pool", lambda g, mt=mt, oi=oi: g.affine_select(
            out=mt[:], in_=mt[:], pattern=[[-1, KT]],
            compare_op=ALU.is_ge, fill=-1e30, base=oi * 128,
            channel_multiplier=1), deps=[t0_])
        masks.append(mt)
        mtoks.append(t1_)
    c32_t, c32_tok = p.constf(1.0 / 32.0)
    bmbase = [0]
    for i in range(NB):
        bmbase.append(bmbase[-1] + 9 + i)

    prev_grp = phaseA_done
    for blk in range(NB):
        nkt = 9 + blk
        NQT = GQB // 128
        lq = [p.dma("spq", qg[c][:],
                    qfm[c * 128:(c + 1) * 128, blk * GQB:blk * GQB + GQB],
                    deps=prev_grp) for c in range(8)]
        o_tok = {}
        a1_prev = []
        pvdep = [[], []]
        kbuf_rd = [list(prev_grp), list(prev_grp)]
        sp_free = [list(prev_grp), list(prev_grp)]
        p_free = [[], []]
        pt_free = [[], []]
        lp_free = [[] for _ in range(NQT)]
        for kt in range(nkt):
            kb = kt % 2
            lk = p.dma("spq", kv_k[kb][:].rearrange("p (c t) -> p c t", c=8),
                       kfm[:, kt * KT:(kt + 1) * KT].rearrange(
                           "(c p) t -> p c t", p=128),
                       deps=kbuf_rd[kb])
            lv = p.dma("actq", kv_v[kb][:].rearrange(
                "p (r d) -> p r d", r=4),
                vtm[kt * KT:(kt + 1) * KT, :].rearrange(
                    "(r p) d -> p r d", p=128),
                deps=kbuf_rd[kb])
            kbuf_rd[kb] = []
            bmc = bm_sb[:, bmbase[blk] + kt:bmbase[blk] + kt + 1]
            for qt in range(NQT):
                pp = (kt * NQT + qt) % 2
                sp = s_ps[pp]
                for c in range(8):
                    p.op("pe", lambda t, c=c, qt=qt, sp=sp, kb=kb: t.matmul(
                        sp[:], qg[c][:, qt * 128:qt * 128 + 128],
                        kv_k[kb][:, c * KT:c * KT + KT],
                        start=(c == 0), stop=(c == 7)),
                        deps=[lk] + lq + sp_free[pp])
                ts = ("pe", p.count["pe"])
                kbuf_rd[kb].append(ts)
                dcol = None
                if kt == blk:
                    dcol = 2 * blk
                elif kt == 8 + blk:
                    dcol = 2 * blk + 1
                dep0 = [ts]
                if dcol is not None:
                    tm_ = p.op("dve", lambda v, sp=sp, qt=qt, dcol=dcol:
                               v.scalar_tensor_tensor(
                        sp[:], masks[qt][:], al_sb[:, dcol:dcol + 1],
                        sp[:], ALU.mult, ALU.add),
                        deps=[ts, lal] + mtoks)
                    dep0 = [tm_]
                qc = slice(qt, qt + 1)
                t_rm = p.op("dve", lambda v, sp=sp, qc=qc: v.reduce_max(
                    rm_sb[:, qc], sp[:], axis=AX.X), deps=dep0)
                t_rp = p.op("dve", lambda v, qc=qc, bmc=bmc: v.tensor_scalar(
                    rm_sb[:, qc], rm_sb[:, qc], bmc, None, ALU.add),
                    deps=[t_rm, lbm])
                if kt == 0:
                    t_mn = p.op("dve", lambda v, qc=qc: v.tensor_copy(
                        mn_sb[:, qc], rm_sb[:, qc]), deps=[t_rp])
                    t_mcp = p.op("dve", lambda v, qc=qc: v.tensor_copy(
                        m_sb[:, qc], mn_sb[:, qc]), deps=[t_mn])
                    t_ae = None
                else:
                    t_mn = p.op("dve", lambda v, qc=qc: v.tensor_tensor(
                        mn_sb[:, qc], m_sb[:, qc], rm_sb[:, qc], ALU.max),
                        deps=[t_rp])
                    t_as = p.op("dve", lambda v, qc=qc: v.tensor_sub(
                        alq_sb[:, qc], m_sb[:, qc], mn_sb[:, qc]),
                        deps=[t_mn])
                    t_ae = p.op("act", lambda a, qc=qc: a.activation(
                        alq_sb[:, qc], alq_sb[:, qc], AF.Exp,
                        scale=c32_t[:, 0:1]), deps=[t_as, c32_tok])
                    t_mcp = p.op("dve", lambda v, qc=qc: v.tensor_copy(
                        m_sb[:, qc], mn_sb[:, qc]), deps=[t_as])
                t_b1 = p.op("dve", lambda v, qc=qc, bmc=bmc:
                            v.scalar_tensor_tensor(
                    bia_sb[:, qc], mn_sb[:, qc], -1.0, bmc,
                    ALU.mult, ALU.add), deps=[t_mn, lbm])
                t_b2 = p.op("dve", lambda v, qc=qc: v.tensor_scalar_mul(
                    bia_sb[:, qc], bia_sb[:, qc], 1.0 / 32.0), deps=[t_b1])
                texp = p.op("act", lambda a, sp=sp, qc=qc, pp=pp: a.activation(
                    p2_sb[pp][:], sp[:], AF.Exp,
                    bias=bia_sb[:, qc],
                    scale=c32_t[:, 0:1],
                    accum_out=lp_sb[:, qc]),
                    deps=[t_b2, c32_tok] + p_free[pp] + lp_free[qt] +
                    ([t_ae] if t_ae else []))
                sp_free[pp] = [texp]
                if kt == 0:
                    t_lu = p.op("dve", lambda v, qc=qc: v.tensor_copy(
                        l_sb[:, qc], lp_sb[:, qc]), deps=[texp])
                else:
                    t_lu = p.op("dve", lambda v, qc=qc: v.scalar_tensor_tensor(
                        l_sb[:, qc], l_sb[:, qc], alq_sb[:, qc],
                        lp_sb[:, qc], ALU.mult, ALU.add),
                        deps=[texp] + ([t_ae] if t_ae else []))
                lp_free[qt] = [t_lu]
                trt = [p.op("pe", lambda t, j=j, pp=pp: t.transpose(
                    pt_ps[:, pp * 512 + j * 128:pp * 512 + j * 128 + 128],
                    p2_sb[pp][:, j * 128:j * 128 + 128], identb[:]),
                    deps=[texp] + identb_deps + pt_free[pp])
                    for j in range(4)]
                tev = p.op("act", lambda a, pp=pp: a.copy(
                    pt2_sb[pp][:], pt_ps[:, pp * 512:pp * 512 + 512]),
                    deps=trt)
                pt_free[pp] = [tev]
                p_free[pp] = trt
                ov = pv_ps[qt % 2]
                for dh in range(2):
                    for r in range(4):
                        p.op("pe", lambda t, r=r, dh=dh, ov=ov, kb=kb, pp=pp: t.matmul(
                            ov[:, dh * 512:dh * 512 + 512],
                            pt2_sb[pp][:, r * 128:r * 128 + 128],
                            kv_v[kb][:, r * D + dh * 512:r * D + dh * 512 + 512],
                            start=(r == 0), stop=(r == 3)),
                            deps=[tev, lv] + pvdep[qt % 2])
                tpv = ("pe", p.count["pe"])
                kbuf_rd[kb].append(tpv)
                if kt == 0:
                    oa = p.op("dve", lambda v, qt=qt, ov=ov: v.tensor_copy(
                        o_sb[qt][:], ov[:]), deps=[tpv])
                else:
                    oa = p.op("dve", lambda v, qt=qt, ov=ov, qc=qc:
                              v.scalar_tensor_tensor(
                        o_sb[qt][:], o_sb[qt][:], alq_sb[:, qc], ov[:],
                        ALU.mult, ALU.add),
                        deps=[tpv, o_tok[qt]] + ([t_ae] if t_ae else []))
                o_tok[qt] = oa
                pvdep[qt % 2] = [oa]
        for qt in range(NQT):
            qc = slice(qt, qt + 1)
            t2_ = p.op("dve", lambda v, qc=qc: v.reciprocal(
                li_sb[:, qc], l_sb[:, qc]), deps=lp_free[qt])
            t3_ = p.op("dve", lambda v, qt=qt, qc=qc: v.tensor_scalar(
                a1_sb[:], o_sb[qt][:], li_sb[:, qc], None, ALU.mult),
                deps=[t2_, o_tok[qt]] + a1_prev)
            sot = p.dma("spq",
                        a1_out[blk * GQB + qt * 128:
                               blk * GQB + qt * 128 + 128, :],
                        a1_sb[:], deps=[t3_])
            a1_prev = [sot]
        prev_grp = p.barrier_token()
    return p.barrier_token()



def build_mega(dbg=False, nph=7):
    TPRE = TOK
    TIN1, TOUT1, TOUT2 = 7168, 5632, 4096
    NE, H = 8, 1408
    nc = bass.Bass("TRN2", target_bir_lowering=False, debug=False)
    dt_ = nc.dram_tensor
    io = {}
    io["x_pad"] = dt_("x_pad", [D, N], F32, kind="ExternalInput").ap()
    for nm, shp in [("wq_t", [D, 256]), ("wk_t", [D, 256]), ("wv_t", [D, D]),
                    ("wo_t", [D, D]), ("gam16", [128, 16]), ("bet16", [128, 16]),
                    ("f_wqkv", [2, D, 1536]), ("f_wo", [2, 512, D]),
                    ("f_w1", [2, D, 2 * FFN_H]), ("f_w2", [2, FFN_H, D]),
                    ("f_qs", [128, 2]), ("f_ks", [128, 2]), ("f_lg", [128, 16]),
                    ("f_lb", [128, 16]), ("f_b1", [128, 4 * FFN_HC]),
                    ("f_b2", [128, 16]), ("f_pos", [128, 1]), ("f_bnd", [128, 2]),
                    ("a_pre", [128, 1]),
                    ("wr_t", [D, NE]), ("l1_wq", [D, D]), ("l1_wk", [D, D]),
                    ("l1_wv", [D, D]), ("msel_bm", [128, 100]),
                    ("msel_al", [128, 16]), ("moe_w1", [NE, D, H]),
                    ("moe_w3", [NE, D, H]), ("moe_w2", [NE, H, D])]:
        io[nm] = dt_(nm, shp, F32, kind="ExternalInput").ap()
    io["out_fm"] = dt_("out_fm", [D, TOK], F32, kind="ExternalOutput").ap()
    KD = "ExternalOutput" if dbg else "Internal"
    io["S_xn_f32"] = dt_("S_xn_f32", [D, N], F32, kind=KD).ap()
    io["S_xn_bf"] = dt_("S_xn_bf", [D, N], BF16, kind=KD).ap()
    io["S_a0"] = dt_("S_a0", [D, TOK], F32, kind=KD).ap()
    io["S_f0"] = dt_("S_f0", [D, TOK], F32, kind=KD).ap()
    io["SCR_a"] = dt_("SCR_a", [D, TIN1 + 1], F32, kind="Internal").ap()
    io["SCR_y1"] = dt_("SCR_y1", [D, TOUT1 + 1], F32, kind="Internal").ap()
    io["S_x1"] = dt_("S_x1", [D, TOK], F32, kind=KD).ap()
    xn1own = dt_("S_xn1own", [D, TOK], BF16, kind="Internal")
    io["S_xn1own"] = xn1own.ap()
    xg = dt_("S_xg", [2 * D, TOK], BF16, kind="Internal")
    io["S_xg"] = xg.ap()
    io["S_wts"] = dt_("S_wts", [TOK, NE], F32, kind=KD).ap()
    io["S_kfm"] = dt_("S_kfm", [D, N], BF16, kind="Internal").ap()
    io["S_vtm"] = dt_("S_vtm", [N, D], BF16, kind="Internal").ap()
    io["S_qfm"] = dt_("S_qfm", [D, TOK], BF16, kind="Internal").ap()
    io["S_a1"] = dt_("S_a1", [TOK, D], F32, kind=KD).ap()

    es = ExitStack()
    with es:
        p = Prog(nc, es)
        ar = Arena(es, nc, 105472)
        p.alloc_fn = ar
        p.psum_fn = ar.psum
        idf, idf_deps = emit_identity(p, es, nc, "identf", F32)
        idb, idb_deps = emit_identity(p, es, nc, "identb", BF16)
        shared = {"identf": idf, "identf_deps": idf_deps,
                  "identb": idb, "identb_deps": idb_deps}
        consts = make_const_tiles(p, es, nc)

        def nxt():
            tok = p.barrier_token()
            p.phase_base = tok
            ar.reset()
            p.epoch += 1
            return tok

        emit_lnphase(p, es, nc, ar, ar.psum,
                     {"x_src": io["x_pad"], "xn_f32": io["S_xn_f32"],
                      "xn_bf": io["S_xn_bf"]}, consts, N)
        prev = nxt()
        if nph < 2:
            block = es.enter_context(nc.Block())
            finish(p, block, p.barrier_token())
            p.build(block)
            return nc
        emit_l0attn(p, es, nc, ar, ar.psum,
                    {"xn_bf": io["S_xn_bf"], "a0_fm": io["S_a0"],
                     **{k: io[k] for k in ("wq_t", "wk_t", "wv_t", "wo_t",
                                           "gam16", "bet16", "a_pre")}},
                    shared, TPRE, TOK, prev=prev)
        prev = nxt()
        if nph < 3:
            block = es.enter_context(nc.Block())
            finish(p, block, p.barrier_token())
            p.build(block)
            return nc
        emit_l0ffn(p, es, nc, ar, ar.psum,
                   {"x_fm": io["S_xn_f32"][:, N - TIN1 - 1:N],
                    "f0_fm": io["S_f0"], "SCR_a": io["SCR_a"],
                    "SCR_y1": io["SCR_y1"],
                    **{k: io[k] for k in ("f_wqkv", "f_wo", "f_w1", "f_w2",
                                          "f_qs", "f_ks", "f_lg", "f_lb",
                                          "f_b1", "f_b2", "f_pos", "f_bnd")}},
                   shared, TIN1, TOUT1, TOUT2, prev=prev, bnd_windows=(6, 3))
        prev = nxt()
        if nph < 4:
            block = es.enter_context(nc.Block())
            finish(p, block, p.barrier_token())
            p.build(block)
            return nc
        emit_mixphase(p, es, nc, ar, ar.psum,
                      {"x_own": io["x_pad"][:, TOK:N], "a0_fm": io["S_a0"],
                       "f0_fm": io["S_f0"], "x1_fm": io["S_x1"],
                       "xn1_bf": io["S_xn1own"], "wts_tm": io["S_wts"],
                       "wr_t": io["wr_t"]}, consts, TOK, prev=prev)
        prev = nxt()
        if nph < 5:
            block = es.enter_context(nc.Block())
            finish(p, block, p.barrier_token())
            p.build(block)
            return nc
        agt = p.op("pool", lambda g: g.collective_compute(
            "AllGather", ALU.bypass,
            replica_groups=[[0, 1], [2, 3], [4, 5], [6, 7]],
            ins=[xn1own.ap().opt()], outs=[xg.ap().opt()]),
            deps=prev)
        prev = nxt()
        if nph < 6:
            block = es.enter_context(nc.Block())
            finish(p, block, p.barrier_token())
            p.build(block)
            return nc
        emit_l1attn_sym(p, es, nc, ar, ar.psum,
                        {"xg_bf": io["S_xg"], "xn1own_bf": io["S_xn1own"],
                         "S_kfm": io["S_kfm"], "S_vtm": io["S_vtm"],
                         "S_qfm": io["S_qfm"], "a1_tm": io["S_a1"],
                         **{k: io[k] for k in ("l1_wq", "l1_wk", "l1_wv",
                                               "msel_bm", "msel_al")}},
                        shared, prev=prev)
        prev = nxt()
        if nph < 7:
            block = es.enter_context(nc.Block())
            finish(p, block, p.barrier_token())
            p.build(block)
            return nc
        emit_moe(p, es, nc, ar, ar.psum,
                 {"xn1_bf": io["S_xn1own"], "x1_fm": io["S_x1"],
                  "a1_tm": io["S_a1"], "wts_tm": io["S_wts"],
                  "out_fm": io["out_fm"],
                  **{k: io[k] for k in ("moe_w1", "moe_w3", "moe_w2")}},
                 shared, TOK, TG=1024, prev=prev)
        block = es.enter_context(nc.Block())
        finish(p, block, p.barrier_token())
        p.build(block)
    return nc


def kernel_fused(dbg=False, **inputs):
    f32 = np.float32
    x = np.asarray(inputs["L0_attn_x"], dtype=f32)

    def tile128(vec):
        return _ct(np.tile(np.asarray(vec, f32)[None, :], (128, 1)))

    wqkv = np.asarray(inputs["L0_ffn_wqkv"], f32)
    wo2 = np.asarray(inputs["L0_ffn_wo"], f32)
    w1 = np.asarray(inputs["L0_ffn_w1"], f32)
    b1 = np.asarray(inputs["L0_ffn_b1"], f32)
    w2 = np.asarray(inputs["L0_ffn_w2"], f32)
    b2 = np.asarray(inputs["L0_ffn_b2"], f32)
    qsc = np.asarray(inputs["L0_ffn_q_scale"], f32)
    ksc = np.asarray(inputs["L0_ffn_k_scale"], f32)
    lg = np.asarray(inputs["L0_ffn_ln_g"], f32)
    lb = np.asarray(inputs["L0_ffn_ln_b"], f32)
    HID = 2730
    w1p = np.zeros((2, D, 2 * FFN_H), f32)
    w1p[:, :, :HID] = w1[:, :, :HID]
    w1p[:, :, FFN_H:FFN_H + HID] = w1[:, :, HID:]
    b1p = np.zeros((2, 2 * FFN_H), f32)
    b1p[:, :HID] = b1[:, :HID]
    b1p[:, FFN_H:FFN_H + HID] = b1[:, HID:]
    w2p = np.zeros((2, FFN_H, D), f32)
    w2p[:, :HID, :] = w2

    com = dict(
        wq_t=_ct(np.asarray(inputs["L0_attn_Wq"], f32).T),
        wk_t=_ct(np.asarray(inputs["L0_attn_Wk"], f32).T),
        wv_t=_ct(np.asarray(inputs["L0_attn_Wv"], f32).T),
        wo_t=_ct(np.asarray(inputs["L0_attn_Wo"], f32).T),
        gam16=tile128(inputs["L0_attn_gamma"]),
        bet16=tile128(inputs["L0_attn_beta"]),
        f_wqkv=wqkv, f_wo=wo2, f_w1=w1p, f_w2=w2p,
        f_qs=_ct(np.tile(qsc.T, (2, 1))), f_ks=_ct(np.tile(ksc.T, (2, 1))),
        f_lg=_ct(lg.reshape(2, 8, 128).transpose(2, 0, 1).reshape(128, 16)),
        f_lb=_ct(lb.reshape(2, 8, 128).transpose(2, 0, 1).reshape(128, 16)),
        f_b1=_ct(b1p.reshape(2, 2 * FFN_HC, 128).transpose(2, 0, 1)
                 .reshape(128, -1)),
        f_b2=_ct(b2.reshape(2, 8, 128).transpose(2, 0, 1).reshape(128, 16)),
        wr_t=_ct(np.asarray(inputs["L1_ffn_w_router"], f32).T),
        l1_wq=np.asarray(inputs["L1_attn_W_Q"], f32),
        l1_wk=np.asarray(inputs["L1_attn_W_K"], f32),
        l1_wv=np.asarray(inputs["L1_attn_W_V"], f32),
        moe_w1=np.asarray(inputs["L1_ffn_w1"], f32),
        moe_w3=np.asarray(inputs["L1_ffn_w3"], f32),
        moe_w2=np.asarray(inputs["L1_ffn_w2"], f32),
    )
    bm = [np.zeros((128, 100), f32), np.zeros((128, 100), f32)]
    al = [np.zeros((128, 16), f32), np.zeros((128, 16), f32)]
    base = 0
    for i in range(8):
        nkt = 9 + i
        bm[0][:, base + i + 1:base + nkt] = -1e30
        al[0][:, 2 * i] = 1.0
        al[1][:, 2 * i + 1] = 1.0
        base += nkt

    nc = build_mega(dbg)
    maps = []
    for b in range(B):
        for h in range(2):
            own = _ct(x[b, h * TOK:(h + 1) * TOK].T)
            if h == 0:
                xpad = np.concatenate([np.zeros((D, TOK), f32), own], 1)
                fpos = -3072.0
                bnd = np.full((128, 2), -1e30, f32)
            else:
                xpad = np.concatenate([_ct(x[b, 0:TOK].T), own], 1)
                fpos = 1024.0
                bnd = np.zeros((128, 2), f32)
            maps.append(dict(
                x_pad=_ct(xpad), f_pos=np.full((128, 1), fpos, f32),
                f_bnd=bnd, a_pre=np.full((128, 1), float(h), f32),
                msel_bm=bm[h], msel_al=al[h], **com))
    res = run_bass_kernel_spmd(nc, maps, core_ids=list(range(8)))
    out = np.empty((B, N, D), f32)
    for b in range(B):
        out[b, 0:TOK] = res.results[2 * b]["out_fm"].T
        out[b, TOK:] = res.results[2 * b + 1]["out_fm"].T
    if dbg:
        return out, res
    return out


kernel_multilaunch = kernel
kernel = kernel_fused


# version 06
# speedup vs baseline: 2.1470x; 1.0692x over previous
import numpy as np
from contextlib import ExitStack

import concourse.bass as bass
import concourse.mybir as mybir
from concourse.bass_utils import run_bass_kernel_spmd

F32 = mybir.dt.float32
BF16 = mybir.dt.bfloat16
AF = mybir.ActivationFunctionType
ALU = mybir.AluOpType
AX = mybir.AxisListType

NCORES = 8
B, N, D = 4, 8192, 1024
FFN_H = 2816
FFN_HC = FFN_H // 128
TOK = N // 2
NT = TOK // 128


NSLOT = 8
ENG_OF_QUEUE = {"spq": "sp", "actq": "act", "poolq": "pool"}
ENGINES = ["pe", "dve", "act", "pool", "sp"]
DMA_STREAMS = [f"{q}{i}" for q in ENG_OF_QUEUE for i in range(NSLOT)]
STREAMS = ENGINES + DMA_STREAMS


class Prog:
    def __init__(self, nc, es):
        self.nc = nc
        self.es = es
        self.sems = {s: nc.alloc_semaphore(f"sem_{s}") for s in STREAMS}
        self.count = {s: 0 for s in STREAMS}
        self.mark = {e: {s: 0 for s in STREAMS} for e in ENGINES}
        self.items = {e: [] for e in ENGINES}
        self._consts = {}
        self._rr = {}
        self._scratch = {}
        self.alloc_fn = alloc
        self.psum_fn = alloc_ps
        self.epoch = 0
        self.phase_base = []

    def scratch(self, name, shape, dtype):
        key = ("sb", self.epoch, name, tuple(shape), dtype)
        if key not in self._scratch:
            self._scratch[key] = self.alloc_fn(self.es, self.nc, name, shape,
                                               dtype)
        return self._scratch[key]

    def psum(self, name, shape, dtype=F32):
        key = ("ps", self.epoch, name, tuple(shape), dtype)
        if key not in self._scratch:
            self._scratch[key] = self.psum_fn(self.es, self.nc, name, shape,
                                              dtype)
        return self._scratch[key]

    def constf(self, value, dtype=F32):
        key = (dtype, float(value))
        if key not in self._consts:
            t = alloc(self.es, self.nc, f"constf_{len(self._consts)}",
                      [128, 1], dtype)
            tok = self.op("dve", lambda v, t=t: v.memset(t[:], value))
            self._consts[key] = (t, tok)
        t, tok = self._consts[key]
        return t, tok

    def _waits(self, eng, deps):
        waits = []
        for (s, v) in deps:
            if s == eng:
                pass
            if v > self.mark[eng][s]:
                waits.append((self.sems[s], v))
                self.mark[eng][s] = v
        return waits

    def op(self, eng, fn, deps=(), n=1):
        waits = self._waits(eng, list(deps) + self.phase_base)
        self.count[eng] += n
        tok = (eng, self.count[eng])
        self.items[eng].append((waits, fn, self.sems[eng], n))
        return tok

    def dma(self, queue, out, in_, deps=(), **kw):
        eng = ENG_OF_QUEUE[queue]
        slot = f"{queue}{self._rr.setdefault(queue, 0)}"
        self._rr[queue] = (self._rr[queue] + 1) % NSLOT
        deps = list(deps) + self.phase_base + [(slot, self.count[slot])]
        waits = self._waits(eng, deps)
        self.count[slot] += 16
        tok = (slot, self.count[slot])

        def fn(b, out=out, in_=in_, kw=kw):
            return b.dma_start(out=out, in_=in_, **kw)

        self.items[eng].append((waits, fn, self.sems[slot], 16))
        return tok

    def barrier_token(self):
        return [(s, self.count[s]) for s in STREAMS if self.count[s] > 0]

    def engines_token(self, engines=("pe", "dve", "act", "pool")):
        return [(e, self.count[e]) for e in engines if self.count[e] > 0]

    def build(self, block):
        builders = {
            "pe": block.tensor, "dve": block.vector, "act": block.scalar,
            "pool": block.gpsimd, "sp": block.sync,
        }
        for eng, items in self.items.items():
            if not items:
                continue

            def body(b, items=items):
                for waits, fn, sem, n in items:
                    for (s, v) in waits:
                        b.wait_ge(s, v)
                    ins = fn(b)
                    ins.then_inc(sem, n)

            builders[eng](body)


def finish(prog, block, final_deps):
    def fn(b):
        return b.nop()
    prog.op("sp", fn, deps=final_deps)



def alloc(es, nc, name, shape, dtype):
    return es.enter_context(nc.sbuf_tensor(name, shape, dtype))


def alloc_ps(es, nc, name, shape, dtype=F32):
    return es.enter_context(nc.psum_tensor(name, shape, dtype))


def make_const_tiles(p, es, nc):
    if hasattr(p, "_ln_consts"):
        return p._ln_consts
    c = {}
    p._ln_consts = c
    c["ones_f32_row"] = alloc(es, nc, "c_ones_f32", [1, 128], F32)
    t0 = p.op("dve", lambda v: v.memset(c["ones_f32_row"][:], 1.0))
    c["oo1024"] = alloc(es, nc, "c_oo1024", [128, 1], F32)
    t1 = p.op("dve", lambda v: v.memset(c["oo1024"][:], 1.0 / 1024.0))
    c["deps"] = [t0, t1]
    return c


def emit_ln_fm(p, es, nc, consts, x_tiles, x_deps, out_bf, out_f32, T, TT=512,
               nchunks=8, tag="ln"):
    sq = [p.scratch(f"{tag}_sq{c}", [128, TT], F32) for c in range(nchunks)]
    mu_ps = p.psum(f"{tag}_mups", [1, TT], F32)
    ms_ps = p.psum(f"{tag}_msps", [1, TT], F32)
    ab_ps = p.psum(f"{tag}_abps", [128, 2 * TT], F32)
    mu_sb = p.scratch(f"{tag}_musb", [1, TT], F32)
    mu2_sb = p.scratch(f"{tag}_mu2sb", [1, TT], F32)
    var_sb = p.scratch(f"{tag}_varsb", [1, TT], F32)
    sd_sb = p.scratch(f"{tag}_sdsb", [1, TT], F32)
    a_sb = p.scratch(f"{tag}_asb", [1, TT], F32)
    b_sb = p.scratch(f"{tag}_bsb", [1, TT], F32)
    tmp = [p.scratch(f"{tag}_tmp{c}", [128, TT], F32) for c in range(nchunks)]

    oo = consts["oo1024"]
    ones_row = consts["ones_f32_row"]
    cdeps = consts["deps"]

    it_prev = []
    for s0 in range(0, T, TT):
        sl = slice(s0, s0 + TT)
        sq_toks = []
        for c in range(nchunks):
            ts = p.op("act", lambda a, c=c, sl=sl: a.activation(
                sq[c][:], x_tiles[c][:, sl], AF.Square),
                deps=list(x_deps) + it_prev)
            sq_toks.append(ts)
        for c in range(nchunks):
            tm = p.op("pe", lambda t, c=c, sl=sl: t.matmul(
                mu_ps[:], oo[:], x_tiles[c][:, sl],
                start=(c == 0), stop=(c == nchunks - 1)),
                deps=cdeps + list(x_deps) + it_prev)
        tmu = (("pe", p.count["pe"]))
        for c in range(nchunks):
            tm2 = p.op("pe", lambda t, c=c: t.matmul(
                ms_ps[:], oo[:], sq[c][:],
                start=(c == 0), stop=(c == nchunks - 1)),
                deps=[sq_toks[c]])
        tms = (("pe", p.count["pe"]))
        t1 = p.op("act", lambda a: a.copy(mu_sb[:], mu_ps[:]), deps=[tmu])
        t2 = p.op("act", lambda a: a.activation(mu2_sb[:], mu_sb[:], AF.Square),
                  deps=[t1])
        t3 = p.op("dve", lambda v: v.tensor_sub(var_sb[:], ms_ps[:], mu2_sb[:]),
                  deps=[tms, t2])
        eps_t, eps_tok = p.constf(1e-5)
        t4 = p.op("act", lambda a, eps_t=eps_t: a.activation(
            sd_sb[:], var_sb[:], AF.Sqrt, bias=eps_t[0:1, 0:1]),
            deps=[t3, eps_tok])
        t5 = p.op("dve", lambda v: v.reciprocal(a_sb[:], sd_sb[:]), deps=[t4])
        t6 = p.op("dve", lambda v: v.scalar_tensor_tensor(
            b_sb[:], mu_sb[:], -1.0, a_sb[:], ALU.mult, ALU.mult),
            deps=[t5, t1])
        t7 = p.op("pe", lambda t: t.matmul(ab_ps[:, 0:TT], ones_row[:],
                                           a_sb[:], start=True, stop=True),
                  deps=cdeps + [t5])
        t8 = p.op("pe", lambda t: t.matmul(ab_ps[:, TT:2 * TT], ones_row[:],
                                           b_sb[:], start=True, stop=True),
                  deps=[t6])
        out_toks = []
        for c in range(nchunks):
            u1 = p.op("dve", lambda v, c=c, sl=sl: v.tensor_mul(
                tmp[c][:], x_tiles[c][:, sl], ab_ps[:, 0:TT]),
                deps=[t7] + x_deps)
            if out_f32 is not None:
                u2 = p.op("dve", lambda v, c=c, sl=sl: v.tensor_add(
                    out_f32[c][:, sl], tmp[c][:], ab_ps[:, TT:2 * TT]),
                    deps=[t8, u1])
                u3 = p.op("dve", lambda v, c=c, sl=sl: v.tensor_copy(
                    out_bf[c][:, sl], out_f32[c][:, sl]), deps=[u2])
                out_toks.append(u3)
            else:
                u2 = p.op("dve", lambda v, c=c, sl=sl: v.tensor_add(
                    out_bf[c][:, sl], tmp[c][:], ab_ps[:, TT:2 * TT]),
                    deps=[t8, u1])
                out_toks.append(u2)
        it_prev = p.engines_token()
        yield s0, out_toks



def build_ln0(T, SW=1024):
    SW = min(SW, T)
    nc = bass.Bass("TRN2", target_bir_lowering=False, debug=False)
    x_in = nc.dram_tensor("x_fm", [D, T], F32, kind="ExternalInput").ap()
    xn_out = nc.dram_tensor("xn_f32", [D, T], F32, kind="ExternalOutput").ap()
    xnb_out = nc.dram_tensor("xn_bf", [D, T], BF16, kind="ExternalOutput").ap()
    es = ExitStack()
    with es:
        p = Prog(nc, es)
        x_t = [alloc(es, nc, f"x{c}", [128, SW], F32) for c in range(8)]
        xn_t = [alloc(es, nc, f"xn{c}", [128, SW], F32) for c in range(8)]
        xnb_t = [alloc(es, nc, f"xnb{c}", [128, SW], BF16) for c in range(8)]
        consts = make_const_tiles(p, es, nc)
        st = []
        prev = []
        for w0 in range(0, T, SW):
            ld = []
            for c in range(8):
                ld.append(p.dma("spq", x_t[c][:],
                                x_in[c * 128:(c + 1) * 128, w0:w0 + SW],
                                deps=prev))
            for s0, toks in emit_ln_fm(p, es, nc, consts, x_t, ld, xnb_t,
                                       xn_t, SW):
                for c in range(8):
                    sl = slice(s0, s0 + 512)
                    gl = slice(w0 + s0, w0 + s0 + 512)
                    st.append(p.dma("spq", xn_out[c * 128:(c + 1) * 128, gl],
                                    xn_t[c][:, sl], deps=toks))
                    st.append(p.dma("spq", xnb_out[c * 128:(c + 1) * 128, gl],
                                    xnb_t[c][:, sl], deps=toks))
            prev = p.barrier_token()
        block = es.enter_context(nc.Block())
        finish(p, block, st)
        p.build(block)
    return nc



def emit_identity(p, es, nc, name="identf", dtype=F32):
    idn = alloc(es, nc, name, [128, 128], dtype)
    t0 = p.op("dve", lambda v: v.memset(idn[:], 0.0))
    t1 = p.op("pool", lambda g: g.affine_select(
        out=idn[:], in_=idn[:], pattern=[[-1, 128]],
        compare_op=ALU.not_equal, fill=1.0, base=0, channel_multiplier=1),
        deps=[t0])
    return idn, [t1]


def emit_moe(p, es, nc, A, APS, io, shared, T, TG=1024, NE=8, H=1408, prev=()):
    if True:
        HC = H // 128
        NTG = TG // 128
        xn1_g = [A(es, nc, f"xn1g{c}", [128, TG], BF16) for c in range(8)]
        acc = [A(es, nc, f"acc{i}", [128, D], F32) for i in range(NTG)]
        wts_sb = A(es, nc, "wts_sb", [128, NTG * NE], F32)
        w1_sb = A(es, nc, "w1_sb", [128, 8 * H], BF16)
        w3_sb = A(es, nc, "w3_sb", [128, 8 * H], BF16)
        w2_sb = A(es, nc, "w2_sb", [128, HC * D], BF16)
        h1s = A(es, nc, "h1s", [128, 512], BF16)
        he = A(es, nc, "he", [128, HC * 512], BF16)
        a1_t = A(es, nc, "a1_t", [128, D], F32)
        s1_sb = A(es, nc, "s1_sb", [128, D], F32)
        x1_t = A(es, nc, "x1_t", [128, D], F32)
        out_sb = A(es, nc, "out_sb", [128, D], F32)
        h_ps = [APS(es, nc, f"h_ps{i}", [128, 512], F32) for i in range(2)]
        y_ps = [APS(es, nc, f"y_ps{i}", [128, 512], F32) for i in range(2)]
        tr_ps = [APS(es, nc, f"tr_ps{i}", [128, 512], F32) for i in range(2)]

        ident, ident_deps = shared['identf'], shared['identf_deps']

        st_all = []
        prev_group = list(prev)
        wr1 = [list(prev) for _ in range(HC)]
        wr3 = [list(prev) for _ in range(HC)]
        wr2 = [list(prev) for _ in range(HC)]
        for g0 in range(0, T, TG):
            gsl = slice(g0, g0 + TG)
            ld_x = [p.dma("spq", xn1_g[c][:], io["xn1_bf"][c * 128:(c + 1) * 128, gsl],
                          deps=prev_group) for c in range(8)]
            ld_w = p.dma("spq", wts_sb[:].rearrange("p (n e) -> p n e", e=NE),
                         io["wts_tm"][gsl, :].rearrange("(n p) e -> p n e", p=128),
                         deps=prev_group)
            acc_tok = {i: None for i in range(NTG)}
            dbg_tok = []
            ydep = [[], []]
            last_he = []
            for e in range(NE):
                lw1 = [p.dma("poolq",
                             w1_sb[:].rearrange("p (c h) -> p c h", c=8)
                             [:, :, hc * 128:hc * 128 + 128],
                             io["moe_w1"][e].rearrange("(c p) h -> p c h", p=128)
                             [:, :, hc * 128:hc * 128 + 128],
                             deps=wr1[hc]) for hc in range(HC)]
                lw3 = [p.dma("poolq",
                             w3_sb[:].rearrange("p (c h) -> p c h", c=8)
                             [:, :, hc * 128:hc * 128 + 128],
                             io["moe_w3"][e].rearrange("(c p) h -> p c h", p=128)
                             [:, :, hc * 128:hc * 128 + 128],
                             deps=wr3[hc]) for hc in range(HC)]
                lw2 = [p.dma("poolq",
                             w2_sb[:].rearrange("p (c d) -> p c d", c=HC)[:, hc, :],
                             io["moe_w2"][e].rearrange("(c p) d -> p c d", p=128)[:, hc, :],
                             deps=wr2[hc]) for hc in range(HC)]
                wr1 = [[] for _ in range(HC)]
                wr3 = [[] for _ in range(HC)]
                wr2 = [[] for _ in range(HC)]
                for tt in range(TG // 512):
                    tsl = slice(tt * 512, tt * 512 + 512)
                    he_toks = []
                    for hc in range(HC):
                        pa, pb = h_ps[0], h_ps[1]
                        for c in range(8):
                            tok = p.op("pe", lambda t, c=c, hc=hc, tsl=tsl, pa=pa: t.matmul(
                                pa[:], w1_sb[:, c * H + hc * 128:c * H + hc * 128 + 128],
                                xn1_g[c][:, tsl], start=(c == 0), stop=(c == 7)),
                                deps=[lw1[hc]] + ld_x + (he_toks[-1:] if he_toks else last_he or list(prev_group)))
                        t_h1 = ("pe", p.count["pe"])
                        wr1[hc].append(t_h1)
                        t_silu = p.op("act", lambda a, pa=pa: a.activation(
                            h1s[:], pa[:], AF.Silu), deps=[t_h1])
                        for c in range(8):
                            tok = p.op("pe", lambda t, c=c, hc=hc, tsl=tsl, pb=pb: t.matmul(
                                pb[:], w3_sb[:, c * H + hc * 128:c * H + hc * 128 + 128],
                                xn1_g[c][:, tsl], start=(c == 0), stop=(c == 7)),
                                deps=[lw3[hc]] + ld_x)
                        t_h3 = ("pe", p.count["pe"])
                        wr3[hc].append(t_h3)
                        t_he = p.op("dve", lambda v, hc=hc, pb=pb: v.tensor_mul(
                            he[:, hc * 512:hc * 512 + 512], h1s[:], pb[:]),
                            deps=[t_silu, t_h3] + dbg_tok)
                        he_toks.append(t_he)
                        last_he = [t_he]
                    for q4 in range(4):
                        ti = (tt * 512 + q4 * 128) // 128
                        for dh in range(2):
                            yp = y_ps[dh]
                            for hc in range(HC):
                                tok = p.op("pe", lambda t, hc=hc, q4=q4, dh=dh, tt=tt, yp=yp: t.matmul(
                                    yp[:], he[:, hc * 512 + q4 * 128:hc * 512 + q4 * 128 + 128],
                                    w2_sb[:, hc * D + dh * 512:hc * D + dh * 512 + 512],
                                    start=(hc == 0), stop=(hc == HC - 1)),
                                    deps=[lw2[hc]] + he_toks + ydep[dh] +
                                    ([acc_tok[ti]] if acc_tok[ti] else []))
                                wr2[hc].append(tok)
                            t_y = ("pe", p.count["pe"])
                            wcol = wts_sb[:, ti * NE + e:ti * NE + e + 1]
                            if e == 0:
                                atok = p.op("dve", lambda v, ti=ti, dh=dh, yp=yp, wcol=wcol: v.tensor_scalar(
                                    acc[ti][:, dh * 512:dh * 512 + 512], yp[:],
                                    wcol, None, ALU.mult),
                                    deps=[t_y, ld_w])
                            else:
                                atok = p.op("dve", lambda v, ti=ti, dh=dh, yp=yp, wcol=wcol: v.scalar_tensor_tensor(
                                    acc[ti][:, dh * 512:dh * 512 + 512], yp[:],
                                    wcol, acc[ti][:, dh * 512:dh * 512 + 512],
                                    ALU.mult, ALU.add),
                                    deps=[t_y, ld_w] + ([acc_tok[ti]] if acc_tok[ti] else []))
                            acc_tok[ti] = atok
                            ydep[dh] = [atok]
            so_prev = []
            for ti in range(NTG):
                t0g = g0 + ti * 128
                la = p.dma("spq", a1_t[:], io["a1_tm"][t0g:t0g + 128, :],
                           deps=p.engines_token(("dve",)))
                ts1 = p.op("dve", lambda v, ti=ti: v.tensor_add(
                    s1_sb[:], acc[ti][:], a1_t[:]), deps=[la, acc_tok[ti]])
                lx = p.dma("spq", x1_t[:].rearrange("p (c t) -> p c t", c=8),
                           io["x1_fm"][:, t0g:t0g + 128].rearrange("(c p) t -> p c t", p=128),
                           deps=p.engines_token(("dve",)))
                tr_toks = []
                for c8 in range(8):
                    trp = tr_ps[c8 // 4]
                    csl = slice((c8 % 4) * 128, (c8 % 4) * 128 + 128)
                    ttr = p.op("pe", lambda t, c8=c8, trp=trp, csl=csl: t.transpose(
                        trp[:, csl], s1_sb[:, c8 * 128:c8 * 128 + 128], ident[:]),
                        deps=[ts1] + ident_deps + p.engines_token(("dve",)))
                    to = p.op("dve", lambda v, c8=c8, trp=trp, csl=csl: v.tensor_add(
                        out_sb[:, c8 * 128:c8 * 128 + 128], trp[:, csl], x1_t[:, c8 * 128:c8 * 128 + 128]),
                        deps=[ttr, lx] + so_prev)
                    tr_toks.append(to)
                so = p.dma("spq", io["out_fm"][:, t0g:t0g + 128].rearrange(
                    "(c p) t -> p c t", p=128),
                    out_sb[:].rearrange("p (c t) -> p c t", c=8), deps=tr_toks)
                so_prev = [so]
                st_all.append(so)
            prev_group = p.barrier_token()

        return p.barrier_token(), st_all


def build_moe(T, TG=1024, NE=8, H=1408):
    HC = H // 128
    nc = bass.Bass("TRN2", target_bir_lowering=False, debug=False)
    io = {
        "xn1_bf": nc.dram_tensor("xn1_bf", [D, T], BF16, kind="ExternalInput").ap(),
        "x1_fm": nc.dram_tensor("x1_fm", [D, T], F32, kind="ExternalInput").ap(),
        "a1_tm": nc.dram_tensor("a1_tm", [T, D], F32, kind="ExternalInput").ap(),
        "wts_tm": nc.dram_tensor("wts_tm", [T, NE], F32, kind="ExternalInput").ap(),
        "moe_w1": nc.dram_tensor("moe_w1", [NE, D, H], F32, kind="ExternalInput").ap(),
        "moe_w3": nc.dram_tensor("moe_w3", [NE, D, H], F32, kind="ExternalInput").ap(),
        "moe_w2": nc.dram_tensor("moe_w2", [NE, H, D], F32, kind="ExternalInput").ap(),
        "out_fm": nc.dram_tensor("out_fm", [D, T], F32, kind="ExternalOutput").ap(),
    }
    es = ExitStack()
    with es:
        p = Prog(nc, es)
        idt, idt_deps = emit_identity(p, es, nc)
        shared = {"identf": idt, "identf_deps": idt_deps}
        emit_moe(p, es, nc, alloc, alloc_ps, io, shared, T, TG, NE, H)
        block = es.enter_context(nc.Block())
        finish(p, block, p.barrier_token())
        p.build(block)
    return nc



def build_l1attn(Q0, QN, GQ=2048, KT=512):
    KVN = Q0 + QN
    nc = bass.Bass("TRN2", target_bir_lowering=False, debug=False)
    xn1_in = nc.dram_tensor("xn1_bf", [D, KVN], BF16, kind="ExternalInput").ap()
    wq_in = nc.dram_tensor("l1_wq", [D, D], F32, kind="ExternalInput").ap()
    wk_in = nc.dram_tensor("l1_wk", [D, D], F32, kind="ExternalInput").ap()
    wv_in = nc.dram_tensor("l1_wv", [D, D], F32, kind="ExternalInput").ap()
    a1_out = nc.dram_tensor("a1_tm", [QN, D], F32, kind="ExternalOutput").ap()
    kfm_scr = nc.dram_tensor("kfm_scr", [D, KVN], BF16, kind="Internal").ap()
    vtm_scr = nc.dram_tensor("vtm_scr", [KVN, D], BF16, kind="Internal").ap()
    qfm_scr = nc.dram_tensor("qfm_scr", [D, QN], BF16, kind="Internal").ap()

    es = ExitStack()
    with es:
        p = Prog(nc, es)
        wq_sb = alloc(es, nc, "wq_sb", [128, 8 * D], BF16)
        wk_sb = alloc(es, nc, "wk_sb", [128, 8 * D], BF16)
        wv_sb = alloc(es, nc, "wv_sb", [128, 8 * D], BF16)
        xn_t = [alloc(es, nc, f"xn_t{c}", [128, 512], BF16) for c in range(8)]
        ev_k = alloc(es, nc, "ev_k", [128, 512], BF16)
        ev_q = alloc(es, nc, "ev_q", [128, 512], BF16)
        ev_v = alloc(es, nc, "ev_v", [128, D], BF16)
        ps512 = [alloc_ps(es, nc, f"ps512_{i}", [128, 512], F32) for i in range(2)]
        ps1024 = [alloc_ps(es, nc, f"ps1024_{i}", [128, D], F32) for i in range(2)]
        k_ps, q_ps, v_ps = ps512[0], ps512[1], ps1024[0]

        lwq = p.dma("poolq", wq_sb[:].rearrange("p (c d) -> p c d", c=8),
                    wq_in.rearrange("(c p) d -> p c d", p=128))
        lwk = p.dma("poolq", wk_sb[:].rearrange("p (c d) -> p c d", c=8),
                    wk_in.rearrange("(c p) d -> p c d", p=128))
        lwv = p.dma("poolq", wv_sb[:].rearrange("p (c d) -> p c d", c=8),
                    wv_in.rearrange("(c p) d -> p c d", p=128))
        prev_it = []
        st_k = st_q = st_v = None
        te_k = te_q = te_v = None
        for t0 in range(0, KVN, 512):
            ld = [p.dma("spq", xn_t[c][:], xn1_in[c * 128:(c + 1) * 128,
                                                  t0:t0 + 512],
                        deps=prev_it) for c in range(8)]
            for m in range(8):
                for c in range(8):
                    p.op("pe", lambda t, m=m, c=c: t.matmul(
                        k_ps[:], wk_sb[:, c * D + m * 128:c * D + m * 128 + 128],
                        xn_t[c][:], start=(c == 0), stop=(c == 7)),
                        deps=[lwk] + ld + prev_it +
                        ([te_k] if te_k else []))
                tk = ("pe", p.count["pe"])
                te_k = p.op("act", lambda a: a.copy(ev_k[:], k_ps[:]),
                            deps=[tk] + ([st_k] if st_k else []))
                st_k = p.dma("spq", kfm_scr[m * 128:(m + 1) * 128, t0:t0 + 512],
                             ev_k[:], deps=[te_k])
            if t0 >= Q0:
                for m in range(8):
                    for c in range(8):
                        p.op("pe", lambda t, m=m, c=c: t.matmul(
                            q_ps[:], wq_sb[:, c * D + m * 128:c * D + m * 128 + 128],
                            xn_t[c][:], start=(c == 0), stop=(c == 7)),
                            deps=[lwq] + ld + prev_it +
                            ([te_q] if te_q else []))
                    tq = ("pe", p.count["pe"])
                    te_q = p.op("act", lambda a: a.copy(ev_q[:], q_ps[:]),
                                deps=[tq] + ([st_q] if st_q else []))
                    st_q = p.dma("spq", qfm_scr[m * 128:(m + 1) * 128,
                                                t0 - Q0:t0 - Q0 + 512],
                                 ev_q[:], deps=[te_q])
            for r in range(4):
                for dh in range(2):
                    for c in range(8):
                        p.op("pe", lambda t, r=r, dh=dh, c=c: t.matmul(
                            v_ps[:, dh * 512:dh * 512 + 512],
                            xn_t[c][:, r * 128:r * 128 + 128],
                            wv_sb[:, c * D + dh * 512:c * D + dh * 512 + 512],
                            start=(c == 0), stop=(c == 7)),
                            deps=[lwv] + ld + prev_it +
                            ([te_v] if te_v else []))
                tv = ("pe", p.count["pe"])
                te_v = p.op("act", lambda a: a.copy(ev_v[:], v_ps[:]),
                            deps=[tv] + ([st_v] if st_v else []))
                st_v = p.dma("spq", vtm_scr[t0 + r * 128:t0 + r * 128 + 128, :],
                             ev_v[:], deps=[te_v])
            prev_it = p.engines_token()
        phaseA_done = p.barrier_token()

        NKT = KVN // KT
        NQTG = GQ // 128
        qg = [alloc(es, nc, f"qg{c}", [128, GQ], BF16) for c in range(8)]
        kv_k = [alloc(es, nc, f"kv_k{i}", [128, 8 * KT], BF16) for i in range(2)]
        kv_v = [alloc(es, nc, f"kv_v{i}", [128, 4 * D], BF16) for i in range(2)]
        o_sb = [alloc(es, nc, f"o_sb{i}", [128, D], F32) for i in range(NQTG)]
        mcols = [alloc(es, nc, f"mcols{i}", [128, NKT], F32) for i in range(NQTG)]
        lcols = [alloc(es, nc, f"lcols{i}", [128, NKT], F32) for i in range(NQTG)]
        m_sb = alloc(es, nc, "m_sb", [128, NQTG], F32)
        nm_sb = alloc(es, nc, "nm_sb", [128, NQTG], F32)
        l_sb = alloc(es, nc, "l_sb", [128, NQTG], F32)
        li_sb = alloc(es, nc, "li_sb", [128, NQTG], F32)
        s_m = alloc(es, nc, "s_m", [128, KT], F32)
        p_sb = alloc(es, nc, "p_sb", [128, KT], BF16)
        pt_sb = alloc(es, nc, "pt_sb", [128, KT], BF16)
        a1_sb = alloc(es, nc, "a1_sb", [128, D], F32)
        s_ps = ps512
        pt_ps = alloc_ps(es, nc, "pt_ps", [128, KT], BF16)
        pv_ps = ps1024

        identb, identb_deps = emit_identity(p, es, nc, name="identb", dtype=BF16)
        masks = []
        mtoks = []
        for oi, off in enumerate(range(0, KT, 128)):
            mt = alloc(es, nc, f"maskadd{oi}", [128, KT], F32)
            t0_ = p.op("dve", lambda v, mt=mt: v.memset(mt[:], 0.0))
            t1_ = p.op("pool", lambda g, mt=mt, off=off: g.affine_select(
                out=mt[:], in_=mt[:], pattern=[[-1, KT]],
                compare_op=ALU.is_ge, fill=-1e30, base=off,
                channel_multiplier=1), deps=[t0_])
            masks.append(mt)
            mtoks.append(t1_)
        c32_t, c32_tok = p.constf(1.0 / 32.0)

        prev_grp = phaseA_done
        for g0 in range(0, QN, GQ):
            NQT = min(GQ, QN - g0) // 128
            lq = [p.dma("spq", qg[c][:, 0:NQT * 128],
                        qfm_scr[c * 128:(c + 1) * 128, g0:g0 + NQT * 128],
                        deps=prev_grp) for c in range(8)]
            kend = Q0 + g0 + NQT * 128
            nkt = (kend + KT - 1) // KT
            o_tok = {}
            a1_prev = []
            pvdep = [[], []]
            kbuf_rd = [list(prev_grp), list(prev_grp)]
            for pss in (0, 1):
                for kt in range(nkt):
                    kb = kt % 2
                    lk = p.dma("spq", kv_k[kb][:].rearrange(
                        "p (c t) -> p c t", c=8),
                        kfm_scr[:, kt * KT:(kt + 1) * KT].rearrange(
                            "(c p) t -> p c t", p=128),
                        deps=kbuf_rd[kb])
                    lv = None
                    if pss == 1:
                        lv = p.dma("actq", kv_v[kb][:].rearrange(
                            "p (r d) -> p r d", r=4),
                            vtm_scr[kt * KT:(kt + 1) * KT, :].rearrange(
                                "(r p) d -> p r d", p=128),
                            deps=kbuf_rd[kb])
                    kbuf_rd[kb] = []
                    for qt in range(NQT):
                        qglob = Q0 + g0 + qt * 128
                        if kt * KT >= qglob + 128:
                            continue
                        sp = s_ps[kt % 2]
                        for c in range(8):
                            p.op("pe", lambda t, c=c, qt=qt, sp=sp, kb=kb: t.matmul(
                                sp[:], qg[c][:, qt * 128:qt * 128 + 128],
                                kv_k[kb][:, c * KT:c * KT + KT],
                                start=(c == 0), stop=(c == 7)),
                                deps=[lk] + lq + p.engines_token(("dve", "act")))
                        ts = ("pe", p.count["pe"])
                        kbuf_rd[kb].append(ts)
                        is_diag = kt == qglob // KT
                        if is_diag:
                            oi = (qglob % KT) // 128
                            tm_ = p.op("dve", lambda v, sp=sp, oi=oi: v.tensor_add(
                                s_m[:], sp[:], masks[oi][:]),
                                deps=[ts] + mtoks + p.engines_token(("act",)))
                            src, dep0 = s_m, [tm_]
                        else:
                            src, dep0 = sp, [ts]
                        if pss == 0:
                            p.op("dve", lambda v, src=src, qt=qt, kt=kt: v.reduce_max(
                                mcols[qt][:, kt:kt + 1], src[:], axis=AX.X),
                                deps=dep0)
                        else:
                            texp = p.op("act", lambda a, src=src, qt=qt, kt=kt: a.activation(
                                p_sb[:], src[:], AF.Exp,
                                bias=nm_sb[:, qt:qt + 1],
                                scale=c32_t[:, 0:1],
                                accum_out=lcols[qt][:, kt:kt + 1]),
                                deps=dep0 + [("dve", p.count["dve"]), c32_tok])
                            trt = []
                            for j in range(4):
                                tt_ = p.op("pe", lambda t, j=j: t.transpose(
                                    pt_ps[:, j * 128:j * 128 + 128],
                                    p_sb[:, j * 128:j * 128 + 128], identb[:]),
                                    deps=[texp] + identb_deps)
                                trt.append(tt_)
                            tev = p.op("act", lambda a: a.copy(pt_sb[:], pt_ps[:]),
                                       deps=trt)
                            ov = pv_ps[qt % 2]
                            nvr = min(4, ((qglob + 128 - kt * KT) + 127) // 128)
                            for dh in range(2):
                                for r in range(nvr):
                                    p.op("pe", lambda t, r=r, dh=dh, ov=ov, kb=kb, nvr=nvr: t.matmul(
                                        ov[:, dh * 512:dh * 512 + 512],
                                        pt_sb[:, r * 128:r * 128 + 128],
                                        kv_v[kb][:, r * D + dh * 512:r * D + dh * 512 + 512],
                                        start=(r == 0), stop=(r == nvr - 1)),
                                        deps=[tev, lv] + pvdep[qt % 2] +
                                        ([o_tok[qt]] if qt in o_tok else []))
                            tpv = ("pe", p.count["pe"])
                            kbuf_rd[kb].append(tpv)
                            if kt == 0:
                                oa = p.op("dve", lambda v, qt=qt, ov=ov: v.tensor_copy(
                                    o_sb[qt][:], ov[:]), deps=[tpv])
                            else:
                                oa = p.op("dve", lambda v, qt=qt, ov=ov: v.tensor_add(
                                    o_sb[qt][:], o_sb[qt][:], ov[:]),
                                    deps=[tpv, o_tok[qt]])
                            o_tok[qt] = oa
                            pvdep[qt % 2] = [oa]
                if pss == 0:
                    for qt in range(NQT):
                        nk_q = (Q0 + g0 + qt * 128) // KT + 1
                        t1_ = p.op("dve", lambda v, qt=qt, nk_q=nk_q: v.reduce_max(
                            m_sb[:, qt:qt + 1], mcols[qt][:, 0:nk_q], axis=AX.X),
                            deps=p.engines_token(("dve",)))
                        p.op("dve", lambda v, qt=qt: v.tensor_scalar_mul(
                            nm_sb[:, qt:qt + 1], m_sb[:, qt:qt + 1], -1.0 / 32.0),
                            deps=[t1_])
                else:
                    for qt in range(NQT):
                        nk_q = (Q0 + g0 + qt * 128) // KT + 1
                        t1_ = p.op("dve", lambda v, qt=qt, nk_q=nk_q: v.reduce_sum(
                            l_sb[:, qt:qt + 1], lcols[qt][:, 0:nk_q], axis=AX.X),
                            deps=p.engines_token(("act",)) + [o_tok[qt]])
                        t2_ = p.op("dve", lambda v, qt=qt: v.reciprocal(
                            li_sb[:, qt:qt + 1], l_sb[:, qt:qt + 1]),
                            deps=[t1_])
                        t3_ = p.op("dve", lambda v, qt=qt: v.tensor_scalar(
                            a1_sb[:], o_sb[qt][:], li_sb[:, qt:qt + 1],
                            None, ALU.mult), deps=[t2_, o_tok[qt]] + a1_prev)
                        sot = p.dma("spq",
                                    a1_out[g0 + qt * 128:g0 + qt * 128 + 128, :],
                                    a1_sb[:], deps=[t3_])
                        a1_prev = [sot]
            prev_grp = p.barrier_token()

        block = es.enter_context(nc.Block())
        finish(p, block, p.barrier_token())
        p.build(block)
    return nc



def emit_l0attn(p, es, nc, A, APS, io, shared, TPRE, T, prev=()):
    NH, FD, HD = 16, 16, 64
    TTOT = TPRE + T
    if True:
        identb, identb_deps = shared['identb'], shared['identb_deps']
        wq_sb = A(es, nc, "wq_sb", [128, 8 * 256], BF16)
        wk_sb = A(es, nc, "wk_sb", [128, 8 * 256], BF16)
        wv_sb = A(es, nc, "wv_sb", [128, 8 * D], BF16)
        wo_sb = A(es, nc, "wo_sb", [128, 8 * D], BF16)
        gam_sb = A(es, nc, "gam_sb", [128, FD], F32)
        bet_sb = A(es, nc, "bet_sb", [128, FD], F32)
        gq_sb = A(es, nc, "gq_sb", [128, FD], F32)
        bq_sb = A(es, nc, "bq_sb", [128, FD], F32)
        lw = [p.dma("poolq", wq_sb[:].rearrange("p (c m) -> p c m", c=8),
                    io["wq_t"].rearrange("(c p) m -> p c m", p=128)),
              p.dma("poolq", wk_sb[:].rearrange("p (c m) -> p c m", c=8),
                    io["wk_t"].rearrange("(c p) m -> p c m", p=128)),
              p.dma("poolq", wv_sb[:].rearrange("p (c m) -> p c m", c=8),
                    io["wv_t"].rearrange("(c p) m -> p c m", p=128)),
              p.dma("poolq", wo_sb[:].rearrange("p (c m) -> p c m", c=8),
                    io["wo_t"].rearrange("(c p) m -> p c m", p=128)),
              p.dma("spq", gam_sb[:], io["gam16"]),
              p.dma("spq", bet_sb[:], io["bet16"])]
        tg1 = p.op("dve", lambda v: v.tensor_scalar_mul(gq_sb[:], gam_sb[:], 0.25),
                   deps=[lw[4]])
        tg2 = p.op("dve", lambda v: v.tensor_scalar_mul(bq_sb[:], bet_sb[:], 0.25),
                   deps=[lw[5]])
        gb_deps = [tg1, tg2, lw[4], lw[5]]

        S_m = [A(es, nc, f"S_m{m}", [128, NH * 65], F32) for m in range(2)]
        S_bf = [A(es, nc, f"S_bf{m}", [128, NH * 65], BF16) for m in range(2)]
        s_up = []
        for m in range(2):
            s_up.append(p.op("dve", lambda v, m=m: v.memset(S_m[m][:], 0.0)))
            s_up.append(p.op("dve", lambda v, m=m: v.memset(S_bf[m][:], 0.0)))

        mask01 = A(es, nc, "mask01", [128, 128], F32)
        tmk0 = p.op("dve", lambda v: v.memset(mask01[:], 1.0))
        tmk1 = p.op("pool", lambda g: g.affine_select(
            out=mask01[:], in_=mask01[:], pattern=[[1, 128]],
            compare_op=ALU.is_ge, fill=0.0, base=0, channel_multiplier=-1),
            deps=[tmk0])
        mask_deps = [tmk1]
        sel2 = A(es, nc, "sel2", [2, 128], BF16)
        ts0 = p.op("dve", lambda v: v.memset(sel2[:], 1.0))
        ts1 = p.op("pool", lambda g: g.affine_select(
            out=sel2[:], in_=sel2[:], pattern=[[1, 128]],
            compare_op=ALU.is_ge, fill=0.0, base=0, channel_multiplier=-64),
            deps=[ts0])
        ts2 = p.op("pool", lambda g: g.affine_select(
            out=sel2[:], in_=sel2[:], pattern=[[-1, 128]],
            compare_op=ALU.is_ge, fill=0.0, base=63, channel_multiplier=64),
            deps=[ts1])
        sel2_deps = [ts2]

        apre_bf = None
        apre_deps = []
        if "a_pre" in io:
            apre_f = A(es, nc, "apre_f", [128, 1], F32)
            apre_bf = A(es, nc, "apre_b", [128, 1], BF16)
            lap = p.dma("spq", apre_f[:], io["a_pre"])
            tap = p.op("dve", lambda v: v.tensor_copy(apre_bf[:], apre_f[:]),
                       deps=[lap])
            apre_deps = [tap]

        xn_t = [A(es, nc, f"xn_t{c}", [128, 512], BF16) for c in range(8)]
        qraw = A(es, nc, "qraw", [128, 256], F32)
        kraw = A(es, nc, "kraw", [128, 256], F32)
        cen = A(es, nc, "cen", [128, 256], F32)
        sq_t = A(es, nc, "sq_t", [128, 256], F32)
        mu_t = A(es, nc, "mu_t", [128, NH], F32)
        var_t = A(es, nc, "var_t", [128, NH], F32)
        sd_t = A(es, nc, "sd_t", [128, NH], F32)
        rstd_t = A(es, nc, "rstd_t", [128, NH], F32)
        qn2 = A(es, nc, "qn2", [128, 256], BF16)
        kn2 = A(es, nc, "kn2", [128, 256], BF16)
        phiq = A(es, nc, "phiq", [128, 4096], BF16)
        phik = A(es, nc, "phik", [128, 4096], BF16)
        qfm = A(es, nc, "qfm", [16, NH * 128], BF16)
        kfm = A(es, nc, "kfm", [16, NH * 128], BF16)
        phiqfm = A(es, nc, "phiqfm", [128, 4096], BF16)
        s_sb = A(es, nc, "s_sb", [128, NH * 128], BF16)
        sm_f = A(es, nc, "sm_f", [128, 512], F32)
        v_aug = A(es, nc, "v_aug", [128, NH * 65], BF16)
        zrow = A(es, nc, "zrow", [65, 1024], F32)
        zasm = A(es, nc, "zasm", [2, 1024], F32)
        zinv = A(es, nc, "zinv", [2, 1024], BF16)
        zb_sb = A(es, nc, "zb_sb", [128, 1024], F32)
        attn_all = A(es, nc, "attn_all", [128, 8 * 512], BF16)
        a0ev = A(es, nc, "a0ev", [128, 512], F32)

        PSA = APS(es, nc, "PSA", [128, 2048], F32)
        PSB = APS(es, nc, "PSB", [128, 2048], F32)
        PSA_bf = PSA[:].bitcast(BF16)
        eps_t, eps_tok = p.constf(1e-5)
        eng = p.engines_token

        prev_tile = list(prev)
        a0_store_prev = []
        for t0 in range(0, TTOT, 512):
            own = t0 >= TPRE
            ld_x = [p.dma("spq", xn_t[c][:],
                          io["xn_bf"][c * 128:(c + 1) * 128, t0:t0 + 512],
                          deps=prev_tile) for c in range(8)]
            attn_toks = {}
            for cc in range(4):
                ck = slice(cc * 128, cc * 128 + 128)
                prj_dep = [lw[0], lw[1], lw[2]] + ld_x + eng(("act", "dve"))
                if own:
                    for c in range(8):
                        p.op("pe", lambda t, c=c, ck=ck: t.matmul(
                            PSB[:, 0:256], xn_t[c][:, ck],
                            wq_sb[:, c * 256:(c + 1) * 256],
                            start=(c == 0), stop=(c == 7)), deps=prj_dep)
                    t_q = ("pe", p.count["pe"])
                for c in range(8):
                    p.op("pe", lambda t, c=c, ck=ck: t.matmul(
                        PSB[:, 256:512], xn_t[c][:, ck],
                        wk_sb[:, c * 256:(c + 1) * 256],
                        start=(c == 0), stop=(c == 7)), deps=prj_dep)
                t_k = ("pe", p.count["pe"])
                for dh in range(2):
                    for c in range(8):
                        p.op("pe", lambda t, c=c, dh=dh, ck=ck: t.matmul(
                            PSB[:, 1024 + dh * 512:1024 + dh * 512 + 512],
                            xn_t[c][:, ck],
                            wv_sb[:, c * D + dh * 512:c * D + dh * 512 + 512],
                            start=(c == 0), stop=(c == 7)), deps=prj_dep)
                t_v = ("pe", p.count["pe"])
                tv1 = p.op("act", lambda a: a.copy(
                    v_aug[:].rearrange("p (h j) -> p h j", j=65)[:, :, 0:64],
                    PSB[:, 1024:2048].rearrange("p (h j) -> p h j", j=64)),
                    deps=[t_v])
                tv2 = p.op("dve", lambda v: v.memset(
                    v_aug[:].rearrange("p (h j) -> p h j", j=65)[:, :, 64:65],
                    1.0), deps=[t_v])
                v_deps = [tv1, tv2]
                if not own and apre_bf is not None:
                    tvm = p.op("dve", lambda v: v.tensor_tensor(
                        v_aug[:], v_aug[:],
                        apre_bf[:, 0:1].broadcast_to([128, NH * 65]),
                        ALU.mult), deps=v_deps + apre_deps)
                    v_deps = [tvm]

                def fmln(src_sl, raw, dst, gcol, bcol, extra_deps):
                    r3 = lambda ap: ap.rearrange("p (h d) -> p h d", d=FD)
                    t1 = p.op("act", lambda a, src_sl=src_sl, raw=raw: a.copy(
                        raw[:], src_sl), deps=extra_deps)
                    t2 = p.op("dve", lambda v, raw=raw: v.tensor_reduce(
                        mu_t[:].unsqueeze(2), r3(raw[:]), AX.X, ALU.add),
                        deps=[t1])
                    t2b = p.op("dve", lambda v: v.tensor_scalar_mul(
                        mu_t[:], mu_t[:], 1.0 / FD), deps=[t2])
                    t3 = p.op("dve", lambda v, raw=raw: v.tensor_sub(
                        r3(cen[:]), r3(raw[:]),
                        mu_t[:].unsqueeze(2).broadcast_to([128, NH, FD])),
                        deps=[t2b, t1])
                    t4 = p.op("act", lambda a: a.activation(
                        sq_t[:], cen[:], AF.Square), deps=[t3])
                    t5 = p.op("dve", lambda v: v.tensor_reduce(
                        var_t[:].unsqueeze(2), r3(sq_t[:]), AX.X, ALU.add),
                        deps=[t4])
                    t5b = p.op("dve", lambda v: v.tensor_scalar_mul(
                        var_t[:], var_t[:], 1.0 / FD), deps=[t5])
                    t6 = p.op("act", lambda a, eps_t=eps_t: a.activation(
                        sd_t[:], var_t[:], AF.Sqrt, bias=eps_t[:, 0:1]),
                        deps=[t5b, eps_tok])
                    t7 = p.op("dve", lambda v: v.reciprocal(rstd_t[:], sd_t[:]),
                              deps=[t6])
                    t8 = p.op("dve", lambda v: v.tensor_mul(
                        r3(cen[:]), r3(cen[:]),
                        rstd_t[:].unsqueeze(2).broadcast_to([128, NH, FD])),
                        deps=[t7, t3])
                    t9 = p.op("dve", lambda v, gcol=gcol: v.tensor_mul(
                        r3(cen[:]), r3(cen[:]),
                        gcol[:].unsqueeze(1).broadcast_to([128, NH, FD])),
                        deps=[t8] + gb_deps)
                    t10 = p.op("dve", lambda v, dst=dst, bcol=bcol: v.tensor_add(
                        r3(dst[:]), r3(cen[:]),
                        bcol[:].unsqueeze(1).broadcast_to([128, NH, FD])),
                        deps=[t9])
                    return t10

                tkn = fmln(PSB[:, 256:512], kraw, kn2, gam_sb, bet_sb, [t_k])
                if own:
                    tqn = fmln(PSB[:, 0:256], qraw, qn2, gq_sb, bq_sb,
                               [t_q, tkn])

                r_phi = lambda x: x.rearrange("p (h d) -> p h d", d=FD)
                tphik = p.op("dve", lambda v, r_phi=r_phi: v.tensor_tensor(
                    phik[:].rearrange("p (h d e) -> p h d e", d=FD, e=FD),
                    r_phi(kn2[:]).unsqueeze(3).broadcast_to([128, NH, FD, FD]),
                    r_phi(kn2[:]).unsqueeze(2).broadcast_to([128, NH, FD, FD]),
                    ALU.mult), deps=[tkn])
                if own:
                    tphiq = p.op("dve", lambda v, r_phi=r_phi: v.tensor_tensor(
                        phiq[:].rearrange("p (h d e) -> p h d e", d=FD, e=FD),
                        r_phi(qn2[:]).unsqueeze(3).broadcast_to([128, NH, FD, FD]),
                        r_phi(qn2[:]).unsqueeze(2).broadcast_to([128, NH, FD, FD]),
                        ALU.mult), deps=[tqn])

                    trq = [p.op("pe", lambda t, h=h: t.transpose(
                        PSA_bf[0:16, h * 128:h * 128 + 128],
                        qn2[:, h * FD:h * FD + FD], identb[:]),
                        deps=[tqn] + identb_deps + eng(("act",)))
                        for h in range(NH)]
                    tevq = p.op("act", lambda a: a.copy(
                        qfm[:], PSA_bf[0:16, 0:2048]), deps=trq)
                    trk = [p.op("pe", lambda t, h=h: t.transpose(
                        PSA_bf[32:48, h * 128:h * 128 + 128],
                        kn2[:, h * FD:h * FD + FD], identb[:]),
                        deps=[tkn] + identb_deps + eng(("act",)))
                        for h in range(NH)]
                    tevk = p.op("act", lambda a: a.copy(
                        kfm[:], PSA_bf[32:48, 0:2048]), deps=trk)
                    phifm_toks = []
                    for rnd in range(2):
                        trs = [p.op("pe", lambda t, fi=rnd * 16 + jj, jj=jj: t.transpose(
                            PSA_bf[:, 2048 + jj * 128:2048 + jj * 128 + 128],
                            phiq[:, fi * 128:fi * 128 + 128], identb[:]),
                            deps=[tphiq] + identb_deps + eng(("act",)))
                            for jj in range(16)]
                        tev = p.op("act", lambda a, rnd=rnd: a.copy(
                            phiqfm[:, rnd * 2048:rnd * 2048 + 2048],
                            PSA_bf[:, 2048:4096]), deps=trs)
                        phifm_toks.append(tev)

                    sq_toks = []
                    for grp in range(4):
                        mms = [p.op("pe", lambda t, h=grp * 4 + hh, hh=hh: t.matmul(
                            PSB[:, 512 + hh * 128:512 + hh * 128 + 128],
                            kfm[0:16, h * 128:h * 128 + 128],
                            qfm[0:16, h * 128:h * 128 + 128],
                            start=True, stop=True),
                            deps=[tevq, tevk] + eng(("dve",)))
                            for hh in range(4)]
                        tm1 = p.op("dve", lambda v: v.tensor_tensor(
                            sm_f[:].rearrange("p (g t) -> p g t", g=4),
                            PSB[:, 512:1024].rearrange("p (g t) -> p g t", g=4),
                            mask01[:].unsqueeze(1).broadcast_to([128, 4, 128]),
                            ALU.mult),
                            deps=mms + mask_deps + sq_toks[-1:])
                        tm2 = p.op("act", lambda a, grp=grp: a.activation(
                            s_sb[:, grp * 512:grp * 512 + 512], sm_f[:],
                            AF.Square), deps=[tm1])
                        sq_toks.append(tm2)

                    o_toks = []
                    z_toks = []
                    for h in range(NH):
                        j, pb = h // 2, 64 * (h % 2)
                        tp = (0, pb)
                        m1 = p.op("pe", lambda t, h=h, j=j, pb=pb, tp=tp: t.matmul(
                            PSA[pb:pb + 64, j * 128:j * 128 + 128],
                            S_bf[0][:, h * 65:h * 65 + 64],
                            phiqfm[:, (2 * h) * 128:(2 * h) * 128 + 128],
                            start=True, stop=False, tile_position=tp),
                            deps=phifm_toks + s_up + eng(("dve",)))
                        m2 = p.op("pe", lambda t, h=h, j=j, pb=pb, tp=tp: t.matmul(
                            PSA[pb:pb + 64, j * 128:j * 128 + 128],
                            S_bf[1][:, h * 65:h * 65 + 64],
                            phiqfm[:, (2 * h + 1) * 128:(2 * h + 1) * 128 + 128],
                            start=False, stop=False, tile_position=tp), deps=[])
                        m3 = p.op("pe", lambda t, h=h, j=j, pb=pb, tp=tp: t.matmul(
                            PSA[pb:pb + 64, j * 128:j * 128 + 128],
                            v_aug[:, h * 65:h * 65 + 64],
                            s_sb[:, h * 128:h * 128 + 128],
                            start=False, stop=True, tile_position=tp),
                            deps=v_deps + sq_toks)
                        o_toks.append(m3)
                        z1 = p.op("pe", lambda t, h=h, j=j, pb=pb, tp=tp: t.matmul(
                            PSA[pb:pb + 1, 1024 + j * 128:1024 + j * 128 + 128],
                            S_bf[0][:, h * 65 + 64:h * 65 + 65],
                            phiqfm[:, (2 * h) * 128:(2 * h) * 128 + 128],
                            start=True, stop=False, tile_position=tp), deps=[])
                        z2 = p.op("pe", lambda t, h=h, j=j, pb=pb, tp=tp: t.matmul(
                            PSA[pb:pb + 1, 1024 + j * 128:1024 + j * 128 + 128],
                            S_bf[1][:, h * 65 + 64:h * 65 + 65],
                            phiqfm[:, (2 * h + 1) * 128:(2 * h + 1) * 128 + 128],
                            start=False, stop=False, tile_position=tp), deps=[])
                        z3 = p.op("pe", lambda t, h=h, j=j, pb=pb, tp=tp: t.matmul(
                            PSA[pb:pb + 1, 1024 + j * 128:1024 + j * 128 + 128],
                            v_aug[:, h * 65 + 64:h * 65 + 65],
                            s_sb[:, h * 128:h * 128 + 128],
                            start=False, stop=True, tile_position=tp), deps=[])
                        z_toks.append(z3)

                    tz1 = p.op("act", lambda a: a.copy(
                        zrow[0:1, 0:1024], PSA[0:1, 1024:2048]), deps=z_toks)
                    tz2 = p.op("act", lambda a: a.copy(
                        zrow[64:65, 0:1024], PSA[64:65, 1024:2048]),
                        deps=z_toks)
                    zr = zrow.ap()
                    zpitch = zr.ap[0][0]
                    zsrc = bass.AP(tensor=zr.tensor, offset=zr.offset,
                                   ap=[[64 * zpitch, 2], [128, 8], [1, 128]])
                    tz3 = p.dma("spq",
                                zasm[:].rearrange("p (j t) -> p j t", t=128),
                                zsrc, deps=[tz1, tz2])
                    tz4 = p.op("dve", lambda v: v.tensor_scalar_add(
                        zasm[:], zasm[:], 1e-5), deps=[tz3])
                    tz5 = p.op("dve", lambda v: v.reciprocal(
                        zasm[:], zasm[:]), deps=[tz4])
                    tz6 = p.op("dve", lambda v: v.tensor_copy(
                        zinv[:], zasm[:]), deps=[tz5])
                    zb_mms = [p.op("pe", lambda t, j=j: t.matmul(
                        PSB[:, j * 128:j * 128 + 128], sel2[:],
                        zinv[:, j * 128:j * 128 + 128],
                        start=True, stop=True),
                        deps=[tz6] + sel2_deps + eng(("act",)))
                        for j in range(8)]
                    tz7 = p.op("act", lambda a: a.copy(
                        zb_sb[:], PSB[:, 0:1024]), deps=zb_mms)
                    tev2 = p.op("dve", lambda v, cc=cc: v.tensor_tensor(
                        attn_all[:].rearrange("p (j t) -> p j t", t=512)[:, :, cc * 128:cc * 128 + 128],
                        PSA[:, 0:1024].rearrange("p (j t) -> p j t", t=128),
                        zb_sb[:].rearrange("p (j t) -> p j t", t=128),
                        ALU.mult), deps=[tz7] + o_toks)
                    attn_toks[cc] = tev2

                upd_deps = [tphik] + v_deps + \
                    ([tev2, tz1, tz2, tz7] if own else []) + eng(("dve",))
                st_mms = []
                for h in range(NH):
                    for m in range(2):
                        tgt = PSA if m == 0 else PSB
                        mm = p.op("pe", lambda t, h=h, m=m, tgt=tgt: t.matmul(
                            tgt[:, h * 128:h * 128 + 65],
                            phik[:, h * 256 + m * 128:h * 256 + m * 128 + 128],
                            v_aug[:, h * 65:h * 65 + 65],
                            start=True, stop=True), deps=upd_deps)
                        st_mms.append(mm)
                up_toks = []
                for m in range(2):
                    tgt = PSA if m == 0 else PSB
                    t1 = p.op("dve", lambda v, m=m, tgt=tgt: v.tensor_add(
                        S_m[m][:].rearrange("p (h x) -> p h x", x=65),
                        S_m[m][:].rearrange("p (h x) -> p h x", x=65),
                        tgt[:].rearrange("p (h x) -> p h x", x=128)[:, :, 0:65]),
                        deps=st_mms + s_up)
                    t2 = p.op("dve", lambda v, m=m: v.tensor_copy(
                        S_bf[m][:], S_m[m][:]),
                        deps=[t1] + (o_toks + z_toks if own else []))
                    up_toks += [t1, t2]
                s_up = up_toks

            if own:
                for mch in range(8):
                    for j in range(8):
                        p.op("pe", lambda t, mch=mch, j=j: t.matmul(
                            PSB[:, 1024:1536],
                            wo_sb[:, j * D + mch * 128:j * D + mch * 128 + 128],
                            attn_all[:, j * 512:j * 512 + 512],
                            start=(j == 0), stop=(j == 7)),
                            deps=[lw[3]] + [attn_toks[cc] for cc in range(4)] +
                            eng(("act", "dve")))
                    tmch = ("pe", p.count["pe"])
                    tev = p.op("act", lambda a: a.copy(a0ev[:], PSB[:, 1024:1536]),
                               deps=[tmch] + a0_store_prev)
                    sot = p.dma("spq",
                                io["a0_fm"][mch * 128:(mch + 1) * 128,
                                       t0 - TPRE:t0 - TPRE + 512],
                                a0ev[:], deps=[tev])
                    a0_store_prev = [sot]
            prev_tile = p.barrier_token()

        return p.barrier_token()


def build_l0attn(TPRE, T):
    nc = bass.Bass("TRN2", target_bir_lowering=False, debug=False)
    TTOT = TPRE + T
    io = {
        "xn_bf": nc.dram_tensor("xn_bf", [D, TTOT], BF16, kind="ExternalInput").ap(),
        "wq_t": nc.dram_tensor("wq_t", [D, 256], F32, kind="ExternalInput").ap(),
        "wk_t": nc.dram_tensor("wk_t", [D, 256], F32, kind="ExternalInput").ap(),
        "wv_t": nc.dram_tensor("wv_t", [D, D], F32, kind="ExternalInput").ap(),
        "wo_t": nc.dram_tensor("wo_t", [D, D], F32, kind="ExternalInput").ap(),
        "gam16": nc.dram_tensor("gam16", [128, 16], F32, kind="ExternalInput").ap(),
        "bet16": nc.dram_tensor("bet16", [128, 16], F32, kind="ExternalInput").ap(),
        "a0_fm": nc.dram_tensor("a0_fm", [D, T], F32, kind="ExternalOutput").ap(),
    }
    es = ExitStack()
    with es:
        p = Prog(nc, es)
        idb, idb_deps = emit_identity(p, es, nc, "identb0", BF16)
        shared = {"identb": idb, "identb_deps": idb_deps}
        emit_l0attn(p, es, nc, alloc, alloc_ps, io, shared, TPRE, T)
        block = es.enter_context(nc.Block())
        finish(p, block, p.barrier_token())
        p.build(block)
    return nc


def emit_rot64(p, es, nc, tag):
    P = alloc(es, nc, f"{tag}_P64", [64, 64], BF16)
    t0 = p.op("dve", lambda v: v.memset(P[:], 0.0))
    t1 = p.op("pool", lambda g: g.affine_select(
        out=P[:], in_=P[:], pattern=[[1, 64]], compare_op=ALU.not_equal,
        fill=1.0, base=-32, channel_multiplier=-1), deps=[t0])
    t2 = p.op("pool", lambda g: g.affine_select(
        out=P[:], in_=P[:], pattern=[[-1, 64]], compare_op=ALU.not_equal,
        fill=-1.0, base=-32, channel_multiplier=1), deps=[t1])
    return P, [t2]


def emit_planes_window(p, sc, pos_sb, pos_deps, base, pl_out, tag):
    tpos, fr, cs, sn, tmp, ifq, lnb, scl, isc = (
        sc["tpos"], sc["fr"], sc["cs"], sc["sn"], sc["tmp"], sc["ifq"],
        sc["lnb"], sc["scl"], sc["isc"])
    TWO_PI = float(2.0 * np.pi)
    deps0 = list(pos_deps) + sc["prev"]
    t5 = p.op("pool", lambda g, base=base: g.iota(
        tpos[:], pattern=[[1, 512]], base=base, channel_multiplier=0,
        allow_small_or_imprecise_dtypes=True), deps=deps0)
    t6 = p.op("dve", lambda v: v.tensor_scalar_add(
        tpos[:], tpos[:], pos_sb[0:128, 0:1]), deps=[t5] + deps0)
    t7 = p.op("dve", lambda v: v.tensor_scalar_mul(
        fr[:], tpos[:], ifq[:, 0:1]), deps=[t6] + sc["const_deps"])
    MAGIC = 12582912.0

    def red_sin(dst, src_expr_tok):
        ta = p.op("dve", lambda v: v.tensor_single_scalar(
            tmp[:], dst[:], 1.0 / TWO_PI, ALU.mult), deps=[src_expr_tok])
        tb = p.op("dve", lambda v: v.tensor_single_scalar(
            tmp[:], tmp[:], MAGIC, ALU.add), deps=[ta])
        tc = p.op("dve", lambda v: v.tensor_single_scalar(
            tmp[:], tmp[:], MAGIC, ALU.subtract), deps=[tb])
        td = p.op("dve", lambda v: v.scalar_tensor_tensor(
            dst[:], tmp[:], -TWO_PI, dst[:], ALU.mult, ALU.add), deps=[tc])
        te = p.op("act", lambda a: a.activation(dst[:], dst[:], AF.Sin),
                  deps=[td])
        return te

    t7b = p.op("dve", lambda v: v.tensor_copy(sn[:], fr[:]), deps=[t7])
    t11 = red_sin(sn, t7b)
    t12 = p.op("dve", lambda v: v.tensor_scalar(
        cs[:], fr[:], -1.0, float(np.pi / 2.0), ALU.mult, ALU.add),
        deps=[t7, t11])
    t16 = red_sin(cs, t12)
    t17 = p.op("dve", lambda v: v.tensor_scalar(
        tmp[:], tpos[:], 1.0 / 256.0, -4096.0 / 256.0, ALU.mult, ALU.add),
        deps=[t6, t16])
    t18 = p.op("dve", lambda v: v.tensor_scalar_mul(
        scl[:], tmp[:], lnb[:, 0:1]), deps=[t17] + sc["const_deps"])
    t19 = p.op("act", lambda a: a.activation(isc[:], scl[:], AF.Exp,
                                             scale=-1.0), deps=[t18])
    t20 = p.op("act", lambda a: a.activation(scl[:], scl[:], AF.Exp),
               deps=[t19])
    o1 = p.op("dve", lambda v: v.tensor_mul(pl_out[0], cs[:], scl[:]),
              deps=[t16, t20])
    o2 = p.op("dve", lambda v: v.tensor_mul(pl_out[1], sn[:], scl[:]),
              deps=[t11, t20])
    o3 = p.op("dve", lambda v: v.tensor_mul(pl_out[2], cs[:], isc[:]),
              deps=[t16, t19])
    o4 = p.op("dve", lambda v: v.tensor_mul(pl_out[3], sn[:], isc[:]),
              deps=[t11, t19])
    sc["prev"] = [o1, o2, o3, o4]
    return [o1, o2, o3, o4]


def emit_l0ffn(p, es, nc, A, APS, io, shared, TIN1, TOUT1, TOUT2, prev=(), bnd_windows=(None, None)):
    TIN2 = TOUT1
    eng = p.engines_token
    if True:
        POOL1 = A(es, nc, "POOL1", [128, 8 * 2 * FFN_H], BF16)
        POOL2 = A(es, nc, "POOL2", [128, FFN_HC * D], BF16)
        P1F = POOL1[:].bitcast(F32)
        _f = lambda a, b: P1F[0:128, a:b]
        qraw = [_f(0, 512), _f(3072, 3584)]
        nrm = [_f(1024, 1536), _f(4096, 4608)]
        qn_f = [_f(1536, 2048), _f(4608, 5120)]
        rt1 = [_f(2048, 2560), _f(5120, 5632)]
        rt2 = [_f(2560, 3072), _f(5632, 6144)]
        psc_f32 = {k: _f(6144 + i * 512, 6656 + i * 512) for i, k in
                   enumerate(("tpos", "fr", "cs", "sn", "tmp", "scl", "isc"))}
        maskP = [_f(9728 + qt * 512, 10240 + qt * 512) for qt in range(4)]
        maskC = _f(11776, 11904)
        _b = lambda a, b: POOL1[0:128, a:b]
        qf_all = _b(23808, 25856)
        kr_all = _b(25856, 29952)
        vr = _b(29952, 34048)
        p_sb = [_b(34048, 35072), _b(35072, 36096)]
        pt_sb = [_b(36096, 37120), _b(37120, 38144)]
        attn_w = _b(38144, 40192)
        qn_b = [_b(40192, 40704), _b(40704, 41216)]
        sqb = [_b(41216, 41728), _b(41728, 42240)]
        PLN = _b(42240, 44288)
        xw = [A(es, nc, f"xw{c}", [128, 513], F32) for c in range(8)]
        hw = [POOL2[0:128, 12288 + c * 512:12288 + (c + 1) * 512]
              for c in range(8)]
        sum128 = A(es, nc, "f_sum128", [128, 128], BF16)
        P128 = A(es, nc, "f_P128", [128, 128], BF16)
        mx_c = [A(es, nc, f"f_mx{i}", [128, 1], F32) for i in range(2)]
        nm_c = [A(es, nc, f"f_nm{i}", [128, 1], F32) for i in range(2)]
        l_c = [A(es, nc, f"f_l{i}", [128, 1], F32) for i in range(2)]
        li_c = [A(es, nc, f"f_li{i}", [128, 1], F32) for i in range(2)]
        xnew = [A(es, nc, f"f_xnew{i}", [128, 512], F32) for i in range(2)]
        he_sb = A(es, nc, "f_he", [128, FFN_HC * 512], BF16)
        gelu_sb = [A(es, nc, f"f_gelu{i}", [128, 512], F32) for i in range(2)]
        hn_bf = [A(es, nc, f"f_hn{c}", [128, 512], BF16) for c in range(8)]
        ln_sq = A(es, nc, "f_lnsq", [128, 512], F32)
        mu_sb = A(es, nc, "f_musb", [1, 512], F32)
        mu2_sb = A(es, nc, "f_mu2sb", [1, 512], F32)
        var_sb = A(es, nc, "f_varsb", [1, 512], F32)
        sd_sb = A(es, nc, "f_sdsb", [1, 512], F32)
        a_sb = A(es, nc, "f_asb", [1, 512], F32)
        b_sb = A(es, nc, "f_bsb", [1, 512], F32)
        tmpc = ln_sq
        qs_sb = A(es, nc, "f_qssb", [128, 2], F32)
        ks_sb = A(es, nc, "f_kssb", [128, 2], F32)
        lg_sb = A(es, nc, "f_lgsb", [128, 16], F32)
        lb_sb = A(es, nc, "f_lbsb", [128, 16], F32)
        b1_sb = A(es, nc, "f_b1sb", [128, 4 * FFN_HC], F32)
        b2_sb = A(es, nc, "f_b2sb", [128, 16], F32)
        pos_sb = A(es, nc, "f_possb", [128, 1], F32)
        zc_sb = A(es, nc, "f_zc", [128, 8], F32)
        psc = dict(psc_f32)
        psc["ifq"] = A(es, nc, "f_psc_ifq", [128, 1], F32)
        psc["lnb"] = A(es, nc, "f_psc_lnb", [128, 1], F32)
        psc["prev"] = []

        PSA = APS(es, nc, "fPSA", [128, 2048], F32)
        PSB = APS(es, nc, "fPSB", [128, 2048], F32)
        PSB_bf = PSB[:].bitcast(BF16)

        bnd_sb = A(es, nc, "f_bndsb", [128, 2], F32)
        if "f_bnd" in io:
            lbnd = p.dma("spq", bnd_sb[:], io["f_bnd"])
        lds = [p.dma("spq", qs_sb[:], io["f_qs"]), p.dma("spq", ks_sb[:], io["f_ks"]),
               p.dma("spq", lg_sb[:], io["f_lg"]), p.dma("spq", lb_sb[:], io["f_lb"]),
               p.dma("spq", b1_sb[:], io["f_b1"]), p.dma("spq", b2_sb[:], io["f_b2"]),
               p.dma("spq", pos_sb[:], io["f_pos"])]
        consts = make_const_tiles(p, es, nc)
        identb2, identb2_deps = shared['identb'], shared['identb_deps']
        eps24_t, eps24_tok = p.constf(1e-24)
        c8_t, c8_tok = p.constf(8.0)

        ts0 = p.op("dve", lambda v: v.memset(sum128[:], 0.0))
        ts1 = p.op("dve", lambda v: v.memset(sum128[0:64, 0:64], 1.0),
                   deps=[ts0])
        ts2 = p.op("dve", lambda v: v.memset(sum128[64:128, 64:128], 1.0),
                   deps=[ts1])
        sum_deps = [ts2]
        tr0 = p.op("dve", lambda v: v.memset(P128[:], 0.0))
        tr1 = p.op("pool", lambda g: g.affine_select(
            out=P128[:], in_=P128[:], pattern=[[1, 128]],
            compare_op=ALU.not_equal, fill=1.0, base=-32,
            channel_multiplier=-1), deps=[tr0])
        tr2 = p.op("pool", lambda g: g.affine_select(
            out=P128[:], in_=P128[:], pattern=[[-1, 128]],
            compare_op=ALU.not_equal, fill=-1.0, base=-32,
            channel_multiplier=1), deps=[tr1])
        tr3 = p.op("dve", lambda v: v.memset(P128[32:64, 64:96], 0.0),
                   deps=[tr2])
        tr4 = p.op("dve", lambda v: v.memset(P128[64:96, 32:64], 0.0),
                   deps=[tr3])
        P128_deps = [tr4]

        tpc0 = p.op("pool", lambda g: g.iota(
            psc["ifq"][:], pattern=[[0, 1]], base=0, channel_multiplier=1,
            allow_small_or_imprecise_dtypes=True))
        tpcA = p.op("dve", lambda v: v.tensor_single_scalar(
            psc["lnb"][:], psc["ifq"][:], 63.5, ALU.is_gt), deps=[tpc0])
        tpcB = p.op("dve", lambda v: v.scalar_tensor_tensor(
            psc["ifq"][:], psc["lnb"][:], -64.0, psc["ifq"][:],
            ALU.mult, ALU.add), deps=[tpcA])
        tpc1a = p.op("dve", lambda v: v.tensor_single_scalar(
            psc["lnb"][:], psc["ifq"][:], 31.5, ALU.is_gt), deps=[tpcB])
        tpc1 = p.op("dve", lambda v: v.scalar_tensor_tensor(
            psc["ifq"][:], psc["lnb"][:], -32.0, psc["ifq"][:],
            ALU.mult, ALU.add), deps=[tpc1a])
        tpc2 = p.op("dve", lambda v: v.tensor_scalar(
            psc["lnb"][:], psc["ifq"][:], 2.0 / 89.6, 25.6 / 89.6,
            ALU.mult, ALU.add), deps=[tpc1])
        tpc3 = p.op("act", lambda a: a.activation(
            psc["lnb"][:], psc["lnb"][:], AF.Ln), deps=[tpc2])
        tpc4 = p.op("act", lambda a: a.activation(
            psc["ifq"][:], psc["ifq"][:], AF.Exp,
            scale=float(-2.0 / 64.0 * np.log(10000.0))), deps=[tpc1, tpc3])
        psc["const_deps"] = [tpc3, tpc4]

        def build_masks(prev):
            deps = []
            for qt in range(4):
                mt = maskP[qt]
                t0_ = p.op("dve", lambda v, mt=mt: v.memset(mt[:], 0.0),
                           deps=prev)
                t1_ = p.op("pool", lambda g, mt=mt, qt=qt: g.affine_select(
                    out=mt[:], in_=mt[:], pattern=[[1, 512]],
                    compare_op=ALU.is_ge, fill=-1e30, base=-128 * qt,
                    channel_multiplier=-1), deps=[t0_])
                deps.append(t1_)
            t0_ = p.op("dve", lambda v: v.memset(maskC[:], 0.0), deps=prev)
            tmc = p.op("pool", lambda g: g.affine_select(
                out=maskC[:], in_=maskC[:], pattern=[[-1, 128]],
                compare_op=ALU.is_ge, fill=-1e30, base=0,
                channel_multiplier=1), deps=[t0_])
            return deps + [tmc]

        tzc = p.op("dve", lambda v: v.memset(zc_sb[:], 0.0))
        ez1 = p.dma("spq", io["SCR_a"][:, 0:1].rearrange("(c p) o -> p c o", p=128),
                    zc_sb[:, 0:8].unsqueeze(2), deps=[tzc],
                    allow_slow_non_contiguous=True)
        ez2 = p.dma("spq", io["SCR_y1"][:, 0:1].rearrange("(c p) o -> p c o", p=128),
                    zc_sb[:, 0:8].unsqueeze(2), deps=[tzc],
                    allow_slow_non_contiguous=True)

        def run_layer(lidx, xin_ap, xin_T, yout_ap, yout_T, loc0, prev_all,
                      final_store):
            NW = xin_T // 512
            W_B0 = (xin_T - yout_T) // 512
            lqkv = p.dma("poolq",
                         POOL2[:, 0:8 * 1536].rearrange("p (c m) -> p c m", c=8),
                         io["f_wqkv"][lidx].rearrange("(c p) m -> p c m", p=128),
                         deps=prev_all)
            lwo = p.dma("poolq",
                        POOL2[:, 16384:16384 + 4 * D].rearrange(
                            "p (c m) -> p c m", c=4),
                        io["f_wo"][lidx].rearrange("(c p) m -> p c m", p=128),
                        deps=prev_all)
            mask_deps = build_masks(prev_all)
            prev_w = list(prev_all) + mask_deps
            pchain = [None, None]
            for w in range(NW):
                has_prev = (w >= 1)
                do_attn = (w >= 1) or (loc0 == 0)
                ldx = [p.dma("spq", xw[c][:],
                             xin_ap[c * 128:(c + 1) * 128,
                                    512 * w:512 * w + 513],
                             deps=prev_w) for c in range(8)]
                hcast = [p.op("dve", lambda v, c=c, sl=(slice(1, 513) if c < 4
                                                        else slice(0, 512)):
                              v.tensor_copy(hw[c][:], xw[c][:, sl]),
                              deps=[ldx[c]]) for c in range(8)]
                wpl = [PLN[0:128, i * 512:(i + 1) * 512] for i in range(4)]
                wpl_deps = emit_planes_window(
                    p, psc, pos_sb, [lds[6]] + list(prev_w), loc0 + 512 * w,
                    wpl, f"pl{lidx}_{w}")
                v_done = []
                tev = None
                for sub in range(4):
                    slot = (4 * w + sub) % 8
                    for c in range(8):
                        p.op("pe", lambda t, c=c, sub=sub: t.matmul(
                            PSB[:, 0:512], hw[c][:, sub * 128:sub * 128 + 128],
                            POOL2[:, c * 1536 + 1024:c * 1536 + 1536],
                            start=(c == 0), stop=(c == 7)),
                            deps=[lqkv] + hcast + ([tev] if tev else list(prev_w)))
                    tv = ("pe", p.count["pe"])
                    tev = p.op("act", lambda a, slot=slot: a.copy(
                        vr[:, slot * 512:slot * 512 + 512], PSB[:, 0:512]),
                        deps=[tv])
                    v_done.append(tev)
                ci = 0
                qk_done = []
                for pr in range(4):
                    for which in ("q", "k"):
                        if which == "q" and not do_attn:
                            continue
                        pi = ci % 2
                        ci += 1
                        pc = pchain[pi]
                        qc0 = (0 if which == "q" else 512) + pr * 128
                        scol = (qs_sb if which == "q" else ks_sb)[:, lidx:lidx + 1]
                        pcs = wpl[0] if which == "q" else wpl[2]
                        pss = wpl[1] if which == "q" else wpl[3]
                        if which == "q":
                            dst, dcol = qf_all, pr * 512
                        else:
                            dst, dcol = kr_all, pr * 1024 + (w % 2) * 512
                        PROJ = PSA[:, pi * 512:pi * 512 + 512]
                        MM = PSA[:, 1024 + pi * 512:1536 + pi * 512]
                        prj_dep = [lqkv] + hcast + \
                            ([pc["te1"]] if pc else list(prev_w))
                        for c in range(8):
                            p.op("pe", lambda t, c=c, qc0=qc0, PROJ=PROJ: t.matmul(
                                PROJ,
                                POOL2[:, c * 1536 + qc0:c * 1536 + qc0 + 128],
                                hw[c][:], start=(c == 0), stop=(c == 7)),
                                deps=prj_dep)
                        tpj = ("pe", p.count["pe"])
                        te1 = p.op("act", lambda a, pi=pi, PROJ=PROJ: a.copy(
                            qraw[pi][:], PROJ),
                            deps=[tpj] + ([pc["te7"]] if pc else []))
                        te2 = p.op("act", lambda a, pi=pi: a.activation(
                            sqb[pi][:], qraw[pi][:], AF.Square),
                            deps=[te1] + ([pc["mm1"]] if pc else []))
                        mm1 = p.op("pe", lambda t, pi=pi, MM=MM: t.matmul(
                            MM, sum128[:], sqb[pi][:], start=True, stop=True),
                            deps=[te2] + sum_deps +
                            ([pc["te10"]] if pc else []))
                        te4 = p.op("act", lambda a, pi=pi, MM=MM: a.activation(
                            nrm[pi][:], MM, AF.Sqrt), deps=[mm1])
                        te5 = p.op("dve", lambda v, pi=pi: v.tensor_scalar_max(
                            nrm[pi][:], nrm[pi][:], 1e-12), deps=[te4])
                        te6 = p.op("dve", lambda v, pi=pi: v.reciprocal(
                            nrm[pi][:], nrm[pi][:]), deps=[te5])
                        te7 = p.op("dve", lambda v, pi=pi, scol=scol:
                                   v.scalar_tensor_tensor(
                            qn_f[pi][:], qraw[pi][:], scol, nrm[pi][:],
                            ALU.mult, ALU.mult), deps=[te6] + lds)
                        te8 = p.op("dve", lambda v, pi=pi: v.tensor_copy(
                            qn_b[pi][:], qn_f[pi][:]),
                            deps=[te7] + ([pc["mm2"]] if pc else []))
                        mm2 = p.op("pe", lambda t, pi=pi, MM=MM: t.matmul(
                            MM, P128[:], qn_b[pi][:], start=True, stop=True),
                            deps=[te8, te4] + P128_deps)
                        te9 = p.op("dve", lambda v, pi=pi, pcs=pcs: v.tensor_mul(
                            rt1[pi][:], qn_f[pi][:], pcs),
                            deps=[te7] + wpl_deps +
                            ([pc["te12"]] if pc else []))
                        te10 = p.op("dve", lambda v, pi=pi, pss=pss, MM=MM:
                                    v.tensor_mul(rt2[pi][:], MM, pss),
                                    deps=[mm2] + wpl_deps +
                                    ([pc["te12"]] if pc else []))
                        te12 = p.op("dve", lambda v, pi=pi, dst=dst, dcol=dcol:
                                    v.tensor_add(
                            dst[0:128, dcol:dcol + 512], rt1[pi][:],
                            rt2[pi][:]), deps=[te9, te10])
                        pchain[pi] = dict(te1=te1, te7=te7, mm1=mm1,
                                          te10=te10, mm2=mm2, te12=te12)
                        qk_done.append(te12)
                attn_evac = []
                if do_attn:
                    sc_free = [[], []]
                    p_free = [[], []]
                    pt_free = [[], []]
                    si = 0
                    tevo_prev = []
                    o_grp_by_qt = {qt: [] for qt in range(4)}
                    pending = None

                    def flush_pending():
                        nonlocal pending, tevo_prev
                        if pending is None:
                            return
                        (qt, h, pi, span, nsb, kb0, t_ps) = pending
                        pending = None
                        PT = PSB_bf[:, 2048 + pi * 1024:
                                    2048 + pi * 1024 + 1024]
                        trs = [p.op("pe", lambda t, j=j, PT=PT, pi=pi:
                                    t.transpose(
                            PT[:, j * 128:j * 128 + 128],
                            p_sb[pi][:, j * 128:j * 128 + 128],
                            identb2[:]),
                            deps=[t_ps] + identb2_deps + pt_free[pi])
                            for j in range(nsb)]
                        t_pt = p.op("act", lambda a, nsb=nsb, PT=PT,
                                    pi=pi: a.copy(
                            pt_sb[pi][:, 0:nsb * 128], PT[:, 0:nsb * 128]),
                            deps=trs)
                        pt_free[pi] = [t_pt]
                        p_free[pi] = trs
                        o_grp = o_grp_by_qt[qt]
                        for j in range(nsb):
                            slot = (kb0 + j) % 8
                            p.op("pe", lambda t, j=j, slot=slot, h=h,
                                 nsb=nsb, pi=pi: t.matmul(
                                PSB[64 * (h % 2):64 * (h % 2) + 64,
                                    512 + (h // 2) * 128:
                                    512 + (h // 2) * 128 + 128],
                                vr[:, slot * 512 + h * 64:
                                   slot * 512 + h * 64 + 64],
                                pt_sb[pi][:, j * 128:j * 128 + 128],
                                start=(j == 0), stop=(j == nsb - 1),
                                tile_position=(0, 64 * (h % 2))),
                                deps=[t_pt] + v_done + o_grp + tevo_prev)
                        o_grp.append(("pe", p.count["pe"]))
                        if len(o_grp) == 8:
                            tevo = p.op("act", lambda a, qt=qt: a.copy(
                                attn_w[:].rearrange("p (j t) -> p j t", t=512)
                                [:, :, qt * 128:qt * 128 + 128],
                                PSB[:, 512:1024].rearrange(
                                    "p (j t) -> p j t", t=128)),
                                deps=o_grp)
                            tevo_prev = [tevo]
                            attn_evac.append(tevo)

                    for qt in range(4):
                        cur = (qt + 1) * 128
                        for h in range(8):
                            pi = si % 2
                            si += 1
                            pr, ro = h // 2, 64 * (h % 2)
                            SC = PSA[:, pi * 1024:pi * 1024 + 1024]
                            span = (512 + cur) if has_prev else cur
                            qsl = qf_all[ro:ro + 64,
                                         pr * 512 + qt * 128:
                                         pr * 512 + qt * 128 + 128]
                            smm_dep = qk_done + sc_free[pi]
                            if has_prev:
                                kprev = kr_all[ro:ro + 64,
                                               pr * 1024 + ((w - 1) % 2) * 512:
                                               pr * 1024 + ((w - 1) % 2) * 512 + 512]
                                p.op("pe", lambda t, SC=SC, qsl=qsl,
                                     kprev=kprev: t.matmul(
                                    SC[:, 0:512], qsl, kprev,
                                    start=True, stop=True), deps=smm_dep)
                            cb = 512 if has_prev else 0
                            kcur = kr_all[ro:ro + 64,
                                          pr * 1024 + (w % 2) * 512:
                                          pr * 1024 + (w % 2) * 512 + cur]
                            p.op("pe", lambda t, SC=SC, qsl=qsl, kcur=kcur,
                                 cb=cb, cur=cur: t.matmul(
                                SC[:, cb:cb + cur], qsl, kcur,
                                start=True, stop=True), deps=smm_dep)
                            tsm = ("pe", p.count["pe"])
                            mk = []
                            if has_prev:
                                mk.append(p.op("dve", lambda v, SC=SC, qt=qt:
                                               v.tensor_add(
                                    SC[:, 0:512], SC[:, 0:512], maskP[qt][:]),
                                    deps=[tsm] + mask_deps))
                                if "f_bnd" in io and w == bnd_w[lidx]:
                                    mk.append(p.op("dve", lambda v, SC=SC,
                                                   lidx=lidx: v.tensor_scalar_add(
                                        SC[:, 0:512], SC[:, 0:512],
                                        bnd_sb[:, lidx:lidx + 1]),
                                        deps=mk + [lbnd]))
                            mk.append(p.op("dve", lambda v, SC=SC, span=span:
                                           v.tensor_add(
                                SC[:, span - 128:span],
                                SC[:, span - 128:span], maskC[:]),
                                deps=[tsm] + mask_deps + mk))
                            t_mx = p.op("dve", lambda v, SC=SC, span=span,
                                        pi=pi: v.reduce_max(
                                mx_c[pi][:], SC[:, 0:span], axis=AX.X),
                                deps=mk + [tsm])
                            t_nm = p.op("dve", lambda v, pi=pi: v.tensor_scalar_mul(
                                nm_c[pi][:], mx_c[pi][:], -8.0), deps=[t_mx])
                            t_ex = p.op("act", lambda a, SC=SC, span=span,
                                        pi=pi: a.activation(
                                p_sb[pi][:, 0:span], SC[:, 0:span], AF.Exp,
                                bias=nm_c[pi][:, 0:1], scale=c8_t[:, 0:1],
                                accum_out=l_c[pi][:, 0:1]),
                                deps=[t_nm, c8_tok] + p_free[pi])
                            sc_free[pi] = [t_ex]
                            t_li = p.op("dve", lambda v, pi=pi: v.reciprocal(
                                li_c[pi][:], l_c[pi][:]), deps=[t_ex])
                            t_ps = p.op("dve", lambda v, pi=pi, span=span:
                                        v.tensor_scalar_mul(
                                p_sb[pi][:, 0:span], p_sb[pi][:, 0:span],
                                li_c[pi][:, 0:1]), deps=[t_li])
                            nsb = span // 128
                            kb0 = 4 * (w - 1) if has_prev else 4 * w
                            flush_pending()
                            pending = (qt, h, pi, span, nsb, kb0, t_ps)
                    flush_pending()
                    if w >= W_B0 - 1:
                        st_prev = [[], []]
                        txn_prev = [[], []]
                        for mch in range(8):
                            mpi = mch % 2
                            WO = PSA[:, mpi * 512:mpi * 512 + 512]
                            for j in range(4):
                                p.op("pe", lambda t, mch=mch, j=j, WO=WO:
                                     t.matmul(
                                    WO, POOL2[:, 16384 + j * D + mch * 128:
                                              16384 + j * D + mch * 128 + 128],
                                    attn_w[:, j * 512:j * 512 + 512],
                                    start=(j == 0), stop=(j == 3)),
                                    deps=[lwo] + attn_evac + txn_prev[mpi])
                            twoe = ("pe", p.count["pe"])
                            txn = p.op("dve", lambda v, mch=mch, mpi=mpi,
                                       WO=WO: v.tensor_add(
                                xnew[mpi][:], WO, xw[mch][:, 1:513]),
                                deps=[twoe] + st_prev[mpi])
                            txn_prev[mpi] = [txn]
                            sot = p.dma("spq",
                                        io["SCR_a"][mch * 128:(mch + 1) * 128,
                                              1 + 512 * w:1 + 512 * w + 512],
                                        xnew[mpi][:], deps=[txn])
                            st_prev[mpi] = [sot]
                prev_w = p.barrier_token()
            lw1a, lw1g = [], []
            for j in range(FFN_HC):
                lw1a.append(p.dma("poolq",
                    POOL1[:].rearrange("p (c m) -> p c m", c=8)
                    [:, :, j * 128:j * 128 + 128],
                    io["f_w1"][lidx].rearrange("(c p) m -> p c m", p=128)
                    [:, :, j * 128:j * 128 + 128], deps=prev_w))
                lw1g.append(p.dma("poolq",
                    POOL1[:].rearrange("p (c m) -> p c m", c=8)
                    [:, :, FFN_H + j * 128:FFN_H + j * 128 + 128],
                    io["f_w1"][lidx].rearrange("(c p) m -> p c m", p=128)
                    [:, :, FFN_H + j * 128:FFN_H + j * 128 + 128],
                    deps=prev_w))
            lw2c = [p.dma("poolq",
                    POOL2[:].rearrange("p (c m) -> p c m", c=FFN_HC)[:, j, :],
                    io["f_w2"][lidx].rearrange("(c p) m -> p c m", p=128)[:, j, :],
                    deps=prev_w) for j in range(FFN_HC)]
            y_stores = []
            prev_b = list(prev_w)
            oo = consts["oo1024"]
            ones_row = consts["ones_f32_row"]
            for ob in range(yout_T // 512):
                wb = W_B0 + ob
                ldx = [p.dma("spq", xw[c][:],
                             io["SCR_a"][c * 128:(c + 1) * 128,
                                   512 * wb:512 * wb + 513],
                             deps=prev_b) for c in range(8)]
                shs = [slice(1, 513) if c < 4 else slice(0, 512)
                       for c in range(8)]
                sq_toks = []
                for c in range(8):
                    ts_ = p.op("act", lambda a, c=c, sl=shs[c]: a.activation(
                        ln_sq[:], xw[c][:, sl], AF.Square),
                        deps=[ldx[c]] + eng(("pe", "dve")))
                    tm_ = p.op("pe", lambda t, c=c: t.matmul(
                        PSA[0:1, 1536:2048], oo[:], ln_sq[:],
                        start=(c == 0), stop=(c == 7)),
                        deps=[ts_] + consts["deps"])
                    sq_toks.append(ts_)
                t_ms = ("pe", p.count["pe"])
                for c in range(8):
                    p.op("pe", lambda t, c=c, sl=shs[c]: t.matmul(
                        PSA[0:1, 1024:1536], oo[:], xw[c][:, sl],
                        start=(c == 0), stop=(c == 7)),
                        deps=[ldx[c]] + consts["deps"] + eng(("act", "dve")))
                t_mu = ("pe", p.count["pe"])
                t1 = p.op("act", lambda a: a.copy(mu_sb[:], PSA[0:1, 1024:1536]),
                          deps=[t_mu])
                t2 = p.op("act", lambda a: a.activation(
                    mu2_sb[:], mu_sb[:], AF.Square), deps=[t1])
                t3 = p.op("dve", lambda v: v.tensor_sub(
                    var_sb[:], PSA[0:1, 1536:2048], mu2_sb[:]),
                    deps=[t_ms, t2])
                eps_t, eps_tok = p.constf(1e-5)
                t4 = p.op("act", lambda a, eps_t=eps_t: a.activation(
                    sd_sb[:], var_sb[:], AF.Sqrt, bias=eps_t[0:1, 0:1]),
                    deps=[t3, eps_tok])
                t5 = p.op("dve", lambda v: v.reciprocal(a_sb[:], sd_sb[:]),
                          deps=[t4])
                t6 = p.op("dve", lambda v: v.scalar_tensor_tensor(
                    b_sb[:], mu_sb[:], -1.0, a_sb[:], ALU.mult, ALU.mult),
                    deps=[t5, t1])
                t7 = p.op("pe", lambda t: t.matmul(
                    PSA[:, 0:512], ones_row[:], a_sb[:], start=True, stop=True),
                    deps=[t5] + consts["deps"] + eng(("act", "dve")))
                t8 = p.op("pe", lambda t: t.matmul(
                    PSA[:, 512:1024], ones_row[:], b_sb[:], start=True,
                    stop=True), deps=[t6])
                tab = ("pe", p.count["pe"])
                hn_toks = []
                for c in range(8):
                    u1 = p.op("dve", lambda v, c=c, sl=shs[c]: v.tensor_mul(
                        tmpc[:], xw[c][:, sl], PSA[:, 0:512]),
                        deps=[tab] + hn_toks[-1:])
                    u2 = p.op("dve", lambda v, c=c: v.tensor_add(
                        tmpc[:], tmpc[:], PSA[:, 512:1024]), deps=[u1])
                    u3 = p.op("dve", lambda v, c=c, lidx=lidx: v.tensor_scalar(
                        hn_bf[c][:], tmpc[:],
                        lg_sb[:, lidx * 8 + c:lidx * 8 + c + 1],
                        lb_sb[:, lidx * 8 + c:lidx * 8 + c + 1],
                        ALU.mult, ALU.add), deps=[u2] + lds)
                    hn_toks.append(u3)
                he_toks = []
                prev_he = [[], []]
                for j in range(FFN_HC):
                    jpi = j % 2
                    if jpi == 0:
                        RA, RG = PSB[:, 0:512], PSB[:, 512:1024]
                    else:
                        RA, RG = PSA[:, 1024:1536], PSA[:, 1536:2048]
                    extra = ([t1, t3] if jpi == 1 else [])
                    for c in range(8):
                        p.op("pe", lambda t, c=c, j=j, RA=RA: t.matmul(
                            RA, POOL1[:, c * 2 * FFN_H + j * 128:
                                  c * 2 * FFN_H + j * 128 + 128],
                            hn_bf[c][:], start=(c == 0), stop=(c == 7)),
                            deps=[lw1a[j], hn_toks[c]] + prev_he[jpi] + extra)
                    t_a = ("pe", p.count["pe"])
                    for c in range(8):
                        p.op("pe", lambda t, c=c, j=j, RG=RG: t.matmul(
                            RG, POOL1[:, c * 2 * FFN_H + FFN_H + j * 128:
                                  c * 2 * FFN_H + FFN_H + j * 128 + 128],
                            hn_bf[c][:], start=(c == 0), stop=(c == 7)),
                            deps=[lw1g[j], hn_toks[c]] + prev_he[jpi] + extra)
                    t_g = ("pe", p.count["pe"])
                    t_gl = p.op("act", lambda a, j=j, jpi=jpi, RG=RG,
                                lidx=lidx: a.activation(
                        gelu_sb[jpi][:], RG, AF.Gelu,
                        bias=b1_sb[:, lidx * 2 * FFN_HC + FFN_HC + j:
                              lidx * 2 * FFN_HC + FFN_HC + j + 1]),
                        deps=[t_g] + lds)
                    t_he = p.op("dve", lambda v, j=j, jpi=jpi, RA=RA,
                                lidx=lidx: v.scalar_tensor_tensor(
                        he_sb[:, j * 512:j * 512 + 512], RA,
                        b1_sb[:, lidx * 2 * FFN_HC + j:
                              lidx * 2 * FFN_HC + j + 1],
                        gelu_sb[jpi][:], ALU.add, ALU.mult),
                        deps=[t_a, t_gl] + lds)
                    prev_he[jpi] = [t_he]
                    he_toks.append(t_he)
                st_prev2 = [[], []]
                tev_prev = [[], []]
                for mch in range(8):
                    mpi = mch % 2
                    W2R = PSB[:, 1024 + mpi * 512:1536 + mpi * 512]
                    for j in range(FFN_HC):
                        p.op("pe", lambda t, mch=mch, j=j, W2R=W2R: t.matmul(
                            W2R, POOL2[:, j * D + mch * 128:j * D + mch * 128 + 128],
                            he_sb[:, j * 512:j * 512 + 512],
                            start=(j == 0), stop=(j == FFN_HC - 1)),
                            deps=[lw2c[j], he_toks[j]] + tev_prev[mpi])
                    t_y = ("pe", p.count["pe"])
                    t_ev = p.op("dve", lambda v, mch=mch, mpi=mpi, W2R=W2R,
                                lidx=lidx: v.scalar_tensor_tensor(
                        xnew[mpi][:], W2R,
                        b2_sb[:, lidx * 8 + mch:lidx * 8 + mch + 1],
                        xw[mch][:, 1:513], ALU.add, ALU.add),
                        deps=[t_y] + lds + st_prev2[mpi])
                    tev_prev[mpi] = [t_ev]
                    if final_store:
                        sot = p.dma("spq",
                                    io["f0_fm"][mch * 128:(mch + 1) * 128,
                                          ob * 512:ob * 512 + 512],
                                    xnew[mpi][:], deps=[t_ev])
                    else:
                        sot = p.dma("spq",
                                    io["SCR_y1"][mch * 128:(mch + 1) * 128,
                                           1 + ob * 512:1 + ob * 512 + 512],
                                    xnew[mpi][:], deps=[t_ev])
                    st_prev2[mpi] = [sot]
                    y_stores.append(sot)
                prev_b = p.barrier_token()
            return p.barrier_token()

        init_done = p.barrier_token()
        bnd_w = bnd_windows
        done1 = run_layer(0, io["x_fm"], TIN1, None, TIN2, 0, init_done,
                          final_store=False)
        done2 = run_layer(1, io["SCR_y1"], TIN2, None, TOUT2, TIN1 - TIN2,
                          done1, final_store=True)

        return p.barrier_token()


def build_l0ffn(TIN1, TOUT1, TOUT2):
    nc = bass.Bass("TRN2", target_bir_lowering=False, debug=False)
    TIN2 = TOUT1
    io = {
        "x_fm": nc.dram_tensor("x_fm", [D, TIN1 + 1], F32, kind="ExternalInput").ap(),
        "f_wqkv": nc.dram_tensor("f_wqkv", [2, D, 1536], F32, kind="ExternalInput").ap(),
        "f_wo": nc.dram_tensor("f_wo", [2, 512, D], F32, kind="ExternalInput").ap(),
        "f_w1": nc.dram_tensor("f_w1", [2, D, 2 * FFN_H], F32, kind="ExternalInput").ap(),
        "f_w2": nc.dram_tensor("f_w2", [2, FFN_H, D], F32, kind="ExternalInput").ap(),
        "f_qs": nc.dram_tensor("f_qs", [128, 2], F32, kind="ExternalInput").ap(),
        "f_ks": nc.dram_tensor("f_ks", [128, 2], F32, kind="ExternalInput").ap(),
        "f_lg": nc.dram_tensor("f_lg", [128, 16], F32, kind="ExternalInput").ap(),
        "f_lb": nc.dram_tensor("f_lb", [128, 16], F32, kind="ExternalInput").ap(),
        "f_b1": nc.dram_tensor("f_b1", [128, 4 * FFN_HC], F32, kind="ExternalInput").ap(),
        "f_b2": nc.dram_tensor("f_b2", [128, 16], F32, kind="ExternalInput").ap(),
        "f_pos": nc.dram_tensor("f_pos", [128, 1], F32, kind="ExternalInput").ap(),
        "f0_fm": nc.dram_tensor("f0_fm", [D, TOUT2], F32, kind="ExternalOutput").ap(),
        "SCR_a": nc.dram_tensor("a_scr", [D, TIN1 + 1], F32, kind="Internal").ap(),
        "SCR_y1": nc.dram_tensor("y1_scr", [D, TIN2 + 1], F32, kind="Internal").ap(),
    }
    es = ExitStack()
    with es:
        p = Prog(nc, es)
        idb, idb_deps = emit_identity(p, es, nc, "identb2", BF16)
        shared = {"identb": idb, "identb_deps": idb_deps}
        emit_l0ffn(p, es, nc, alloc, alloc_ps, io, shared, TIN1, TOUT1, TOUT2)
        block = es.enter_context(nc.Block())
        finish(p, block, p.barrier_token())
        p.build(block)
    return nc



def build_mix0(T, SW=1024):
    NE = 8
    nc = bass.Bass("TRN2", target_bir_lowering=False, debug=False)
    x_in = nc.dram_tensor("x_fm", [D, T], F32, kind="ExternalInput").ap()
    a0_in = nc.dram_tensor("a0_fm", [D, T], F32, kind="ExternalInput").ap()
    f0_in = nc.dram_tensor("f0_fm", [D, T], F32, kind="ExternalInput").ap()
    wr_in = nc.dram_tensor("wr_t", [D, NE], F32, kind="ExternalInput").ap()
    x1_out = nc.dram_tensor("x1_fm", [D, T], F32, kind="ExternalOutput").ap()
    xn1_out = nc.dram_tensor("xn1_bf", [D, T], BF16, kind="ExternalOutput").ap()
    wts_out = nc.dram_tensor("wts_tm", [T, NE], F32, kind="ExternalOutput").ap()
    es = ExitStack()
    with es:
        p = Prog(nc, es)
        eng = p.engines_token
        x_t = [alloc(es, nc, f"x{c}", [128, SW], F32) for c in range(8)]
        a_t = [alloc(es, nc, f"a{c}", [128, SW], F32) for c in range(8)]
        f_t = [alloc(es, nc, f"f{c}", [128, SW], F32) for c in range(8)]
        xn_t = [alloc(es, nc, f"xn{c}", [128, SW], F32) for c in range(8)]
        xnb_t = [alloc(es, nc, f"xnb{c}", [128, SW], BF16) for c in range(8)]
        wr_sb = alloc(es, nc, "wr_sb", [128, 8 * NE], F32)
        sc_ps = alloc_ps(es, nc, "sc_ps", [128, 2 * NE], F32)
        probs = alloc(es, nc, "probs", [128, NE], F32)
        p2 = alloc(es, nc, "p2", [128, NE], F32)
        wts_sb = alloc(es, nc, "wts_sb", [128, NE], F32)
        m_c = alloc(es, nc, "mx_c", [128, 1], F32)
        nm_c = alloc(es, nc, "nm_c", [128, 1], F32)
        l_cc = alloc(es, nc, "l_cc", [128, 1], F32)
        li_c = alloc(es, nc, "li_c", [128, 1], F32)
        m2_c = alloc(es, nc, "m2_c", [128, 1], F32)
        consts = make_const_tiles(p, es, nc)
        lwr = p.dma("spq", wr_sb[:].rearrange("p (c e) -> p c e", c=8),
                    wr_in.rearrange("(c p) e -> p c e", p=128))
        st = []
        prev = []
        for w0 in range(0, T, SW):
            ldx = [p.dma("spq", x_t[c][:], x_in[c * 128:(c + 1) * 128,
                                                w0:w0 + SW], deps=prev)
                   for c in range(8)]
            lda = [p.dma("spq", a_t[c][:], a0_in[c * 128:(c + 1) * 128,
                                                 w0:w0 + SW], deps=prev)
                   for c in range(8)]
            ldf = [p.dma("actq", f_t[c][:], f0_in[c * 128:(c + 1) * 128,
                                                  w0:w0 + SW], deps=prev)
                   for c in range(8)]
            s_toks = []
            for c in range(8):
                t1 = p.op("dve", lambda v, c=c: v.tensor_add(
                    x_t[c][:], x_t[c][:], a_t[c][:]),
                    deps=[ldx[c], lda[c]] + prev)
                t2 = p.op("dve", lambda v, c=c: v.tensor_add(
                    x_t[c][:], x_t[c][:], f_t[c][:]), deps=[t1, ldf[c]])
                s_toks.append(t2)
                st.append(p.dma("spq", x1_out[c * 128:(c + 1) * 128,
                                               w0:w0 + SW], x_t[c][:],
                                deps=[t2]))
            for s0, toks in emit_ln_fm(p, es, nc, consts, x_t, s_toks,
                                       xnb_t, xn_t, SW):
                for c in range(8):
                    sl = slice(s0, s0 + 512)
                    gl = slice(w0 + s0, w0 + s0 + 512)
                    st.append(p.dma("spq", xn1_out[c * 128:(c + 1) * 128, gl],
                                    xnb_t[c][:, sl], deps=toks))
                for sub in range(4):
                    t0c = s0 + sub * 128
                    pp = sc_ps[:, (sub % 2) * NE:(sub % 2) * NE + NE]
                    for c in range(8):
                        p.op("pe", lambda t, c=c, t0c=t0c, pp=pp: t.matmul(
                            pp, xn_t[c][:, t0c:t0c + 128],
                            wr_sb[:, c * NE:(c + 1) * NE],
                            start=(c == 0), stop=(c == 7)),
                            deps=toks + [lwr] + eng(("dve", "act")))
                    tsc = ("pe", p.count["pe"])
                    r1 = p.op("dve", lambda v, pp=pp: v.reduce_max(
                        m_c[:], pp, axis=AX.X), deps=[tsc] + eng(("act",)))
                    r2 = p.op("dve", lambda v: v.tensor_scalar_mul(
                        nm_c[:], m_c[:], -1.0), deps=[r1])
                    r3 = p.op("act", lambda a, pp=pp: a.activation(
                        probs[:], pp, AF.Exp, bias=nm_c[:, 0:1],
                        accum_out=l_cc[:, 0:1]), deps=[r2] + eng(("dve",)))
                    r4 = p.op("dve", lambda v: v.reciprocal(
                        li_c[:], l_cc[:]), deps=[r3])
                    r5 = p.op("dve", lambda v: v.tensor_scalar_mul(
                        probs[:], probs[:], li_c[:, 0:1]), deps=[r4, r3])
                    r6 = p.op("dve", lambda v: v.reduce_max(
                        m_c[:], probs[:], axis=AX.X), deps=[r5])
                    r7 = p.op("dve", lambda v: v.tensor_scalar(
                        p2[:], probs[:], m_c[:, 0:1], None, ALU.is_lt),
                        deps=[r6])
                    r8 = p.op("dve", lambda v: v.tensor_mul(
                        p2[:], probs[:], p2[:]), deps=[r7])
                    r9 = p.op("dve", lambda v: v.reduce_max(
                        m2_c[:], p2[:], axis=AX.X), deps=[r8])
                    r10 = p.op("dve", lambda v: v.tensor_scalar(
                        wts_sb[:], probs[:], m2_c[:, 0:1], None, ALU.is_ge),
                        deps=[r9])
                    r11 = p.op("dve", lambda v: v.tensor_mul(
                        wts_sb[:], wts_sb[:], probs[:]), deps=[r10])
                    st.append(p.dma("spq", wts_out[w0 + t0c:w0 + t0c + 128, :],
                                    wts_sb[:], deps=[r11]))
            prev = p.barrier_token()
        block = es.enter_context(nc.Block())
        finish(p, block, p.barrier_token())
        p.build(block)
    return nc



S1 = 5120
HALO1 = 3072


def _ct(a):
    return np.ascontiguousarray(a)


def kernel(**inputs):
    import concourse.bass_utils as bu
    f32 = np.float32
    x = np.asarray(inputs["L0_attn_x"], dtype=f32)

    def tile128(vec):
        return _ct(np.tile(np.asarray(vec, f32)[None, :], (128, 1)))

    xT = [_ct(x[b].T) for b in range(B)]

    nc = build_ln0(TOK)
    maps = [{"x_fm": _ct(xT[b][:, h * TOK:(h + 1) * TOK])}
            for b in range(B) for h in range(2)]
    res = run_bass_kernel_spmd(nc, maps, core_ids=list(range(8)))
    xn_f = [np.concatenate([res.results[2 * b]["xn_f32"],
                            res.results[2 * b + 1]["xn_f32"]], 1)
            for b in range(B)]
    xn_b = [np.concatenate([res.results[2 * b]["xn_bf"],
                            res.results[2 * b + 1]["xn_bf"]], 1)
            for b in range(B)]

    wq_t = _ct(np.asarray(inputs["L0_attn_Wq"], f32).T)
    wk_t = _ct(np.asarray(inputs["L0_attn_Wk"], f32).T)
    wv_t = _ct(np.asarray(inputs["L0_attn_Wv"], f32).T)
    wo_t = _ct(np.asarray(inputs["L0_attn_Wo"], f32).T)
    gam16 = tile128(inputs["L0_attn_gamma"])
    bet16 = tile128(inputs["L0_attn_beta"])
    wcom = dict(wq_t=wq_t, wk_t=wk_t, wv_t=wv_t, wo_t=wo_t,
                gam16=gam16, bet16=bet16)
    nc0 = build_l0attn(0, TOK)
    maps0 = [dict(xn_bf=_ct(xn_b[b][:, 0:TOK]), **wcom) for b in range(B)]
    r0 = run_bass_kernel_spmd(nc0, maps0, core_ids=[0, 2, 4, 6])
    nc1 = build_l0attn(TOK, TOK)
    maps1 = [dict(xn_bf=_ct(xn_b[b]), **wcom) for b in range(B)]
    r1 = run_bass_kernel_spmd(nc1, maps1, core_ids=[1, 3, 5, 7])
    a0 = [np.concatenate([r0.results[b]["a0_fm"], r1.results[b]["a0_fm"]], 1)
          for b in range(B)]

    wqkv = np.asarray(inputs["L0_ffn_wqkv"], f32)
    wo2 = np.asarray(inputs["L0_ffn_wo"], f32)
    w1 = np.asarray(inputs["L0_ffn_w1"], f32)
    b1 = np.asarray(inputs["L0_ffn_b1"], f32)
    w2 = np.asarray(inputs["L0_ffn_w2"], f32)
    b2 = np.asarray(inputs["L0_ffn_b2"], f32)
    qsc = np.asarray(inputs["L0_ffn_q_scale"], f32)
    ksc = np.asarray(inputs["L0_ffn_k_scale"], f32)
    lg = np.asarray(inputs["L0_ffn_ln_g"], f32)
    lb = np.asarray(inputs["L0_ffn_ln_b"], f32)
    HID = 2730
    w1p = np.zeros((2, D, 2 * FFN_H), f32)
    w1p[:, :, :HID] = w1[:, :, :HID]
    w1p[:, :, FFN_H:FFN_H + HID] = w1[:, :, HID:]
    b1p = np.zeros((2, 2 * FFN_H), f32)
    b1p[:, :HID] = b1[:, :HID]
    b1p[:, FFN_H:FFN_H + HID] = b1[:, HID:]
    w2p = np.zeros((2, FFN_H, D), f32)
    w2p[:, :HID, :] = w2
    fcom = dict(
        f_wqkv=wqkv, f_wo=wo2, f_w1=w1p, f_w2=w2p,
        f_qs=_ct(np.tile(qsc.T, (2, 1))), f_ks=_ct(np.tile(ksc.T, (2, 1))),
        f_lg=_ct(lg.reshape(2, 8, 128).transpose(2, 0, 1).reshape(128, 16)),
        f_lb=_ct(lb.reshape(2, 8, 128).transpose(2, 0, 1).reshape(128, 16)),
        f_b1=_ct(b1p.reshape(2, 2 * FFN_HC, 128).transpose(2, 0, 1)
                 .reshape(128, -1)),
        f_b2=_ct(b2.reshape(2, 8, 128).transpose(2, 0, 1).reshape(128, 16)),
    )
    nc0 = build_l0ffn(TOK, TOK, TOK)
    maps0 = []
    for b in range(B):
        xfm = np.concatenate([np.zeros((D, 1), f32), xn_f[b][:, 0:TOK]], 1)
        maps0.append(dict(x_fm=_ct(xfm),
                          f_pos=np.full((128, 1), 0.0, f32), **fcom))
    r0 = run_bass_kernel_spmd(nc0, maps0, core_ids=[0, 2, 4, 6])
    s_in = N - (TOK + HALO1)
    nc1 = build_l0ffn(TOK + HALO1, TOK + HALO1 // 2, TOK)
    maps1 = []
    for b in range(B):
        xfm = np.concatenate([xn_f[b][:, s_in - 1:s_in],
                              xn_f[b][:, s_in:N]], 1)
        maps1.append(dict(x_fm=_ct(xfm),
                          f_pos=np.full((128, 1), float(s_in), f32), **fcom))
    r1 = run_bass_kernel_spmd(nc1, maps1, core_ids=[1, 3, 5, 7])
    f0 = [np.concatenate([r0.results[b]["f0_fm"], r1.results[b]["f0_fm"]], 1)
          for b in range(B)]

    wr_t = _ct(np.asarray(inputs["L1_ffn_w_router"], f32).T)
    nc = build_mix0(TOK)
    maps = []
    for b in range(B):
        for h in range(2):
            sl = slice(h * TOK, (h + 1) * TOK)
            maps.append(dict(x_fm=_ct(xT[b][:, sl]), a0_fm=_ct(a0[b][:, sl]),
                             f0_fm=_ct(f0[b][:, sl]), wr_t=wr_t))
    res = run_bass_kernel_spmd(nc, maps, core_ids=list(range(8)))
    x1 = [np.concatenate([res.results[2 * b]["x1_fm"],
                          res.results[2 * b + 1]["x1_fm"]], 1)
          for b in range(B)]
    xn1 = [np.concatenate([res.results[2 * b]["xn1_bf"],
                           res.results[2 * b + 1]["xn1_bf"]], 1)
           for b in range(B)]
    wts = [np.concatenate([res.results[2 * b]["wts_tm"],
                           res.results[2 * b + 1]["wts_tm"]], 0)
           for b in range(B)]

    l1w = dict(l1_wq=np.asarray(inputs["L1_attn_W_Q"], f32),
               l1_wk=np.asarray(inputs["L1_attn_W_K"], f32),
               l1_wv=np.asarray(inputs["L1_attn_W_V"], f32))
    nc0 = build_l1attn(0, S1)
    maps0 = [dict(xn1_bf=_ct(xn1[b][:, 0:S1]), **l1w) for b in range(B)]
    r0 = run_bass_kernel_spmd(nc0, maps0, core_ids=[0, 2, 4, 6])
    nc1 = build_l1attn(S1, N - S1)
    maps1 = [dict(xn1_bf=_ct(xn1[b]), **l1w) for b in range(B)]
    r1 = run_bass_kernel_spmd(nc1, maps1, core_ids=[1, 3, 5, 7])
    a1 = [np.concatenate([r0.results[b]["a1_tm"], r1.results[b]["a1_tm"]], 0)
          for b in range(B)]

    mw = dict(moe_w1=np.asarray(inputs["L1_ffn_w1"], f32),
              moe_w3=np.asarray(inputs["L1_ffn_w3"], f32),
              moe_w2=np.asarray(inputs["L1_ffn_w2"], f32))
    nc0 = build_moe(S1, TG=1024)
    maps0 = [dict(xn1_bf=_ct(xn1[b][:, 0:S1]), x1_fm=_ct(x1[b][:, 0:S1]),
                  a1_tm=_ct(a1[b][0:S1]), wts_tm=_ct(wts[b][0:S1]), **mw)
             for b in range(B)]
    r0 = run_bass_kernel_spmd(nc0, maps0, core_ids=[0, 2, 4, 6])
    nc1 = build_moe(N - S1, TG=1024)
    maps1 = [dict(xn1_bf=_ct(xn1[b][:, S1:]), x1_fm=_ct(x1[b][:, S1:]),
                  a1_tm=_ct(a1[b][S1:]), wts_tm=_ct(wts[b][S1:]), **mw)
             for b in range(B)]
    r1 = run_bass_kernel_spmd(nc1, maps1, core_ids=[1, 3, 5, 7])

    out = np.empty((B, N, D), f32)
    for b in range(B):
        out[b, 0:S1] = r0.results[b]["out_fm"].T
        out[b, S1:] = r1.results[b]["out_fm"].T
    return out



class Arena:

    def __init__(self, es, nc, sbuf_cols_bf16, name="ARENA"):
        self.nc = nc
        self.pool = alloc(es, nc, name, [128, sbuf_cols_bf16], BF16)
        self.pool_f32 = self.pool[:].bitcast(F32)
        self.cols = sbuf_cols_bf16
        self.off = 0
        self.ps = [alloc_ps(es, nc, f"{name}_psA", [128, 2048], F32),
                   alloc_ps(es, nc, f"{name}_psB", [128, 2048], F32)]
        self.ps_off = 0

    def reset(self):
        self.off = 0
        self.ps_off = 0

    def __call__(self, es_unused, nc_unused, name, shape, dtype):
        P, Nfree = shape
        units = Nfree if dtype == BF16 else 2 * Nfree
        units = (units + 7) & ~7
        assert self.off + units <= self.cols, f"arena OOM at {name}"
        if dtype == BF16:
            t = self.pool[0:P, self.off:self.off + units][:, 0:Nfree]
        else:
            t = self.pool_f32[0:P, self.off // 2:(self.off + units) // 2][:, 0:Nfree]
        self.off += units
        return _APWrap(t)

    def psum(self, es_unused, nc_unused, name, shape, dtype=F32):
        P, Nfree = shape
        nb = ((Nfree if dtype == F32 else Nfree // 2) + 511) // 512
        assert nb <= 4, f"psum request too big: {name}"
        which = self.ps_off // 4
        rem = 4 - (self.ps_off % 4)
        if nb > rem:
            self.ps_off += rem
            which = self.ps_off // 4
        assert which < 2, f"psum arena OOM at {name}"
        b0 = self.ps_off % 4
        self.ps_off += nb
        t = self.ps[which]
        if dtype == F32:
            return _APWrap(t[0:P, b0 * 512:b0 * 512 + Nfree])
        return _APWrap(t[:].bitcast(BF16)[0:P, b0 * 1024:b0 * 1024 + Nfree])


class _APWrap:

    def __init__(self, ap):
        self._ap = ap

    def __getitem__(self, key):
        return self._ap[key]

    def ap(self):
        return self._ap



def emit_lnphase(p, es, nc, A, APS, io, consts, T, SW=1024, prev=()):
    x_in, xn_out, xnb_out = io["x_src"], io["xn_f32"], io["xn_bf"]
    x_t = [A(es, nc, f"lnx{c}", [128, SW], F32) for c in range(8)]
    xn_t = [A(es, nc, f"lnxn{c}", [128, SW], F32) for c in range(8)]
    xnb_t = [A(es, nc, f"lnxnb{c}", [128, SW], BF16) for c in range(8)]
    st = []
    prev = list(prev)
    for w0 in range(0, T, SW):
        ld = [p.dma("spq", x_t[c][:], x_in[c * 128:(c + 1) * 128, w0:w0 + SW],
                    deps=prev) for c in range(8)]
        for s0, toks in emit_ln_fm(p, es, nc, consts, x_t, ld, xnb_t, xn_t, SW):
            for c in range(8):
                sl = slice(s0, s0 + 512)
                gl = slice(w0 + s0, w0 + s0 + 512)
                st.append(p.dma("spq", xn_out[c * 128:(c + 1) * 128, gl],
                                xn_t[c][:, sl], deps=toks))
                st.append(p.dma("spq", xnb_out[c * 128:(c + 1) * 128, gl],
                                xnb_t[c][:, sl], deps=toks))
        prev = p.barrier_token()
    return p.barrier_token()


def emit_mixphase(p, es, nc, A, APS, io, consts, T, SW=1024, prev=(), NE=8):
    x_in, a0_in, f0_in = io["x_own"], io["a0_fm"], io["f0_fm"]
    x1_out, xn1_out, wts_out = io["x1_fm"], io["xn1_bf"], io["wts_tm"]
    wr_in = io["wr_t"]
    eng = p.engines_token
    x_t = [A(es, nc, f"mx{c}", [128, SW], F32) for c in range(8)]
    a_t = [A(es, nc, f"ma{c}", [128, SW], F32) for c in range(8)]
    f_t = [A(es, nc, f"mf{c}", [128, SW], F32) for c in range(8)]
    xn_t = [A(es, nc, f"mxn{c}", [128, SW], F32) for c in range(8)]
    xnb_t = [A(es, nc, f"mxnb{c}", [128, SW], BF16) for c in range(8)]
    wr_sb = A(es, nc, "m_wr", [128, 8 * NE], F32)
    sc_ps = APS(es, nc, "m_scps", [128, 2 * NE], F32)
    probs = A(es, nc, "m_probs", [128, NE], F32)
    p2 = A(es, nc, "m_p2", [128, NE], F32)
    wts_sb = A(es, nc, "m_wts", [128, NE], F32)
    m_c = A(es, nc, "m_mx", [128, 1], F32)
    nm_c = A(es, nc, "m_nm", [128, 1], F32)
    l_cc = A(es, nc, "m_l", [128, 1], F32)
    li_c = A(es, nc, "m_li", [128, 1], F32)
    m2_c = A(es, nc, "m_m2", [128, 1], F32)
    lwr = p.dma("spq", wr_sb[:].rearrange("p (c e) -> p c e", c=8),
                wr_in.rearrange("(c p) e -> p c e", p=128), deps=list(prev))
    st = []
    prev = list(prev)
    for w0 in range(0, T, SW):
        ldx = [p.dma("spq", x_t[c][:], x_in[c * 128:(c + 1) * 128, w0:w0 + SW],
                     deps=prev) for c in range(8)]
        lda = [p.dma("spq", a_t[c][:], a0_in[c * 128:(c + 1) * 128, w0:w0 + SW],
                     deps=prev) for c in range(8)]
        ldf = [p.dma("actq", f_t[c][:], f0_in[c * 128:(c + 1) * 128, w0:w0 + SW],
                     deps=prev) for c in range(8)]
        s_toks = []
        for c in range(8):
            t1 = p.op("dve", lambda v, c=c: v.tensor_add(
                x_t[c][:], x_t[c][:], a_t[c][:]),
                deps=[ldx[c], lda[c]] + prev)
            t2 = p.op("dve", lambda v, c=c: v.tensor_add(
                x_t[c][:], x_t[c][:], f_t[c][:]), deps=[t1, ldf[c]])
            s_toks.append(t2)
            st.append(p.dma("spq", x1_out[c * 128:(c + 1) * 128, w0:w0 + SW],
                            x_t[c][:], deps=[t2]))
        for s0, toks in emit_ln_fm(p, es, nc, consts, x_t, s_toks,
                                   xnb_t, xn_t, SW, tag="mln"):
            for c in range(8):
                sl = slice(s0, s0 + 512)
                gl = slice(w0 + s0, w0 + s0 + 512)
                st.append(p.dma("spq", xn1_out[c * 128:(c + 1) * 128, gl],
                                xnb_t[c][:, sl], deps=toks))
            for sub in range(4):
                t0c = s0 + sub * 128
                pp = sc_ps[:, (sub % 2) * NE:(sub % 2) * NE + NE]
                for c in range(8):
                    p.op("pe", lambda t, c=c, t0c=t0c, pp=pp: t.matmul(
                        pp, xn_t[c][:, t0c:t0c + 128],
                        wr_sb[:, c * NE:(c + 1) * NE],
                        start=(c == 0), stop=(c == 7)),
                        deps=toks + [lwr] + eng(("dve", "act")))
                tsc = ("pe", p.count["pe"])
                r1 = p.op("dve", lambda v, pp=pp: v.reduce_max(
                    m_c[:], pp, axis=AX.X), deps=[tsc] + eng(("act",)))
                r2 = p.op("dve", lambda v: v.tensor_scalar_mul(
                    nm_c[:], m_c[:], -1.0), deps=[r1])
                r3 = p.op("act", lambda a, pp=pp: a.activation(
                    probs[:], pp, AF.Exp, bias=nm_c[:, 0:1],
                    accum_out=l_cc[:, 0:1]), deps=[r2] + eng(("dve",)))
                r4 = p.op("dve", lambda v: v.reciprocal(li_c[:], l_cc[:]),
                          deps=[r3])
                r5 = p.op("dve", lambda v: v.tensor_scalar_mul(
                    probs[:], probs[:], li_c[:, 0:1]), deps=[r4, r3])
                r6 = p.op("dve", lambda v: v.reduce_max(
                    m_c[:], probs[:], axis=AX.X), deps=[r5])
                r7 = p.op("dve", lambda v: v.tensor_scalar(
                    p2[:], probs[:], m_c[:, 0:1], None, ALU.is_lt), deps=[r6])
                r8 = p.op("dve", lambda v: v.tensor_mul(
                    p2[:], probs[:], p2[:]), deps=[r7])
                r9 = p.op("dve", lambda v: v.reduce_max(
                    m2_c[:], p2[:], axis=AX.X), deps=[r8])
                r10 = p.op("dve", lambda v: v.tensor_scalar(
                    wts_sb[:], probs[:], m2_c[:, 0:1], None, ALU.is_ge),
                    deps=[r9])
                r11 = p.op("dve", lambda v: v.tensor_mul(
                    wts_sb[:], wts_sb[:], probs[:]), deps=[r10])
                st.append(p.dma("spq", wts_out[w0 + t0c:w0 + t0c + 128, :],
                                wts_sb[:], deps=[r11]))
        prev = p.barrier_token()
    return p.barrier_token()


def emit_l1attn_sym(p, es, nc, A, APS, io, shared, prev=(), GQB=512, KT=512):
    xg = io["xg_bf"]
    xq = io["xn1own_bf"]
    wq_in, wk_in, wv_in = io["l1_wq"], io["l1_wk"], io["l1_wv"]
    kfm, vtm, qfm = io["S_kfm"], io["S_vtm"], io["S_qfm"]
    a1_out = io["a1_tm"]
    eng = p.engines_token
    NB = 8
    TOWN = NB * GQB

    wq_sb = A(es, nc, "awq", [128, 8 * D], BF16)
    wk_sb = A(es, nc, "awk", [128, 8 * D], BF16)
    wv_sb = A(es, nc, "awv", [128, 8 * D], BF16)
    xn_t = [A(es, nc, f"axn{c}", [128, 512], BF16) for c in range(8)]
    ev_k = A(es, nc, "aevk", [128, 512], BF16)
    ev_v = A(es, nc, "aevv", [128, D], BF16)
    P512a = APS(es, nc, "aP512a", [128, 512], F32)
    P512b = APS(es, nc, "aP512b", [128, 512], F32)
    P1024a = APS(es, nc, "aP1024a", [128, D], F32)
    P1024b = APS(es, nc, "aP1024b", [128, D], F32)
    PTbf = APS(es, nc, "aPTbf", [128, 1024], BF16)
    k_ps, v_ps = P512a, P1024a
    lwq = p.dma("poolq", wq_sb[:].rearrange("p (c d) -> p c d", c=8),
                wq_in.rearrange("(c p) d -> p c d", p=128), deps=list(prev))
    lwk = p.dma("poolq", wk_sb[:].rearrange("p (c d) -> p c d", c=8),
                wk_in.rearrange("(c p) d -> p c d", p=128), deps=list(prev))
    lwv = p.dma("poolq", wv_sb[:].rearrange("p (c d) -> p c d", c=8),
                wv_in.rearrange("(c p) d -> p c d", p=128), deps=list(prev))
    prev_it = list(prev)
    st_k = st_q = st_v = None
    te_k = te_q = te_v = None
    for half in range(2):
        for t0 in range(0, TOWN, 512):
            gcol = half * TOWN + t0
            ld = [p.dma("spq", xn_t[c][:],
                        xg[half * 1024 + c * 128:half * 1024 + (c + 1) * 128,
                           t0:t0 + 512], deps=prev_it) for c in range(8)]
            for m in range(8):
                for c in range(8):
                    p.op("pe", lambda t, m=m, c=c: t.matmul(
                        k_ps[:], wk_sb[:, c * D + m * 128:c * D + m * 128 + 128],
                        xn_t[c][:], start=(c == 0), stop=(c == 7)),
                        deps=[lwk] + ld + prev_it + ([te_k] if te_k else []))
                tk = ("pe", p.count["pe"])
                te_k = p.op("act", lambda a: a.copy(ev_k[:], k_ps[:]),
                            deps=[tk] + ([st_k] if st_k else []))
                st_k = p.dma("spq", kfm[m * 128:(m + 1) * 128, gcol:gcol + 512],
                             ev_k[:], deps=[te_k])
            for r in range(4):
                for dh in range(2):
                    for c in range(8):
                        p.op("pe", lambda t, r=r, dh=dh, c=c: t.matmul(
                            v_ps[:, dh * 512:dh * 512 + 512],
                            xn_t[c][:, r * 128:r * 128 + 128],
                            wv_sb[:, c * D + dh * 512:c * D + dh * 512 + 512],
                            start=(c == 0), stop=(c == 7)),
                            deps=[lwv] + ld + prev_it + ([te_v] if te_v else []))
                tv = ("pe", p.count["pe"])
                te_v = p.op("act", lambda a: a.copy(ev_v[:], v_ps[:]),
                            deps=[tv] + ([st_v] if st_v else []))
                st_v = p.dma("spq",
                             vtm[gcol + r * 128:gcol + r * 128 + 128, :],
                             ev_v[:], deps=[te_v])
            prev_it = p.engines_token()
    for t0 in range(0, TOWN, 512):
        ld = [p.dma("spq", xn_t[c][:],
                    xq[c * 128:(c + 1) * 128, t0:t0 + 512], deps=prev_it)
              for c in range(8)]
        for m in range(8):
            for c in range(8):
                p.op("pe", lambda t, m=m, c=c: t.matmul(
                    k_ps[:], wq_sb[:, c * D + m * 128:c * D + m * 128 + 128],
                    xn_t[c][:], start=(c == 0), stop=(c == 7)),
                    deps=[lwq] + ld + prev_it + ([te_k] if te_k else []))
            tq = ("pe", p.count["pe"])
            te_k = p.op("act", lambda a: a.copy(ev_k[:], k_ps[:]),
                        deps=[tq] + ([st_q] if st_q else []))
            st_q = p.dma("spq", qfm[m * 128:(m + 1) * 128, t0:t0 + 512],
                         ev_k[:], deps=[te_k])
        prev_it = p.engines_token()
    phaseA_done = p.barrier_token()

    qg = [A(es, nc, f"aqg{c}", [128, GQB], BF16) for c in range(8)]
    kv_k = [A(es, nc, f"akvk{i}", [128, 8 * KT], BF16) for i in range(2)]
    kv_v = [A(es, nc, f"akvv{i}", [128, 4 * D], BF16) for i in range(2)]
    o_sb = [A(es, nc, f"aosb{i}", [128, D], F32) for i in range(4)]
    m_sb = A(es, nc, "amsb", [128, 4], F32)
    mn_sb = A(es, nc, "amnsb", [128, 4], F32)
    rm_sb = A(es, nc, "armsb", [128, 4], F32)
    alq_sb = A(es, nc, "aalq", [128, 4], F32)
    bia_sb = A(es, nc, "abia", [128, 4], F32)
    lp_sb = A(es, nc, "alpsb", [128, 4], F32)
    l_sb = A(es, nc, "alsb", [128, 4], F32)
    li_sb = A(es, nc, "alisb", [128, 4], F32)
    p2_sb = [A(es, nc, f"apsb{i}", [128, KT], BF16) for i in range(2)]
    pt2_sb = [A(es, nc, f"aptsb{i}", [128, KT], BF16) for i in range(2)]
    a1_sb = A(es, nc, "aa1sb", [128, D], F32)
    bm_sb = A(es, nc, "abm", [128, 100], F32)
    al_sb = A(es, nc, "aal", [128, 16], F32)
    s_ps = [P512a, P512b]
    pv_ps = [P1024a, P1024b]
    pt_ps = PTbf
    identb, identb_deps = shared["identb"], shared["identb_deps"]
    lbm = p.dma("spq", bm_sb[:], io["msel_bm"], deps=list(prev))
    lal = p.dma("spq", al_sb[:], io["msel_al"], deps=list(prev))
    masks, mtoks = [], []
    for oi in range(4):
        mt = A(es, nc, f"amask{oi}", [128, KT], F32)
        t0_ = p.op("dve", lambda v, mt=mt: v.memset(mt[:], 0.0),
                   deps=list(prev))
        t1_ = p.op("pool", lambda g, mt=mt, oi=oi: g.affine_select(
            out=mt[:], in_=mt[:], pattern=[[-1, KT]],
            compare_op=ALU.is_ge, fill=-1e30, base=oi * 128,
            channel_multiplier=1), deps=[t0_])
        masks.append(mt)
        mtoks.append(t1_)
    c32_t, c32_tok = p.constf(1.0 / 32.0)
    bmbase = [0]
    for i in range(NB):
        bmbase.append(bmbase[-1] + 9 + i)

    prev_grp = phaseA_done
    for blk in range(NB):
        nkt = 9 + blk
        NQT = GQB // 128
        lq = [p.dma("spq", qg[c][:],
                    qfm[c * 128:(c + 1) * 128, blk * GQB:blk * GQB + GQB],
                    deps=prev_grp) for c in range(8)]
        o_tok = {}
        a1_prev = []
        pvdep = [[], []]
        kbuf_rd = [list(prev_grp), list(prev_grp)]
        sp_free = [list(prev_grp), list(prev_grp)]
        p_free = [[], []]
        pt_free = [[], []]
        lp_free = [[] for _ in range(NQT)]
        fpend = [None]

        def flush_fpend():
            (qt, kt, pp, kb, texp, t_ae) = fpend[0]
            fpend[0] = None
            qc = slice(qt, qt + 1)
            trt = [p.op("pe", lambda t, j=j, pp=pp: t.transpose(
                pt_ps[:, pp * 512 + j * 128:pp * 512 + j * 128 + 128],
                p2_sb[pp][:, j * 128:j * 128 + 128], identb[:]),
                deps=[texp] + identb_deps + pt_free[pp])
                for j in range(4)]
            tev = p.op("act", lambda a, pp=pp: a.copy(
                pt2_sb[pp][:], pt_ps[:, pp * 512:pp * 512 + 512]),
                deps=trt)
            pt_free[pp] = [tev]
            p_free[pp] = trt
            ov = pv_ps[qt % 2]
            for dh in range(2):
                for r in range(4):
                    p.op("pe", lambda t, r=r, dh=dh, ov=ov, kb=kb, pp=pp:
                         t.matmul(
                        ov[:, dh * 512:dh * 512 + 512],
                        pt2_sb[pp][:, r * 128:r * 128 + 128],
                        kv_v[kb][:, r * D + dh * 512:r * D + dh * 512 + 512],
                        start=(r == 0), stop=(r == 3)),
                        deps=[tev, lvs[kb]] + pvdep[qt % 2])
            tpv = ("pe", p.count["pe"])
            kbuf_rd[kb].append(tpv)
            if kt == 0:
                oa = p.op("dve", lambda v, qt=qt, ov=ov: v.tensor_copy(
                    o_sb[qt][:], ov[:]), deps=[tpv])
            else:
                oa = p.op("dve", lambda v, qt=qt, ov=ov, qc=qc:
                          v.scalar_tensor_tensor(
                    o_sb[qt][:], o_sb[qt][:], alq_sb[:, qc], ov[:],
                    ALU.mult, ALU.add),
                    deps=[tpv, o_tok[qt]] + ([t_ae] if t_ae else []))
            o_tok[qt] = oa
            pvdep[qt % 2] = [oa]

        lvs = [None, None]
        for kt in range(nkt):
            kb = kt % 2
            lk = p.dma("spq", kv_k[kb][:].rearrange("p (c t) -> p c t", c=8),
                       kfm[:, kt * KT:(kt + 1) * KT].rearrange(
                           "(c p) t -> p c t", p=128),
                       deps=kbuf_rd[kb])
            lvs[kb] = p.dma("actq", kv_v[kb][:].rearrange(
                "p (r d) -> p r d", r=4),
                vtm[kt * KT:(kt + 1) * KT, :].rearrange(
                    "(r p) d -> p r d", p=128),
                deps=kbuf_rd[kb])
            kbuf_rd[kb] = []
            bmc = bm_sb[:, bmbase[blk] + kt:bmbase[blk] + kt + 1]
            for qt in range(NQT):
                pp = (kt * NQT + qt) % 2
                sp = s_ps[pp]
                for c in range(8):
                    p.op("pe", lambda t, c=c, qt=qt, sp=sp, kb=kb: t.matmul(
                        sp[:], qg[c][:, qt * 128:qt * 128 + 128],
                        kv_k[kb][:, c * KT:c * KT + KT],
                        start=(c == 0), stop=(c == 7)),
                        deps=[lk] + lq + sp_free[pp])
                ts = ("pe", p.count["pe"])
                kbuf_rd[kb].append(ts)
                dcol = None
                if kt == blk:
                    dcol = 2 * blk
                elif kt == 8 + blk:
                    dcol = 2 * blk + 1
                dep0 = [ts]
                if dcol is not None:
                    tm_ = p.op("dve", lambda v, sp=sp, qt=qt, dcol=dcol:
                               v.scalar_tensor_tensor(
                        sp[:], masks[qt][:], al_sb[:, dcol:dcol + 1],
                        sp[:], ALU.mult, ALU.add),
                        deps=[ts, lal] + mtoks)
                    dep0 = [tm_]
                qc = slice(qt, qt + 1)
                t_rm = p.op("dve", lambda v, sp=sp, qc=qc: v.reduce_max(
                    rm_sb[:, qc], sp[:], axis=AX.X), deps=dep0)
                t_rp = p.op("dve", lambda v, qc=qc, bmc=bmc: v.tensor_scalar(
                    rm_sb[:, qc], rm_sb[:, qc], bmc, None, ALU.add),
                    deps=[t_rm, lbm])
                if kt == 0:
                    t_mn = p.op("dve", lambda v, qc=qc: v.tensor_copy(
                        mn_sb[:, qc], rm_sb[:, qc]), deps=[t_rp])
                    t_mcp = p.op("dve", lambda v, qc=qc: v.tensor_copy(
                        m_sb[:, qc], mn_sb[:, qc]), deps=[t_mn])
                    t_ae = None
                else:
                    t_mn = p.op("dve", lambda v, qc=qc: v.tensor_tensor(
                        mn_sb[:, qc], m_sb[:, qc], rm_sb[:, qc], ALU.max),
                        deps=[t_rp])
                    t_as = p.op("dve", lambda v, qc=qc: v.tensor_sub(
                        alq_sb[:, qc], m_sb[:, qc], mn_sb[:, qc]),
                        deps=[t_mn])
                    t_ae = p.op("act", lambda a, qc=qc: a.activation(
                        alq_sb[:, qc], alq_sb[:, qc], AF.Exp,
                        scale=c32_t[:, 0:1]), deps=[t_as, c32_tok])
                    t_mcp = p.op("dve", lambda v, qc=qc: v.tensor_copy(
                        m_sb[:, qc], mn_sb[:, qc]), deps=[t_as])
                t_b1 = p.op("dve", lambda v, qc=qc, bmc=bmc:
                            v.scalar_tensor_tensor(
                    bia_sb[:, qc], mn_sb[:, qc], -1.0, bmc,
                    ALU.mult, ALU.add), deps=[t_mn, lbm])
                t_b2 = p.op("dve", lambda v, qc=qc: v.tensor_scalar_mul(
                    bia_sb[:, qc], bia_sb[:, qc], 1.0 / 32.0), deps=[t_b1])
                texp = p.op("act", lambda a, sp=sp, qc=qc, pp=pp: a.activation(
                    p2_sb[pp][:], sp[:], AF.Exp,
                    bias=bia_sb[:, qc],
                    scale=c32_t[:, 0:1],
                    accum_out=lp_sb[:, qc]),
                    deps=[t_b2, c32_tok] + p_free[pp] + lp_free[qt] +
                    ([t_ae] if t_ae else []))
                sp_free[pp] = [texp]
                if kt == 0:
                    t_lu = p.op("dve", lambda v, qc=qc: v.tensor_copy(
                        l_sb[:, qc], lp_sb[:, qc]), deps=[texp])
                else:
                    t_lu = p.op("dve", lambda v, qc=qc: v.scalar_tensor_tensor(
                        l_sb[:, qc], l_sb[:, qc], alq_sb[:, qc],
                        lp_sb[:, qc], ALU.mult, ALU.add),
                        deps=[texp] + ([t_ae] if t_ae else []))
                lp_free[qt] = [t_lu]
                if fpend[0] is not None:
                    flush_fpend()
                fpend[0] = (qt, kt, pp, kb, texp, t_ae)
        if fpend[0] is not None:
            flush_fpend()
        for qt in range(NQT):
            qc = slice(qt, qt + 1)
            t2_ = p.op("dve", lambda v, qc=qc: v.reciprocal(
                li_sb[:, qc], l_sb[:, qc]), deps=lp_free[qt])
            t3_ = p.op("dve", lambda v, qt=qt, qc=qc: v.tensor_scalar(
                a1_sb[:], o_sb[qt][:], li_sb[:, qc], None, ALU.mult),
                deps=[t2_, o_tok[qt]] + a1_prev)
            sot = p.dma("spq",
                        a1_out[blk * GQB + qt * 128:
                               blk * GQB + qt * 128 + 128, :],
                        a1_sb[:], deps=[t3_])
            a1_prev = [sot]
        prev_grp = p.barrier_token()
    return p.barrier_token()



def build_mega(dbg=False, nph=7):
    TPRE = TOK
    TIN1, TOUT1, TOUT2 = 7168, 5632, 4096
    NE, H = 8, 1408
    nc = bass.Bass("TRN2", target_bir_lowering=False, debug=False)
    dt_ = nc.dram_tensor
    io = {}
    io["x_pad"] = dt_("x_pad", [D, N], F32, kind="ExternalInput").ap()
    for nm, shp in [("wq_t", [D, 256]), ("wk_t", [D, 256]), ("wv_t", [D, D]),
                    ("wo_t", [D, D]), ("gam16", [128, 16]), ("bet16", [128, 16]),
                    ("f_wqkv", [2, D, 1536]), ("f_wo", [2, 512, D]),
                    ("f_w1", [2, D, 2 * FFN_H]), ("f_w2", [2, FFN_H, D]),
                    ("f_qs", [128, 2]), ("f_ks", [128, 2]), ("f_lg", [128, 16]),
                    ("f_lb", [128, 16]), ("f_b1", [128, 4 * FFN_HC]),
                    ("f_b2", [128, 16]), ("f_pos", [128, 1]), ("f_bnd", [128, 2]),
                    ("a_pre", [128, 1]),
                    ("wr_t", [D, NE]), ("l1_wq", [D, D]), ("l1_wk", [D, D]),
                    ("l1_wv", [D, D]), ("msel_bm", [128, 100]),
                    ("msel_al", [128, 16]), ("moe_w1", [NE, D, H]),
                    ("moe_w3", [NE, D, H]), ("moe_w2", [NE, H, D])]:
        io[nm] = dt_(nm, shp, F32, kind="ExternalInput").ap()
    io["out_fm"] = dt_("out_fm", [D, TOK], F32, kind="ExternalOutput").ap()
    KD = "ExternalOutput" if dbg else "Internal"
    io["S_xn_f32"] = dt_("S_xn_f32", [D, N], F32, kind=KD).ap()
    io["S_xn_bf"] = dt_("S_xn_bf", [D, N], BF16, kind=KD).ap()
    io["S_a0"] = dt_("S_a0", [D, TOK], F32, kind=KD).ap()
    io["S_f0"] = dt_("S_f0", [D, TOK], F32, kind=KD).ap()
    io["SCR_a"] = dt_("SCR_a", [D, TIN1 + 1], F32, kind="Internal").ap()
    io["SCR_y1"] = dt_("SCR_y1", [D, TOUT1 + 1], F32, kind="Internal").ap()
    io["S_x1"] = dt_("S_x1", [D, TOK], F32, kind=KD).ap()
    xn1own = dt_("S_xn1own", [D, TOK], BF16, kind="Internal")
    io["S_xn1own"] = xn1own.ap()
    xg = dt_("S_xg", [2 * D, TOK], BF16, kind="Internal")
    io["S_xg"] = xg.ap()
    io["S_wts"] = dt_("S_wts", [TOK, NE], F32, kind=KD).ap()
    io["S_kfm"] = dt_("S_kfm", [D, N], BF16, kind="Internal").ap()
    io["S_vtm"] = dt_("S_vtm", [N, D], BF16, kind="Internal").ap()
    io["S_qfm"] = dt_("S_qfm", [D, TOK], BF16, kind="Internal").ap()
    io["S_a1"] = dt_("S_a1", [TOK, D], F32, kind=KD).ap()

    es = ExitStack()
    with es:
        p = Prog(nc, es)
        ar = Arena(es, nc, 105472)
        p.alloc_fn = ar
        p.psum_fn = ar.psum
        idf, idf_deps = emit_identity(p, es, nc, "identf", F32)
        idb, idb_deps = emit_identity(p, es, nc, "identb", BF16)
        shared = {"identf": idf, "identf_deps": idf_deps,
                  "identb": idb, "identb_deps": idb_deps}
        consts = make_const_tiles(p, es, nc)

        def nxt():
            tok = p.barrier_token()
            p.phase_base = tok
            ar.reset()
            p.epoch += 1
            return tok

        emit_lnphase(p, es, nc, ar, ar.psum,
                     {"x_src": io["x_pad"], "xn_f32": io["S_xn_f32"],
                      "xn_bf": io["S_xn_bf"]}, consts, N)
        prev = nxt()
        if nph < 2:
            block = es.enter_context(nc.Block())
            finish(p, block, p.barrier_token())
            p.build(block)
            return nc
        emit_l0attn(p, es, nc, ar, ar.psum,
                    {"xn_bf": io["S_xn_bf"], "a0_fm": io["S_a0"],
                     **{k: io[k] for k in ("wq_t", "wk_t", "wv_t", "wo_t",
                                           "gam16", "bet16", "a_pre")}},
                    shared, TPRE, TOK, prev=prev)
        prev = nxt()
        if nph < 3:
            block = es.enter_context(nc.Block())
            finish(p, block, p.barrier_token())
            p.build(block)
            return nc
        emit_l0ffn(p, es, nc, ar, ar.psum,
                   {"x_fm": io["S_xn_f32"][:, N - TIN1 - 1:N],
                    "f0_fm": io["S_f0"], "SCR_a": io["SCR_a"],
                    "SCR_y1": io["SCR_y1"],
                    **{k: io[k] for k in ("f_wqkv", "f_wo", "f_w1", "f_w2",
                                          "f_qs", "f_ks", "f_lg", "f_lb",
                                          "f_b1", "f_b2", "f_pos", "f_bnd")}},
                   shared, TIN1, TOUT1, TOUT2, prev=prev, bnd_windows=(6, 3))
        prev = nxt()
        if nph < 4:
            block = es.enter_context(nc.Block())
            finish(p, block, p.barrier_token())
            p.build(block)
            return nc
        emit_mixphase(p, es, nc, ar, ar.psum,
                      {"x_own": io["x_pad"][:, TOK:N], "a0_fm": io["S_a0"],
                       "f0_fm": io["S_f0"], "x1_fm": io["S_x1"],
                       "xn1_bf": io["S_xn1own"], "wts_tm": io["S_wts"],
                       "wr_t": io["wr_t"]}, consts, TOK, prev=prev)
        prev = nxt()
        if nph < 5:
            block = es.enter_context(nc.Block())
            finish(p, block, p.barrier_token())
            p.build(block)
            return nc
        agt = p.op("pool", lambda g: g.collective_compute(
            "AllGather", ALU.bypass,
            replica_groups=[[0, 1], [2, 3], [4, 5], [6, 7]],
            ins=[xn1own.ap().opt()], outs=[xg.ap().opt()]),
            deps=prev)
        prev = nxt()
        if nph < 6:
            block = es.enter_context(nc.Block())
            finish(p, block, p.barrier_token())
            p.build(block)
            return nc
        emit_l1attn_sym(p, es, nc, ar, ar.psum,
                        {"xg_bf": io["S_xg"], "xn1own_bf": io["S_xn1own"],
                         "S_kfm": io["S_kfm"], "S_vtm": io["S_vtm"],
                         "S_qfm": io["S_qfm"], "a1_tm": io["S_a1"],
                         **{k: io[k] for k in ("l1_wq", "l1_wk", "l1_wv",
                                               "msel_bm", "msel_al")}},
                        shared, prev=prev)
        prev = nxt()
        if nph < 7:
            block = es.enter_context(nc.Block())
            finish(p, block, p.barrier_token())
            p.build(block)
            return nc
        emit_moe(p, es, nc, ar, ar.psum,
                 {"xn1_bf": io["S_xn1own"], "x1_fm": io["S_x1"],
                  "a1_tm": io["S_a1"], "wts_tm": io["S_wts"],
                  "out_fm": io["out_fm"],
                  **{k: io[k] for k in ("moe_w1", "moe_w3", "moe_w2")}},
                 shared, TOK, TG=1024, prev=prev)
        block = es.enter_context(nc.Block())
        finish(p, block, p.barrier_token())
        p.build(block)
    return nc


def kernel_fused(dbg=False, **inputs):
    f32 = np.float32
    x = np.asarray(inputs["L0_attn_x"], dtype=f32)

    def tile128(vec):
        return _ct(np.tile(np.asarray(vec, f32)[None, :], (128, 1)))

    wqkv = np.asarray(inputs["L0_ffn_wqkv"], f32)
    wo2 = np.asarray(inputs["L0_ffn_wo"], f32)
    w1 = np.asarray(inputs["L0_ffn_w1"], f32)
    b1 = np.asarray(inputs["L0_ffn_b1"], f32)
    w2 = np.asarray(inputs["L0_ffn_w2"], f32)
    b2 = np.asarray(inputs["L0_ffn_b2"], f32)
    qsc = np.asarray(inputs["L0_ffn_q_scale"], f32)
    ksc = np.asarray(inputs["L0_ffn_k_scale"], f32)
    lg = np.asarray(inputs["L0_ffn_ln_g"], f32)
    lb = np.asarray(inputs["L0_ffn_ln_b"], f32)
    HID = 2730
    w1p = np.zeros((2, D, 2 * FFN_H), f32)
    w1p[:, :, :HID] = w1[:, :, :HID]
    w1p[:, :, FFN_H:FFN_H + HID] = w1[:, :, HID:]
    b1p = np.zeros((2, 2 * FFN_H), f32)
    b1p[:, :HID] = b1[:, :HID]
    b1p[:, FFN_H:FFN_H + HID] = b1[:, HID:]
    w2p = np.zeros((2, FFN_H, D), f32)
    w2p[:, :HID, :] = w2

    com = dict(
        wq_t=_ct(np.asarray(inputs["L0_attn_Wq"], f32).T),
        wk_t=_ct(np.asarray(inputs["L0_attn_Wk"], f32).T),
        wv_t=_ct(np.asarray(inputs["L0_attn_Wv"], f32).T),
        wo_t=_ct(np.asarray(inputs["L0_attn_Wo"], f32).T),
        gam16=tile128(inputs["L0_attn_gamma"]),
        bet16=tile128(inputs["L0_attn_beta"]),
        f_wqkv=wqkv, f_wo=wo2, f_w1=w1p, f_w2=w2p,
        f_qs=_ct(np.tile(qsc.T, (2, 1))), f_ks=_ct(np.tile(ksc.T, (2, 1))),
        f_lg=_ct(lg.reshape(2, 8, 128).transpose(2, 0, 1).reshape(128, 16)),
        f_lb=_ct(lb.reshape(2, 8, 128).transpose(2, 0, 1).reshape(128, 16)),
        f_b1=_ct(b1p.reshape(2, 2 * FFN_HC, 128).transpose(2, 0, 1)
                 .reshape(128, -1)),
        f_b2=_ct(b2.reshape(2, 8, 128).transpose(2, 0, 1).reshape(128, 16)),
        wr_t=_ct(np.asarray(inputs["L1_ffn_w_router"], f32).T),
        l1_wq=np.asarray(inputs["L1_attn_W_Q"], f32),
        l1_wk=np.asarray(inputs["L1_attn_W_K"], f32),
        l1_wv=np.asarray(inputs["L1_attn_W_V"], f32),
        moe_w1=np.asarray(inputs["L1_ffn_w1"], f32),
        moe_w3=np.asarray(inputs["L1_ffn_w3"], f32),
        moe_w2=np.asarray(inputs["L1_ffn_w2"], f32),
    )
    bm = [np.zeros((128, 100), f32), np.zeros((128, 100), f32)]
    al = [np.zeros((128, 16), f32), np.zeros((128, 16), f32)]
    base = 0
    for i in range(8):
        nkt = 9 + i
        bm[0][:, base + i + 1:base + nkt] = -1e30
        al[0][:, 2 * i] = 1.0
        al[1][:, 2 * i + 1] = 1.0
        base += nkt

    nc = build_mega(dbg)
    maps = []
    for b in range(B):
        for h in range(2):
            own = _ct(x[b, h * TOK:(h + 1) * TOK].T)
            if h == 0:
                xpad = np.concatenate([np.zeros((D, TOK), f32), own], 1)
                fpos = -3072.0
                bnd = np.full((128, 2), -1e30, f32)
            else:
                xpad = np.concatenate([_ct(x[b, 0:TOK].T), own], 1)
                fpos = 1024.0
                bnd = np.zeros((128, 2), f32)
            maps.append(dict(
                x_pad=_ct(xpad), f_pos=np.full((128, 1), fpos, f32),
                f_bnd=bnd, a_pre=np.full((128, 1), float(h), f32),
                msel_bm=bm[h], msel_al=al[h], **com))
    res = run_bass_kernel_spmd(nc, maps, core_ids=list(range(8)))
    out = np.empty((B, N, D), f32)
    for b in range(B):
        out[b, 0:TOK] = res.results[2 * b]["out_fm"].T
        out[b, TOK:] = res.results[2 * b + 1]["out_fm"].T
    if dbg:
        return out, res
    return out


kernel_multilaunch = kernel
kernel = kernel_fused
